# version 07
# baseline (speedup 1.0000x reference)
import numpy as np
import ml_dtypes
from contextlib import ExitStack

import concourse.bass as bass
import concourse.mybir as mybir
from concourse.bass_utils import run_bass_kernel_spmd

F32 = mybir.dt.float32
BF16 = mybir.dt.bfloat16
U8 = mybir.dt.uint8
AF = mybir.ActivationFunctionType
ALU = mybir.AluOpType
AX = mybir.AxisListType

BATCH, T, C = 8, 2048, 1024
NKT = C // 128
NCORES = 8
BF = ml_dtypes.bfloat16
ESZ = {F32: 4, BF16: 2, U8: 1}


class KB:

    ENGS = ("sync", "vector", "scalar", "tensor", "gpsimd")

    def __init__(self, arena_bytes=212736):
        self.nc = bass.Bass("TRN2", target_bir_lowering=False, num_devices=8)
        self.scope = ""
        self.es = ExitStack()
        self.thunks = {e: [] for e in self.ENGS}
        self.counts = {}
        self.handles = {}
        self.last_wait = {}
        self.eng_sem = {"vector": "sv", "scalar": "sc", "tensor": "sn",
                        "gpsimd": "sg", "sync": "sy"}
        self.dma_sem = {"sync": "dy", "gpsimd": "dg", "scalar": "dc"}
        self.consts = {}
        self.out_names = []
        self.AR = self._sb_real("ARENA", [128, arena_bytes], U8)
        self.ar_off = 0
        self.ar_stack = []
        self.sar_off = 0
        self.PA = self.ps("PA", [128, 2048], F32)
        self.PB = self.ps("PB", [128, 2048], F32)
        self.proj_guard = None
        self._ccols = {}
        self._ccol_t = self._sb_real("CCOLS", [128, 16], F32)
        self.tail = {}
        self.DUM = self._sb_real("DUMMY", [2, 8], F32)

    def din(self, name, shape, dtype=F32):
        return self.nc.dram_tensor(self.scope + name, list(shape), dtype,
                                   kind="ExternalInput").ap()

    def dout(self, name, shape, dtype=F32):
        self.out_names.append(self.scope + name)
        return self.nc.dram_tensor(self.scope + name, list(shape), dtype,
                                   kind="ExternalOutput").ap()

    def dtmp(self, name, shape, dtype=F32):
        return self.nc.dram_tensor(self.scope + name, list(shape), dtype,
                                   kind="Internal").ap()

    def cin(self, name, arr):
        arr = np.ascontiguousarray(arr)
        name = self.scope + name
        self.consts[name] = arr
        dt = {np.dtype(np.float32): F32, np.dtype(BF): BF16}[arr.dtype]
        return self.nc.dram_tensor(name, list(arr.shape), dt,
                                   kind="ExternalInput").ap()

    def _sb_real(self, name, shape, dtype=F32):
        return self.es.enter_context(
            self.nc.sbuf_tensor(self.scope + name, list(shape), dtype))

    def sb(self, name, shape, dtype=F32):
        p = shape[0]
        n = int(np.prod(shape[1:])) * ESZ[dtype]
        n_al = (n + 63) & ~63
        self.sar_off += n_al
        off = self.AR.shape[1] - self.sar_off
        assert self.ar_off <= off, (
            f"arena collision: abuf at {self.ar_off}, named {name} at {off}")
        v = self.AR[0:p, off:off + n].bitcast(dtype)
        if len(shape) > 2:
            pat = " ".join(f"a{i}" for i in range(1, len(shape)))
            kw = {f"a{i}": shape[i] for i in range(2, len(shape))}
            v = v.rearrange(f"p ({pat}) -> p {pat}", **kw)
        return v

    def ps(self, name, shape, dtype=F32):
        return self.es.enter_context(
            self.nc.psum_tensor(name, list(shape), dtype))

    def abuf(self, shape, dtype):
        n = int(np.prod(shape[1:])) * ESZ[dtype]
        n_al = (n + 63) & ~63
        off = self.ar_off
        assert off + n_al <= self.AR.shape[1] - self.sar_off, (
            f"arena overflow: need {n_al} at {off} (named top at "
            f"{self.AR.shape[1] - self.sar_off})")
        self.ar_off = off + n_al
        v = self.AR[:, off:off + n].bitcast(dtype)
        if len(shape) > 2:
            pat = " ".join(f"a{i}" for i in range(1, len(shape)))
            kw = {f"a{i}": shape[i] for i in range(2, len(shape))}
            v = v.rearrange(f"p ({pat}) -> p {pat}", **kw)
        return v

    def apush(self):
        self.ar_stack.append(self.ar_off)

    def apop(self):
        self.ar_off = self.ar_stack.pop()

    def _wait(self, eng, toks, hard=()):
        own = self.eng_sem[eng]
        for t in list(toks) + list(hard):
            if t is None:
                continue
            sem, val = t
            if sem == own and t not in hard:
                continue
            cur = self.last_wait.get((eng, sem), 0)
            if val > cur:
                self.last_wait[(eng, sem)] = val
                self.thunks[eng].append(
                    lambda e, s=sem, v=val: e.wait_ge(self.handles[s], v))

    def emit(self, eng, fn, waits=(), sig=False, hard=()):
        self._wait(eng, waits, hard)
        self.tail[eng] = not sig
        if sig:
            sem = self.eng_sem[eng]
            self.counts[sem] = self.counts.get(sem, 0) + 1
            val = self.counts[sem]
            def thunk(e, fn=fn, s=sem):
                fn(e).then_inc(self.handles[s], 1)
            self.thunks[eng].append(thunk)
            return (sem, val)
        self.thunks[eng].append(lambda e, fn=fn: fn(e))
        return None

    def tok(self, eng):
        sem = self.eng_sem[eng]
        return (sem, self.counts.get(sem, 0))

    def dma(self, out, in_, waits=(), eng="sync"):
        self._wait(eng, waits)
        sem = self.dma_sem[eng]
        self.counts[sem] = self.counts.get(sem, 0) + 16
        val = self.counts[sem]
        def thunk(e, out=out, in_=in_, s=sem):
            e.dma_start(out=out, in_=in_).then_inc(self.handles[s], 16)
        self.thunks[eng].append(thunk)
        return (sem, val)

    def mm(self, out, lhsT, rhs, start, stop, waits=(), sig=False):
        return self.emit(
            "tensor",
            lambda e, o=out, l=lhsT, r=rhs, s1=start, s2=stop: e.matmul(
                o, l, r, start=s1, stop=s2),
            waits, sig)

    def ccol(self, val):
        val = float(val)
        if val not in self._ccols:
            i = len(self._ccols)
            assert i < 16
            self.memset(self._ccol_t[:, i:i + 1], val, sig=True)
            self._ccols[val] = (i, self.tok("vector"))
        return self._ccols[val]

    def act(self, out, in_, func, bias=None, scale=None, accum=None,
            waits=(), sig=False, hard=()):
        kw = {}
        if isinstance(bias, float):
            i, tok = self.ccol(bias)
            p = in_.shape[0]
            bias = self._ccol_t[0:p, i:i + 1]
            waits = list(waits) + [tok]
        if bias is not None:
            kw["bias"] = bias
        if scale is not None:
            kw["scale"] = scale
        if accum is not None:
            kw["accum_out"] = accum
        return self.emit(
            "scalar",
            lambda e, o=out, i=in_, f=func, kw=kw: e.activation(o, i, f, **kw),
            waits, sig, hard)

    def tt(self, out, a, b, op, waits=(), sig=False, eng="vector", hard=()):
        return self.emit(
            eng, lambda e, o=out, x=a, y=b, p=op: e.tensor_tensor(o, x, y, p),
            waits, sig, hard)

    def ts(self, out, in0, s1, s2, op0, op1=None, accum=None, waits=(),
           sig=False, eng="vector", hard=()):
        def fn(e, o=out, i=in0, a=s1, b=s2, p0=op0, p1=op1, ac=accum):
            kw = {}
            if ac is not None:
                kw["accum_out"] = ac
            if p1 is None:
                return e.tensor_scalar(o, i, a, None, p0, **kw)
            return e.tensor_scalar(o, i, a, b, p0, p1, **kw)
        return self.emit(eng, fn, waits, sig, hard)

    def stt(self, out, in0, scal, in1, op0, op1, waits=(), sig=False,
            eng="vector", hard=()):
        return self.emit(
            eng,
            lambda e, o=out, i0=in0, s=scal, i1=in1, p0=op0, p1=op1:
                e.scalar_tensor_tensor(o, i0, s, i1, p0, p1),
            waits, sig, hard)

    def red(self, out, in_, op, absval=None, negate=None, waits=(),
            sig=False, hard=()):
        return self.emit(
            "vector",
            lambda e, o=out, i=in_, p=op, a=absval, n=negate: e.tensor_reduce(
                o, i, AX.X, p, apply_absolute_value=a, negate=n),
            waits, sig, hard)

    def recip(self, out, in_, waits=(), sig=False):
        return self.emit(
            "vector", lambda e, o=out, i=in_: e.reciprocal(o, i), waits, sig)

    def copy(self, out, in_, waits=(), sig=False, eng="vector"):
        return self.emit(
            eng, lambda e, o=out, i=in_: e.tensor_copy(o, i), waits, sig)

    def memset(self, ap, val, waits=(), sig=False, eng="vector"):
        return self.emit(
            eng, lambda e, a=ap, v=val: e.memset(a, v), waits, sig)

    def cc_op(self, kind, op, out, in_, waits=()):
        self._wait("gpsimd", waits)
        sem = self.eng_sem["gpsimd"]
        self.counts[sem] = self.counts.get(sem, 0) + 1
        val = self.counts[sem]
        def thunk(e, o=out, i=in_, k=kind, p=op, s=sem):
            e.collective_compute(
                k, p, [list(range(NCORES))], [i], [o]).then_inc(
                    self.handles[s], 1)
        self.thunks["gpsimd"].append(thunk)
        return (sem, val)

    def cc_allgather(self, out, in_, waits=()):
        ag = self.cc_op("AllGather", ALU.bypass, out, in_, waits)
        if not hasattr(self, "_ccbar"):
            bs = self.dtmp("CCBARsrc", [1, 1], F32)
            bd = self.dtmp("CCBARdst", [1, 1], F32)
            t0 = self.memset(self.DUM[0:1, 4:5], 0.0, sig=True)
            t1 = self.dma(bs, self.DUM[0:1, 4:5], waits=[t0], eng="gpsimd")
            self._ccbar = (bs, bd, t1)
        bs, bd, t1 = self._ccbar
        return self.cc_op("AllReduce", ALU.add, bd, bs, waits=[t1, ag])

    def barrier(self):
        if self.tail.get("vector"):
            self.memset(self.DUM[0:1, 0:1], 0.0, sig=True)
        if self.tail.get("gpsimd"):
            self.memset(self.DUM[0:1, 1:2], 0.0, sig=True, eng="gpsimd")
        if self.tail.get("scalar"):
            self.act(self.DUM[0:1, 2:3], self.DUM[0:1, 3:4], AF.Identity,
                     sig=True)
        if self.tail.get("tensor"):
            self.mm(self.PA[0:1, 0:1], self.DUM[0:1, 0:1],
                    self.DUM[0:1, 0:1], start=True, stop=True, sig=True)
        snap = dict(self.counts)
        for eng in self.ENGS:
            for sem, val in snap.items():
                if val <= 0:
                    continue
                cur = self.last_wait.get((eng, sem), 0)
                if val > cur:
                    self.last_wait[(eng, sem)] = val
                    self.thunks[eng].append(
                        lambda e, s=sem, v=val: e.wait_ge(self.handles[s], v))

    def dbg(self, name, ap, waits=()):
        if not getattr(self, "dbg_on", False):
            return
        shape = list(ap.shape)
        dt = ap.dtype
        d = self.dout("dbg_" + name, shape, dt)
        self.dma(d, ap, waits=waits, eng="gpsimd")

    def build(self):
        for s in set(list(self.eng_sem.values()) + list(self.dma_sem.values())):
            self.handles[s] = self.es.enter_context(self.nc.semaphore(s))
        for sem in self.dma_sem.values():
            if self.counts.get(sem, 0):
                self.thunks["sync"].append(
                    lambda e, s=sem, v=self.counts[sem]: e.wait_ge(
                        self.handles[s], v))
        block = self.es.enter_context(self.nc.Block())
        for eng in self.ENGS:
            lst = self.thunks[eng]
            dec = getattr(block, eng)
            def body(e, lst=lst):
                for t in lst:
                    t(e)
            dec(body)
        self.es.close()
        return self.nc



def stage_cols(kb, name, vec, scale=1.0, n=None):
    arr = (np.asarray(vec, np.float64) * scale).astype(np.float32)
    if n is None:
        n = arr.size // 128
    d = kb.cin(name, arr.reshape(n, 128).T.copy())
    s = kb.sb(name + "_sb", [128, n], F32)
    tok = kb.dma(s[:], d)
    return s, tok


_BCN = [0]


def _nancheck(tag, out):
    nf = ~np.isfinite(out)
    if nf.any():
        for k in range(out.shape[0]):
            if nf[k].any():
                ft = np.where(nf[k].any(0))[0]
                tk = np.where(nf[k].any(1))[0]
                print(f"[{tag}-NANDBG] core {k}: n={nf[k].sum()} "
                      f"ftiles={sorted(set(ft // 128))} nt={len(tk)} "
                      f"t[{tk[0]}..{tk[-1]}]", flush=True)

def bcast_row(kb, dst128, src_row, waits):
    _BCN[0] += 1
    n = src_row.shape[-1]
    tmp = kb.dtmp(f"bc_tmp{_BCN[0]}", [1, n], F32)
    t1 = kb.dma(tmp, src_row, waits=waits, eng="gpsimd")
    return kb.dma(dst128, tmp.squeeze(0).partition_broadcast(128),
                  waits=[t1], eng="gpsimd")


class ProjFM:

    def __init__(self, kb, wdram, n_ot, rhs_fn, evac, T=T,
                 scale_cols=None, scale2_cols=None, rhs2_fn=None,
                 rhs_waits=(), n_kt=NKT, ps_guard=(), ot_off=0, m_rows=128,
                 k_rows=128):
        self.kb = kb
        nc2 = 2 if rhs2_fn is not None else 1
        wst, wbf = kb.WST, kb.WBF
        cast_tok = [kb.proj_guard, kb.proj_guard]
        evac_tok = [None, None]
        mm_prev = [kb.proj_guard, kb.proj_guard]
        guard = list(ps_guard)
        for ot in range(n_ot):
            b = ot % 2
            wot = ot + ot_off
            wv = wdram[:, wot * m_rows:(wot + 1) * m_rows].rearrange(
                "(k p) m -> p k m", p=k_rows)
            tok_d = kb.dma(wst[0:k_rows, b, 0:n_kt, 0:m_rows], wv,
                           waits=[cast_tok[b]])
            ct = None
            for kt in range(n_kt):
                last = kt == n_kt - 1
                if scale_cols is None:
                    ct = kb.copy(wbf[0:k_rows, b, 0, kt, 0:m_rows],
                                 wst[0:k_rows, b, kt, 0:m_rows],
                                 waits=[tok_d, mm_prev[b]],
                                 sig=(last and nc2 == 1))
                else:
                    ct = kb.ts(wbf[0:k_rows, b, 0, kt, 0:m_rows],
                               wst[0:k_rows, b, kt, 0:m_rows],
                               scale_cols[0:k_rows, kt:kt + 1], None,
                               ALU.mult, waits=[tok_d, mm_prev[b]],
                               sig=(last and nc2 == 1))
                if nc2 == 2:
                    ct = kb.ts(wbf[0:k_rows, b, 1, kt, 0:m_rows],
                               wst[0:k_rows, b, kt, 0:m_rows],
                               scale2_cols[0:k_rows, kt:kt + 1], None,
                               ALU.mult, waits=[tok_d, mm_prev[b]], sig=last)
            cast_tok[b] = ct
            ps = [kb.PA, kb.PB][b]
            tok_mm = None
            bounds = list(range(0, T, 512))
            nchunk = len(bounds)
            for c, c0 in enumerate(bounds):
                sl = slice(c0, min(T, c0 + 512))
                n_in_group = n_kt * nc2
                idx = 0
                for kt in range(n_kt):
                    for v2 in range(nc2):
                        rhs = (rhs_fn(kt) if v2 == 0 else rhs2_fn(kt))
                        last = (idx == n_in_group - 1)
                        tok_mm = kb.mm(
                            ps[0:m_rows, sl],
                            wbf[0:k_rows, b, v2, kt, 0:m_rows],
                            rhs[:, sl],
                            start=(idx == 0), stop=last,
                            waits=[ct, evac_tok[b]] + list(rhs_waits) + guard,
                            sig=(last and c == nchunk - 1))
                        idx += 1
            mm_prev[b] = tok_mm
            evac_tok[b] = evac(ot, ps, tok_mm)
        kb.proj_guard = kb.tok("tensor")


def emit_ln(kb, xdram, xn, xbf, sq, xst, hrow, rb, mb, ones_bf, hrow2,
            save_rows=False):
    mz = kb.memset(xn[:, :, 0:16], 0.0, sig=True)
    prev = None
    ct = st_ = None
    for kt in range(NKT):
        tokd = kb.dma(xst, xdram[kt], waits=[prev])
        ct = kb.copy(xbf[:, kt], xst, waits=[tokd], sig=True)
        prev = ct
        st_ = kb.act(sq[:, kt], xbf[:, kt], AF.Square, waits=[ct],
                     sig=(kt == NKT - 1))
    xb_tok, sq_tok = ct, st_
    t_s = t_q = None
    for c in range(4):
        sl = slice(c * 512, (c + 1) * 512)
        for kt in range(NKT):
            t_s = kb.mm(kb.PA[0:1, sl], ones_bf, xbf[:, kt, sl],
                        start=(kt == 0), stop=(kt == NKT - 1),
                        waits=[xb_tok], sig=(kt == NKT - 1 and c == 3))
        for kt in range(NKT):
            t_q = kb.mm(kb.PB[0:1, sl], ones_bf, sq[:, kt, sl],
                        start=(kt == 0), stop=(kt == NKT - 1),
                        waits=[sq_tok], sig=(kt == NKT - 1 and c == 3))
    e1 = kb.act(hrow[0:1, :], kb.PA[0:1, :], AF.Identity, waits=[t_s],
                sig=True)
    e2 = kb.act(rb[0:1, :], kb.PB[0:1, :], AF.Identity, waits=[t_q],
                sig=True)
    tokm = bcast_row(kb, mb, hrow[0:1, :], [e1])
    tokr = bcast_row(kb, rb, rb[0:1, :], [e2])
    t1 = kb.ts(mb, mb, 1.0 / C, None, ALU.mult, waits=[tokm], sig=True)
    t3 = kb.act(hrow2, mb, AF.Square, waits=[t1], sig=True)
    t4 = kb.stt(rb, rb, 1.0 / C, hrow2, ALU.mult, ALU.subtract,
                waits=[tokr, t3], sig=True)
    t5 = kb.act(rb, rb, AF.Sqrt, bias=1e-5, waits=[t4], sig=True)
    tokr = kb.recip(rb, rb, waits=[t5], sig=True)
    tokm = t1
    if save_rows:
        _BCN[0] += 1
        mrow_d = kb.dtmp(f"ln_mrow{_BCN[0]}", [1, T], F32)
        rrow_d = kb.dtmp(f"ln_rrow{_BCN[0]}", [1, T], F32)
        msave = kb.dma(mrow_d, mb[0:1, :], waits=[tokm], eng="gpsimd")
        rsave = kb.dma(rrow_d, rb[0:1, :], waits=[tokr], eng="gpsimd")
        kb.ln_rows = (mrow_d, rrow_d)
        kb.ln_row_toks = [msave, rsave]
    out_toks = []
    prev = ct
    for kt in range(NKT):
        tokd = kb.dma(xst, xdram[kt], waits=[prev])
        kb.tt(xst, xst, mb, ALU.subtract, waits=[tokd, tokm, mz])
        tq = kb.tt(xn[:, kt, 16:2064], xst, rb, ALU.mult,
                   waits=[tokr], sig=True)
        prev = tq
        out_toks.append(tq)
    return out_toks



def emit_layer3(kb, W, xdram, xodram):
    ones_bf = kb.sb("ones_bf", [128, 4], BF16)
    ones_row = kb.sb("ones_row", [1, 128], BF16)
    onescol_bf = kb.sb("onec_bf", [128, 1], BF16)
    kb.memset(ones_bf[:], 1.0, sig=True)
    t_or = kb.memset(ones_row[:], 1.0, sig=True)
    t_oc = kb.memset(onescol_bf[:], 1.0, sig=True)
    bias_sb = kb.sb("l3bias", [128, 256], F32)
    DD = kb.sb("DD", [128, 8], F32)
    mq = kb.sb("l3mq", [16, 4], F32)
    bound_bf = kb.sb("l3bound_bf", [16, 2048], BF16)
    qb_sb, _ = stage_cols(kb, "l3qb", W["q_b"].reshape(-1))
    kb_sb, _ = stage_cols(kb, "l3kb", W["k_b"].reshape(-1))
    lb_sb, _ = stage_cols(kb, "l3lb", W["l_b"].reshape(-1))
    fb_sb, _ = stage_cols(kb, "l3fb", W["bias"])
    sel_np = np.zeros((C, 16), np.float32)
    for d in range(C):
        sel_np[d, d // 64] = 1.0
    sel_d = kb.cin("l3sel16", sel_np.astype(BF))
    id16_d = kb.cin("id16", np.eye(16).astype(BF))
    id16 = kb.sb("id16_sb", [16, 16], BF16)
    t_id16 = kb.dma(id16[:], id16_d)
    sel_sb = kb.sb("sel16", [128, NKT, 16], BF16)
    tok_sel = kb.dma(sel_sb[:], sel_d.rearrange("(k p) m -> p k m", p=128))
    vb_row_d = kb.cin("l3vb_row", W["v_b"].reshape(1, 1024))
    vb_row = kb.sb("l3vb_sb", [1, 1024], F32)
    vb_bf = kb.sb("l3vb_bf", [1, 1024], BF16)
    tvb = kb.dma(vb_row[:], vb_row_d)
    tvb = kb.copy(vb_bf[:], vb_row[:], waits=[tvb], sig=True)

    qw_d = kb.cin("l3qw", W["q_w"])
    kw_d = kb.cin("l3kw", W["k_w"])
    vw_d = kb.cin("l3vw", W["v_w"])
    lw_d = kb.cin("l3lw", W["l_w"])
    w1_d = kb.cin("l3w1", W["weight"])
    w2_d = kb.cin("l3w2", W["aux_weight1"])
    w3_d = kb.cin("l3w3", W["aux_weight2"])

    sdram = kb.dtmp("s_spill", [NKT, 128, 2048], F32)

    XN = kb.abuf([128, NKT, 2064], BF16)
    H16 = kb.abuf([128, 2048], F32)
    kb.WST = kb.abuf([128, 2, NKT, 128], F32)
    kb.WBF = kb.abuf([128, 2, 2, NKT, 128], BF16)
    A = kb.abuf([128, NKT, 2048], BF16)
    Bb = kb.abuf([128, NKT, 2048], BF16)
    Cc = kb.abuf([128, NKT * 4096], U8)
    Dd = kb.abuf([128, NKT * 4096], U8)
    Ff = kb.abuf([128, 2048], F32)

    QF, KF = A, Bb
    VT = Cc[:, 0:32768].bitcast(BF16).rearrange("p (a b) -> p a b", b=1024)
    GT = Cc[:, 0:8192].bitcast(F32)
    XT = Cc[:, 8192:16384].bitcast(F32)
    ST = Cc[:, 16384:24576].bitcast(F32)
    VPT = kb.sb("VPT_sb", [128, 2, 64], BF16)
    MB = Dd[:, 16384:24576].bitcast(F32)
    RB = Dd[:, 24576:32768].bitcast(F32)
    VWBF = Dd[:, 0:16384].bitcast(BF16).rearrange("p (a b) -> p a b", b=1024)
    VWST = Dd[:, 16384:20480].bitcast(F32)
    SQ2 = Dd[:, 0:32768].bitcast(BF16).rearrange("p (a b) -> p a b", b=2048)
    NQ = Dd[:, 24576:32768].bitcast(F32)
    WOIN = Dd[:, 0:32768].bitcast(BF16).rearrange("p (a b) -> p a b", b=2048)
    XST = Ff[:, 0:2048]
    EX = Ff[:, 0:2048].bitcast(BF16).rearrange("p (a b) -> p a b", b=2048)
    SOUT = Ff[:, 0:2048]

    xn_toks = emit_ln(kb, xdram, XN, A, Bb, XST, H16[0:16, :],
                      RB, MB, ones_bf[:, 0:1], H16)

    kb.dbg("xn", XN, waits=xn_toks)

    def xn_t(kt):
        return XN[:, kt, 16:2064]

    kb.proj_guard = kb.tok("tensor")

    def evac_q(ot, ps, tok):
        return kb.act(QF[:, ot, :], ps[:], AF.Identity,
                      bias=qb_sb[:, ot:ot + 1], waits=[tok], sig=True)

    def evac_k(ot, ps, tok):
        return kb.act(KF[:, ot, :], ps[:], AF.Identity,
                      bias=kb_sb[:, ot:ot + 1], waits=[tok], sig=True)

    ProjFM(kb, qw_d, NKT, xn_t, evac_q, rhs_waits=xn_toks)
    q_done = kb.tok("scalar")
    ProjFM(kb, kw_d, NKT, xn_t, evac_k, rhs_waits=xn_toks, ps_guard=[q_done])
    k_done = kb.tok("scalar")

    vw_toks = []
    prev_cast = kb.proj_guard
    for kt in range(NKT):
        tok = kb.dma(VWST, vw_d[kt * 128:(kt + 1) * 128, :],
                     waits=[prev_cast])
        prev_cast = kb.copy(VWBF[:, kt], VWST, waits=[tok], sig=True)
        vw_toks.append(prev_cast)
    ev_tok = [k_done, k_done]
    for tt in range(16):
        b = tt % 2
        ps = [kb.PA, kb.PB][b]
        tok_mm = None
        for c in range(2):
            sl = slice(c * 512, (c + 1) * 512)
            for kt in range(NKT):
                tok_mm = kb.mm(ps[:, sl], xn_t(kt)[:, tt * 128:(tt + 1) * 128],
                               VWBF[:, kt, sl], start=(kt == 0), stop=False,
                               waits=[vw_toks[kt], ev_tok[b], t_or, tvb,
                                      k_done])
            tok_mm = kb.mm(ps[:, sl], ones_row[:, :], vb_bf[:, sl],
                           start=False, stop=True, waits=[], sig=(c == 1))
        ev_tok[b] = kb.act(VT[:, tt, :], ps[:, 0:1024], AF.Identity,
                           waits=[tok_mm], sig=True)
    vt_tok = kb.tok("scalar")
    kb.dbg("qf", QF, waits=[q_done])
    kb.dbg("kf", KF, waits=[k_done])
    kb.dbg("vt", VT, waits=[vt_tok])

    mm_tok = {}
    for idx, src_t in enumerate((KF, QF)):
        stok = None
        for kt in range(NKT):
            stok = kb.act(SQ2[:, kt], src_t[:, kt, :], AF.Square,
                          waits=[vt_tok, mm_tok.get(0)],
                          sig=(kt == NKT - 1))
        ps = [kb.PA, kb.PB][idx]
        for c in range(4):
            sl = slice(c * 512, (c + 1) * 512)
            for kt in range(NKT):
                mm_tok[idx] = kb.mm(
                    ps[0:16, sl], sel_sb[:, kt, :], SQ2[:, kt, sl],
                    start=(kt == 0), stop=(kt == NKT - 1),
                    waits=[stok, tok_sel, vt_tok],
                    sig=(kt == NKT - 1 and c == 3))
    t_nk = kb.act(H16[0:16, :], kb.PA[0:16, :], AF.Sqrt,
                  waits=[mm_tok[0]], sig=True)
    t_nq = kb.act(NQ[0:16, :], kb.PB[0:16, :], AF.Sqrt,
                  waits=[mm_tok[1]], sig=True)
    t_mq = kb.red(mq[:, 0:1], NQ[0:16, :], ALU.max, waits=[t_nq],
                  sig=True)
    kb.dbg("nk", H16[0:16, :], waits=[t_nk])
    kb.dbg("nq", NQ[0:16, :], waits=[t_nq])
    kb.dbg("mq", mq[:], waits=[t_mq])
    t_mqs = kb.ts(mq[:, 1:2], mq[:, 0:1], -0.125, None, ALU.mult,
                  waits=[], hard=[t_mq], sig=True)
    t_b = kb.ts(NQ[0:16, :], H16[0:16, :], mq[:, 1:2], None, ALU.mult,
                waits=[t_nk], hard=[t_mqs], sig=True)
    t_bb = kb.copy(bound_bf[:], NQ[0:16, :], waits=[t_b], sig=True)

    bias_prev = t_nk
    for ki in range(16):
        tok = kb.mm(kb.PA[:, 0:16],
                    bound_bf[0:16, ki * 128:(ki + 1) * 128], id16[:],
                    start=True, stop=True,
                    waits=[t_bb, t_id16, bias_prev], sig=True)
        bias_prev = kb.act(bias_sb[:, ki * 16:ki * 16 + 16],
                           kb.PA[:, 0:16], AF.Identity, waits=[tok], sig=True)
    bias_done = bias_prev

    kb.dbg("bias", bias_sb[:], waits=[bias_done])
    kb.dbg("boundbf", bound_bf[:], waits=[t_bb])

    MXC = kb.sb("l3mxc", [128, 8], F32)
    ctx_evac = t_nq
    exp_last = bias_done
    recip_last = None
    mx_last = None
    ctx_last = {0: None, 1: None}
    for h in range(16):
        dt, po = h // 2, (h % 2) * 64
        ctx_mm = None
        for ki in range(16):
            eb = ki % 2
            sc_tok = None
            mxr = None
            for c in range(4):
                sl = slice(c * 512, (c + 1) * 512)
                sc_tok = kb.mm(
                    kb.PA[:, sl],
                    KF[po:po + 64, dt, ki * 128:(ki + 1) * 128],
                    QF[po:po + 64, dt, sl],
                    start=True, stop=True,
                    waits=[exp_last, bias_done], sig=True)
                mxr = kb.red(MXC[:, c:c + 1], kb.PA[:, sl], ALU.max,
                             waits=[sc_tok, mx_last], sig=(c == 3))
            mx4 = kb.red(MXC[:, 4:5], MXC[:, 0:4], ALU.max,
                         waits=[], hard=[mxr], sig=True)
            mxs = kb.ts(MXC[:, 5:6], MXC[:, 4:5], -0.125, None, ALU.mult,
                        waits=[], hard=[mx4], sig=True)
            mx_last = mxs
            e_tok = kb.act(EX[:, eb, :], kb.PA[:], AF.Exp,
                           bias=MXC[:, 5:6],
                           scale=0.125, accum=DD[:, 0:1],
                           waits=[sc_tok, ctx_last[eb], recip_last, mxs],
                           sig=True)
            exp_last = e_tok
            d_tok = kb.act(DD[:, 2:3], DD[:, 0:1], AF.Identity,
                           waits=[], sig=True)
            recip_last = kb.recip(DD[:, 1:2], DD[:, 2:3], waits=[d_tok],
                                  sig=True)
            vp_tok = kb.ts(VPT[:, eb, :], VT[:, ki, h * 64:(h + 1) * 64],
                           DD[:, 1:2], None, ALU.mult,
                           waits=[ctx_last[eb]], hard=[recip_last], sig=True)
            for c in range(4):
                sl = slice(c * 512, (c + 1) * 512)
                ctx_mm = kb.mm(kb.PB[po:po + 64, sl], VPT[:, eb, :],
                               EX[:, eb, sl],
                               start=(ki == 0), stop=(ki == 15),
                               waits=[vp_tok, e_tok, ctx_evac],
                               sig=(c == 3))
            ctx_last[eb] = ctx_mm
        ctx_evac = kb.act(WOIN[po:po + 64, dt, :], kb.PB[po:po + 64, :],
                          AF.Identity, waits=[ctx_mm], sig=True)
    ctx_done = ctx_evac

    st_tok = [None]

    def evac_s(ot, ps, tok):
        t1 = kb.act(SOUT, ps[:], AF.Identity,
                    bias=lb_sb[:, ot:ot + 1], waits=[tok, st_tok[0]],
                    sig=True)
        st_tok[0] = kb.dma(sdram[ot], SOUT, waits=[t1])
        return t1

    kb.dbg("woin", WOIN, waits=[ctx_done])
    ProjFM(kb, lw_d, NKT, lambda kt: WOIN[:, kt, :], evac_s,
           rhs_waits=[ctx_done], ps_guard=[exp_last, ctx_done])
    s_done = ("dy", kb.counts["dy"])
    s_evac_done = kb.tok("scalar")

    def evac_f1(ot, ps, tok):
        return kb.act(QF[:, ot, :], ps[:], AF.Identity, waits=[tok], sig=True)

    ProjFM(kb, w1_d, NKT, xn_t, evac_f1, rhs_waits=xn_toks + [ctx_done],
           ps_guard=[s_evac_done])
    f1_tok = kb.tok("scalar")

    def evac_f2(ot, ps, tok):
        return kb.act(KF[:, ot, :], ps[:], AF.Identity, waits=[tok], sig=True)

    ProjFM(kb, w2_d, NKT, lambda kt: QF[:, kt, :], evac_f2,
           rhs_waits=[f1_tok], ps_guard=[f1_tok])
    f2_tok = kb.tok("scalar")

    upd_tok = [None]

    def evac_f3(ot, ps, tok):
        tx = kb.dma(XT, xdram[ot], waits=[upd_tok[0]])
        tss = kb.dma(ST, sdram[ot], waits=[s_done, upd_tok[0]])
        t1 = kb.act(GT, ps[:], AF.Tanh, bias=fb_sb[:, ot:ot + 1],
                    waits=[tok, upd_tok[0], vt_tok], sig=False)
        t1 = kb.act(GT, GT, AF.Silu, waits=[tok], sig=True)
        t2 = kb.tt(GT, GT, ST, ALU.mult, waits=[t1, tss], sig=False)
        t2 = kb.tt(XT, XT, GT, ALU.add, waits=[tx], sig=True)
        upd_tok[0] = kb.dma(xodram[ot], XT, waits=[t2])
        return t2

    kb.dbg("s", sdram, waits=[s_done])
    kb.dbg("f2", KF, waits=[f2_tok])
    ProjFM(kb, w3_d, NKT, lambda kt: KF[:, kt, :], evac_f3,
           rhs_waits=[f2_tok], ps_guard=[f2_tok])



def emit_layer1(kb, W, xdram, xodram):
    ones_bf = kb.sb("ones_bf", [128, 4], BF16)
    ones_row = kb.sb("ones_row", [1, 128], BF16)
    kb.memset(ones_bf[:], 1.0, sig=True)
    t_or = kb.memset(ones_row[:], 1.0, sig=True)
    mq = kb.sb("l1mk", [16, 4], F32)
    VA = kb.sb("VA_sb", [128, 2, 72], BF16)
    t_va = kb.memset(VA[:], 0.0, sig=True)
    t_va = kb.memset(VA[:, 0, 64:65], 1.0, sig=True)
    t_va = kb.memset(VA[:, 1, 64:65], 1.0, sig=True)
    NM = kb.sb("NM_sb", [1, 2048], BF16)
    TRI = kb.sb("TRI_sb", [128, 128], BF16)
    tri_np = (np.arange(128)[:, None] <= np.arange(128)[None, :])
    tri_d = kb.cin("l1tri", tri_np.astype(BF))
    t_tri = kb.dma(TRI[:], tri_d)
    sel_np = np.zeros((C, 16), np.float32)
    for d in range(C):
        sel_np[d, d // 64] = 1.0
    sel_d = kb.cin("l1sel16", sel_np.astype(BF))
    sel_sb = kb.sb("sel16", [128, NKT, 16], BF16)
    tok_sel = kb.dma(sel_sb[:], sel_d.rearrange("(k p) m -> p k m", p=128))

    qkvb = W["b_qkv"]
    qb_sb, tqb = stage_cols(kb, "l1qb", qkvb[0:1024], scale=0.125)
    kb_sb, tkb = stage_cols(kb, "l1kb", qkvb[1024:2048])
    ob_sb, tob = stage_cols(kb, "l1ob", W["b_out"])
    vb_row_d = kb.cin("l1vb_row",
                      np.ascontiguousarray(qkvb[2048:3072].reshape(1, 1024)))
    vb_row = kb.sb("l1vb_sb", [1, 1024], F32)
    vb_bf = kb.sb("l1vb_bf", [1, 1024], BF16)
    tvb = kb.dma(vb_row[:], vb_row_d)
    tvb = kb.copy(vb_bf[:], vb_row[:], waits=[tvb], sig=True)
    b1_sb, tb1 = stage_cols(kb, "l1b1", W["b1"], n=32)
    b2_sb, tb2 = stage_cols(kb, "l1b2", W["b2"])

    wqkv_d = kb.cin("l1wqkv", W["w_qkv"])
    wout_d = kb.cin("l1wout", W["w_out"])
    u1_d = kb.cin("l1u1", W["U1"])
    v1_d = kb.cin("l1v1", W["V1"])
    u2_d = kb.cin("l1u2", W["U2"])
    v2_d = kb.cin("l1v2", W["V2"])

    dk = 64
    inv = (10000.0 ** (-(np.arange(0, dk, 2, dtype=np.float32)) / dk)
           ).astype(np.float32)
    ang = np.arange(T, dtype=np.float32)[:, None] * inv[None, :]
    cos, sin = np.cos(ang), np.sin(ang)
    cos_e = np.repeat(cos.T, 2, axis=0)
    sin_e = np.repeat(sin.T, 2, axis=0)
    sgn = np.ones((64, 1), np.float32)
    sgn[0::2] = -1.0
    sin_s = sin_e * sgn
    cos128 = np.tile(cos_e, (2, 1))
    sin128 = np.tile(sin_s, (2, 1))
    cq_d = kb.cin("l1cosq", (cos128 * 0.125).astype(BF))
    sq_d = kb.cin("l1sinq", (sin128 * 0.125).astype(BF))
    ck_d = kb.cin("l1cosk", cos128.astype(BF))
    sk_d = kb.cin("l1sink", sin128.astype(BF))
    perm = np.zeros((128, 128), np.float32)
    for f in range(64):
        perm[2 * f, 2 * f + 1] = 1.0
        perm[2 * f + 1, 2 * f] = 1.0
    perm_d = kb.cin("l1perm", perm.astype(BF))
    PERM = kb.sb("l1perm_sb", [128, 128], BF16)
    t_perm = kb.dma(PERM[:], perm_d)

    nm_dram = kb.dtmp("l1nm", [16, 2048], BF16)
    sdram = kb.dtmp("s_spill", [NKT, 128, 2048], F32)
    dn_dram = kb.dtmp("l1dn", [1, 2048], F32)

    XN = kb.abuf([128, NKT, 2064], BF16)
    H16 = kb.abuf([128, 2048], F32)
    kb.WST = kb.abuf([128, 2, NKT, 128], F32)
    kb.WBF = kb.abuf([128, 2, 2, NKT, 128], BF16)
    A = kb.abuf([128, NKT, 2048], BF16)
    Bb = kb.abuf([128, NKT, 2048], BF16)
    Cc = kb.abuf([128, NKT * 4096], U8)
    Dd = kb.abuf([128, NKT * 4096], U8)
    Ff = kb.abuf([128, 2048], F32)

    QF, KF = A, Bb
    VT = Cc[:, 0:32768].bitcast(BF16).rearrange("p (a b) -> p a b", b=1024)
    ROPET = Cc[:, 0:16384].bitcast(BF16).rearrange("p (a b) -> p a b", b=2048)
    GT = Cc[:, 0:8192].bitcast(F32)
    XT = Cc[:, 8192:16384].bitcast(F32)
    ST = Cc[:, 16384:24576].bitcast(F32)
    HS = Cc[:, 24576:28672].bitcast(BF16)
    HS2 = Cc[:, 28672:32768].bitcast(BF16)
    MB = Dd[:, 16384:24576].bitcast(F32)
    RB = Dd[:, 24576:32768].bitcast(F32)
    VWBF = Dd[:, 0:16384].bitcast(BF16).rearrange("p (a b) -> p a b", b=1024)
    VWST = Dd[:, 16384:20480].bitcast(F32)
    SQ2 = Dd[:, 0:32768].bitcast(BF16).rearrange("p (a b) -> p a b", b=2048)
    NQ = Dd[:, 24576:32768].bitcast(F32)
    WOIN = Dd[:, 0:32768].bitcast(BF16).rearrange("p (a b) -> p a b", b=2048)
    TQ = Dd[:, 16384:24576].bitcast(F32)
    SW = Dd[:, 24576:32768].bitcast(F32)
    XST = Ff[:, 0:2048]
    EX = Ff[:, 0:2048].bitcast(BF16).rearrange("p (a b) -> p a b", b=2048)
    SOUT = Ff[:, 0:2048]
    DNRB = H16
    OSH = Ff.bitcast(BF16)

    xn_toks = emit_ln(kb, xdram, XN, A, Bb, XST, H16[0:16, :],
                      RB, MB, ones_bf[:, 0:1], H16)

    def xn_t(kt):
        return XN[:, kt, 16:2064]

    t_cq = kb.dma(ROPET[:, 0, :], cq_d)
    t_sq = kb.dma(ROPET[:, 1, :], sq_d)
    t_ck = kb.dma(ROPET[:, 2, :], ck_d)
    t_sk = kb.dma(ROPET[:, 3, :], sk_d)
    rope_toks = [t_cq, t_sq, t_ck, t_sk, t_perm]

    kb.proj_guard = kb.tok("tensor")

    HSB = HS
    rope_prev = [None]

    def mk_evac_rope(dst, cos_t, sin_t, bias_cols):
        prev = rope_prev

        def evac(ot, ps, tok):
            t1 = kb.act(TQ, ps[:], AF.Identity, bias=bias_cols[:, ot:ot + 1],
                        waits=[tok, prev[0]], sig=True)
            t1b = kb.act(HSB, ps[:], AF.Identity,
                         bias=bias_cols[:, ot:ot + 1], waits=[tok], sig=True)
            tok2 = None
            for c in range(4):
                sl = slice(c * 512, (c + 1) * 512)
                tok2 = kb.mm(ps[:, sl], PERM[:], HSB[:, sl],
                             start=True, stop=True,
                             waits=[t1, t1b] + rope_toks, sig=(c == 3))
            t3 = kb.act(SW, ps[:], AF.Identity, waits=[tok2], sig=True)
            kb.tt(TQ, TQ, cos_t, ALU.mult, waits=[t3, t1])
            kb.tt(SW, SW, sin_t, ALU.mult, waits=[])
            t5 = kb.tt(dst[:, ot, :], TQ, SW, ALU.add, waits=[], sig=True)
            prev[0] = t5
            return t5
        return evac

    ProjFM(kb, wqkv_d, NKT, xn_t,
           mk_evac_rope(QF, ROPET[:, 0, :], ROPET[:, 1, :], qb_sb),
           rhs_waits=xn_toks, ot_off=0)
    q_done = kb.tok("vector")
    ProjFM(kb, wqkv_d, NKT, xn_t,
           mk_evac_rope(KF, ROPET[:, 2, :], ROPET[:, 3, :], kb_sb),
           rhs_waits=xn_toks, ot_off=8, ps_guard=[q_done])
    k_done = kb.tok("vector")

    vw_toks = []
    prev_cast = k_done
    for kt in range(NKT):
        tok = kb.dma(VWST, wqkv_d[kt * 128:(kt + 1) * 128, 2048:3072],
                     waits=[prev_cast, k_done])
        prev_cast = kb.copy(VWBF[:, kt], VWST, waits=[tok, k_done], sig=True)
        vw_toks.append(prev_cast)
    ev_tok = [k_done, k_done]
    for tt in range(16):
        b = tt % 2
        ps = [kb.PA, kb.PB][b]
        tok_mm = None
        for c in range(2):
            sl = slice(c * 512, (c + 1) * 512)
            for kt in range(NKT):
                tok_mm = kb.mm(ps[:, sl], xn_t(kt)[:, tt * 128:(tt + 1) * 128],
                               VWBF[:, kt, sl], start=(kt == 0), stop=False,
                               waits=[vw_toks[kt], ev_tok[b], t_or, tvb,
                                      k_done])
            tok_mm = kb.mm(ps[:, sl], ones_row[:, :], vb_bf[:, sl],
                           start=False, stop=True, waits=[], sig=(c == 1))
        ev_tok[b] = kb.act(VT[:, tt, :], ps[:, 0:1024], AF.Identity,
                           waits=[tok_mm], sig=True)
    vt_tok = kb.tok("scalar")
    kb.dbg("qf", QF, waits=[q_done])
    kb.dbg("kf", KF, waits=[k_done])
    kb.dbg("vt", VT, waits=[vt_tok])

    mm_tok = {}
    for idx, src_t in enumerate((QF, KF)):
        stok = None
        for kt in range(NKT):
            stok = kb.act(SQ2[:, kt], src_t[:, kt, :], AF.Square,
                          waits=[vt_tok, mm_tok.get(0)],
                          sig=(kt == NKT - 1))
        ps = [kb.PA, kb.PB][idx]
        for c in range(4):
            sl = slice(c * 512, (c + 1) * 512)
            for kt in range(NKT):
                mm_tok[idx] = kb.mm(
                    ps[0:16, sl], sel_sb[:, kt, :], SQ2[:, kt, sl],
                    start=(kt == 0), stop=(kt == NKT - 1),
                    waits=[stok, tok_sel, vt_tok],
                    sig=(kt == NKT - 1 and c == 3))
    t_nq = kb.act(H16[0:16, :], kb.PA[0:16, :], AF.Sqrt,
                  waits=[mm_tok[0]], sig=True)
    t_nk = kb.act(NQ[0:16, :], kb.PB[0:16, :], AF.Sqrt,
                  waits=[mm_tok[1]], sig=True)
    t_mk = kb.red(mq[:, 0:1], NQ[0:16, :], ALU.max, waits=[t_nk], sig=True)
    t_mks = kb.ts(mq[:, 1:2], mq[:, 0:1], -1.0, None, ALU.mult,
                  waits=[], hard=[t_mk], sig=True)
    t_b = kb.ts(NQ[0:16, :], H16[0:16, :], mq[:, 1:2], 40.0, ALU.mult,
                ALU.add, waits=[t_nq], hard=[t_mks], sig=True)
    t_nm = kb.dma(nm_dram, NQ[0:16, :], waits=[t_b], eng="gpsimd")

    den_dram = kb.dtmp("l1den", [16, 2048], F32)
    rec_dram = kb.dtmp("l1rec", [16, 2048], F32)
    den_prev = None
    exp_last = t_b
    ex_last = {0: None, 1: None}
    nm_tok = None
    pb_free = t_nk
    for h in range(16):
        dt, po = h // 2, (h % 2) * 64
        nm_tok = kb.dma(NM[:], nm_dram[h:h + 1, :], waits=[t_nm, exp_last], eng="gpsimd")
        ctx_mm = None
        va_tok = {0: t_va, 1: t_va}
        for ki in range(16):
            eb = ki % 2
            tq0 = ki * 128
            ln = T - tq0
            sc_tok = None
            nch = (ln + 511) // 512
            for c in range(nch):
                a0 = c * 512
                a1 = min(ln, a0 + 512)
                sc_tok = kb.mm(
                    kb.PA[:, a0:a1],
                    KF[po:po + 64, dt, tq0:tq0 + 128],
                    QF[po:po + 64, dt, tq0 + a0:tq0 + a1],
                    start=True, stop=False,
                    waits=[exp_last, nm_tok], sig=False)
                sc_tok = kb.mm(
                    kb.PA[:, a0:a1], ones_row[:, :],
                    NM[:, tq0 + a0:tq0 + a1],
                    start=False, stop=True,
                    waits=[], sig=(c == nch - 1))
            e_tok = kb.act(EX[:, eb, 0:ln], kb.PA[:, 0:ln], AF.Exp,
                           waits=[sc_tok, ex_last[eb]], sig=True)
            m_tok = kb.tt(EX[:, eb, 0:128], EX[:, eb, 0:128], TRI[:],
                          ALU.mult, waits=[e_tok, t_tri], sig=True)
            va_t = kb.copy(VA[:, eb, 0:64], VT[:, ki, h * 64:(h + 1) * 64],
                           waits=[vt_tok, va_tok[eb]], sig=True)
            segs = []
            g0 = tq0
            while g0 < T:
                g1 = min(T, (g0 // 512 + 1) * 512)
                segs.append((g0, g1))
                g0 = g1
            for si, (g0, g1) in enumerate(segs):
                ctx_mm = kb.mm(kb.PB[0:65, g0:g1],
                               VA[:, eb, 0:65], EX[:, eb, g0 - tq0:g1 - tq0],
                               start=(ki == 0), stop=(ki == 15),
                               waits=[va_t, m_tok, pb_free],
                               sig=(si == len(segs) - 1))
            ex_last[eb] = ctx_mm
            va_tok[eb] = ctx_mm
            exp_last = e_tok
        d1 = kb.act(DNRB[64:65, :], kb.PB[64:65, :], AF.Identity,
                    waits=[ctx_mm, den_prev], sig=True)
        den_prev = kb.dma(den_dram[h:h + 1], DNRB[64:65, :], waits=[d1],
                          eng="gpsimd")
        if po == 0:
            pb_free = kb.act(WOIN[0:64, dt, :], kb.PB[0:64, :], AF.Identity,
                             waits=[ctx_mm], sig=True)
        else:
            t5 = kb.act(OSH[0:64, 0:2048], kb.PB[0:64, :], AF.Identity,
                        waits=[ctx_mm, ex_last[0], ex_last[1]], sig=True)
            pb_free = kb.dma(WOIN[64:128, dt, :], OSH[0:64, 0:2048],
                             waits=[t5], eng="gpsimd")
        exp_last = pb_free if po else exp_last
    ctx_done = pb_free

    tld = kb.dma(DNRB[0:16, :], den_dram, waits=[den_prev], eng="gpsimd")
    trc = kb.recip(DNRB[0:16, :], DNRB[0:16, :], waits=[tld], sig=True)
    trd = kb.dma(rec_dram, DNRB[0:16, :], waits=[trc], eng="gpsimd")
    norm_btok = {0: trd, 1: trd}
    norm_last = None
    for h in range(16):
        dt, po = h // 2, (h % 2) * 64
        bt = kb.dma(DNRB[po:po + 64, :],
                    rec_dram[h:h + 1].squeeze(0).partition_broadcast(64),
                    waits=[trd, norm_btok[h % 2]], eng="gpsimd")
        norm_last = kb.tt(WOIN[po:po + 64, dt, :], WOIN[po:po + 64, dt, :],
                          DNRB[po:po + 64, :], ALU.mult,
                          waits=[bt, ctx_done], sig=True)
        norm_btok[h % 2] = norm_last
    ctx_done = norm_last

    kb.dbg("woin", WOIN, waits=[ctx_done])

    st_tok = [None]

    def evac_s(ot, ps, tok):
        t1 = kb.act(SOUT, ps[:], AF.Identity,
                    bias=ob_sb[:, ot:ot + 1], waits=[tok, st_tok[0], t_nm],
                    sig=True)
        st_tok[0] = kb.dma(sdram[ot], SOUT, waits=[t1])
        return t1

    ProjFM(kb, wout_d, NKT, lambda kt: WOIN[:, kt, :], evac_s,
           rhs_waits=[ctx_done], ps_guard=[exp_last, ctx_done])
    s_done = ("dy", kb.counts["dy"])
    s_evac_done = kb.tok("scalar")

    PBUF = Dd[:, 0:8192].bitcast(BF16).rearrange("p (a b) -> p a b", b=2048)

    def evac_p(ot, ps, tok):
        return kb.act(PBUF[:, ot, :], ps[:], AF.Identity, waits=[tok],
                      sig=True)

    ProjFM(kb, u1_d, 2, xn_t, evac_p, rhs_waits=xn_toks + [ctx_done],
           ps_guard=[s_evac_done])
    p_tok = kb.tok("scalar")

    QS = Dd[:, 8192:16384].bitcast(BF16).rearrange("p (a b) -> p a b", b=2048)
    u2st = Dd[:, 16384:18432].bitcast(F32).rearrange("p (a b) -> p a b", b=256)
    u2bf = Dd[:, 18432:19456].bitcast(BF16).rearrange("p (a b) -> p a b", b=256)
    hq_last = p_tok
    qmm_last = {0: p_tok, 1: p_tok}
    u2c_prev = [p_tok, p_tok]
    q_mm = None
    for th in range(2):
        tsl = slice(th * 1024, (th + 1) * 1024)
        for fo in range(32):
            b = fo % 2
            wv = v1_d[:, fo * 128:(fo + 1) * 128].rearrange(
                "(k p) m -> p k m", p=128)
            tok_d = kb.dma(kb.WST[:, b, 0:2], wv, waits=[u2c_prev[b]])
            ct = None
            for kt in range(2):
                ct = kb.copy(kb.WBF[:, b, 0, kt], kb.WST[:, b, kt],
                             waits=[tok_d, qmm_last[b]], sig=(kt == 1))
            tu = kb.dma(u2st[:, b, :], u2_d[fo * 128:(fo + 1) * 128, :],
                        waits=[u2c_prev[b]])
            cu = kb.copy(u2bf[:, b, :], u2st[:, b, :], waits=[tu], sig=True)
            u2c_prev[b] = cu
            h_mm = None
            for c in range(2):
                sl = slice(c * 512, (c + 1) * 512)
                gl = slice(th * 1024 + c * 512, th * 1024 + (c + 1) * 512)
                for kt in range(2):
                    h_mm = kb.mm(kb.PA[:, sl], kb.WBF[:, b, 0, kt],
                                 PBUF[:, kt, gl],
                                 start=(kt == 0), stop=(kt == 1),
                                 waits=[ct, p_tok, hq_last],
                                 sig=(c == 1 and kt == 1))
            hsb = [HS, HS2][b]
            hq = kb.act(hsb[:, 0:1024], kb.PA[:, 0:1024], AF.Gelu_apprx_tanh,
                        bias=b1_sb[:, fo:fo + 1],
                        waits=[h_mm, qmm_last[b]], sig=True)
            hq_last = hq
            for qt in range(2):
                for c in range(2):
                    sl = slice(qt * 1024 + c * 512, qt * 1024 + (c + 1) * 512)
                    hsl = slice(c * 512, (c + 1) * 512)
                    q_mm = kb.mm(kb.PB[:, sl],
                                 u2bf[:, b, qt * 128:(qt + 1) * 128],
                                 hsb[:, hsl],
                                 start=(fo == 0), stop=(fo == 31),
                                 waits=[cu, hq, s_evac_done],
                                 sig=(qt == 1 and c == 1))
            qmm_last[b] = q_mm
        eq = None
        for qt in range(2):
            eq = kb.act(QS[:, qt, tsl], kb.PB[:, qt * 1024:(qt + 1) * 1024],
                        AF.Identity, waits=[q_mm], sig=True)
        hq_last = eq
        qmm_last = {0: eq, 1: eq}
        u2c_prev = [eq, eq]
    q_done2 = kb.tok("scalar")

    kb.dbg("s", sdram, waits=[s_done])
    kb.dbg("pbuf", PBUF, waits=[p_tok])
    kb.dbg("qs", QS, waits=[q_done2])

    upd_tok = [None]

    def evac_f3(ot, ps, tok):
        tx = kb.dma(XT, xdram[ot], waits=[upd_tok[0]])
        tss = kb.dma(ST, sdram[ot], waits=[s_done, upd_tok[0]])
        t1 = kb.act(GT, ps[:], AF.Silu, bias=b2_sb[:, ot:ot + 1],
                    waits=[tok, upd_tok[0]], sig=True)
        t2 = kb.tt(GT, GT, ST, ALU.mult, waits=[t1, tss], sig=False)
        t2 = kb.tt(XT, XT, GT, ALU.add, waits=[tx], sig=True)
        upd_tok[0] = kb.dma(xodram[ot], XT, waits=[t2])
        return t2

    ProjFM(kb, v2_d, NKT, lambda kt: QS[:, kt, :], evac_f3,
           rhs_waits=[q_done2], ps_guard=[q_done2], n_kt=2)


def prep_weights_l1(inputs):
    return dict(
        w_qkv=np.ascontiguousarray(inputs["L1_attn_w_qkv"].T),
        b_qkv=inputs["L1_attn_b_qkv"],
        w_out=np.ascontiguousarray(inputs["L1_attn_w_out"].T),
        b_out=inputs["L1_attn_b_out"],
        U1=inputs["L1_ffn_U1"], V1=inputs["L1_ffn_V1"],
        b1=inputs["L1_ffn_b1"],
        U2=inputs["L1_ffn_U2"], V2=inputs["L1_ffn_V2"],
        b2=inputs["L1_ffn_b2"],
    )


def run_l1(x_full, inputs, dbg=False):
    W = prep_weights_l1(inputs)
    kb = KB()
    kb.dbg_on = dbg
    xin = kb.din("Xin", [NKT, 128, 2048], F32)
    xout = kb.dout("Xout", [NKT, 128, 2048], F32)
    emit_layer1(kb, W, xin, xout)
    kb.build()
    in_maps = []
    for k in range(NCORES):
        m = dict(kb.consts)
        m["Xin"] = np.ascontiguousarray(x_full[k].T.reshape(NKT, 128, 2048))
        in_maps.append(m)
    res = run_bass_kernel_spmd(kb.nc, in_maps, core_ids=list(range(NCORES)))
    out = np.stack([
        res.results[k]["Xout"].reshape(1024, 2048).T for k in range(NCORES)])
    _nancheck("L1", out)
    if dbg:
        return out, res.results
    return out



def emit_layer0a(kb, W, xpdram, cfu_out):
    ones_bf = kb.sb("ones_bf", [128, 4], BF16)
    kb.memset(ones_bf[:], 1.0, sig=True)
    ONESF = kb.sb("onesf", [128, 256], F32)
    t_of = kb.memset(ONESF[:], 1.0, sig=True)
    id_d = kb.cin("id128", np.eye(128).astype(BF))
    ID = kb.sb("id128_sb", [128, 128], BF16)
    t_id = kb.dma(ID[:], id_d)
    bf_sb, _ = stage_cols(kb, "l0bf", W["bf"])
    bi_sb, _ = stage_cols(kb, "l0bi", W["bi"])
    bc_sb, _ = stage_cols(kb, "l0bc", W["bc"])
    inc_sb, _ = stage_cols(kb, "l0inc", W["in_c"])
    wf_d = kb.cin("l0wf", W["wfT"])
    wi_d = kb.cin("l0wi", W["wiT"])
    wc_d = kb.cin("l0wc", W["wcT"])
    u_d = {0: kb.cin("l0uf", W["ufT"]), 1: kb.cin("l0ui", W["uiT"]),
           2: kb.cin("l0uc", W["ucT"])}
    uo_d = kb.cin("l0uo", W["uoT"])
    ubf_dram = kb.dtmp("l0ubf", [3, 1024, 1024], BF16)

    XNP = kb.abuf([128, NKT, 2064], BF16)
    H16 = kb.abuf([128, 2048], F32)
    kb.WST = kb.abuf([128, 2, NKT, 128], F32)
    kb.WBF = kb.abuf([128, 2, 2, NKT, 128], BF16)
    A = kb.abuf([128, NKT * 4096], U8)
    Bb = kb.abuf([128, NKT * 4096], U8)
    Cc = kb.abuf([128, NKT * 4096], U8)
    Dd = kb.abuf([128, NKT * 4096], U8)
    Ff = kb.abuf([128, 2048], F32)

    Abf = A.bitcast(BF16).rearrange("p (a b) -> p a b", b=2048)
    Bbf = Bb.bitcast(BF16).rearrange("p (a b) -> p a b", b=2048)
    XF = Abf
    XI = Bbf
    XC = Cc.bitcast(BF16).rearrange("p (a b) -> p a b", b=2048)
    MB = Dd[:, 16384:24576].bitcast(F32)
    RB = Dd[:, 24576:32768].bitcast(F32)
    C = Dd[:, 0:8192].bitcast(F32).rearrange("p (a b) -> p a b", b=256)
    CBF = Dd[:, 8192:12288].bitcast(BF16).rearrange("p (a b) -> p a b", b=256)
    FG = Dd[:, 16384:24576].bitcast(F32).rearrange("p (a b) -> p a b", b=256)
    IG = Dd[:, 24576:32768].bitcast(F32).rearrange("p (a b) -> p a b", b=256)
    UBUF = Dd[:, 12288:16384].bitcast(BF16).rearrange(
        "p (a b c) -> p a b c", b=NKT, c=128)
    GG = Ff.bitcast(F32).rearrange("p (a b) -> p a b", b=256)
    XST = Ff[:, 0:2048]

    xn_toks = emit_ln(kb, xpdram, XNP, Abf, Bbf, XST, H16[0:16, :],
                      RB, MB, ones_bf[:, 0:1], H16)
    ln_done = kb.tok("vector")

    def xnp_t(kt):
        return XNP[:, kt, 16:2064]

    kb.proj_guard = kb.tok("tensor")

    UST = Dd[:, 0:2048].bitcast(BF16).rearrange(
        "p (a b) -> p a b", b=128)
    ucast_tok = None
    for g3 in range(3):
        for kt in range(NKT):
            t1 = kb.dma(UST[:, :], u_d[g3][kt * 128:(kt + 1) * 128, :]
                        .rearrange("p (a b) -> p a b", b=128),
                        waits=[ucast_tok], eng="gpsimd")
            ucast_tok = kb.dma(ubf_dram[g3, kt * 128:(kt + 1) * 128, :]
                               .rearrange("p (a b) -> p a b", b=128),
                               UST[:, :], waits=[t1], eng="gpsimd")
    u_ready = ucast_tok

    def mk_evac(dst):
        def evac(ot, ps, tok):
            return kb.act(dst[:, ot, :], ps[:], AF.Identity, waits=[tok],
                          sig=True)
        return evac

    ProjFM(kb, wf_d, NKT, xnp_t, mk_evac(XF), rhs_waits=xn_toks)
    ProjFM(kb, wi_d, NKT, xnp_t, mk_evac(XI), ps_guard=[kb.tok("scalar")])
    ProjFM(kb, wc_d, NKT, xnp_t, mk_evac(XC), ps_guard=[kb.tok("scalar")])
    xw_done = kb.tok("scalar")

    c_tok = None
    for ot in range(NKT):
        c_tok = kb.ts(C[:, ot, :], ONESF[:], inc_sb[:, ot:ot + 1], None,
                      ALU.mult, waits=[t_of, ln_done, u_ready],
                      sig=(ot == NKT - 1))
    cb_tok = None
    for ot in range(NKT):
        cb_tok = kb.copy(CBF[:, ot, :], C[:, ot, :], waits=[c_tok],
                         sig=(ot == NKT - 1))

    gate_ps = [kb.PA, kb.PB]
    u_used = {0: u_ready, 1: u_ready}
    for t in range(8):
        tsl = slice(t * 256, (t + 1) * 256)
        evac_toks = []
        for gi, (XG, bias, dst, fn) in enumerate((
                (XF, bf_sb, FG, AF.Sigmoid),
                (XI, bi_sb, IG, AF.Sigmoid),
                (XC, bc_sb, GG, AF.Tanh))):
            ps = gate_ps[gi % 2]
            guard = evac_toks[-2] if len(evac_toks) >= 2 else cb_tok
            mms = None
            for ot in range(NKT):
                ub = ot % 2
                osl = slice(ot * 256, (ot + 1) * 256)
                tu = kb.dma(UBUF[:, ub], ubf_dram[gi, :, ot * 128:
                            (ot + 1) * 128].rearrange(
                                "(k p) m -> p k m", p=128),
                            waits=[u_used[ub], u_ready])
                for kt in range(NKT):
                    mms = kb.mm(ps[:, osl], UBUF[:, ub, kt, :],
                                CBF[:, kt, :],
                                start=(kt == 0), stop=False,
                                waits=[tu, cb_tok, guard, t_id, xw_done])
                mms = kb.mm(ps[:, osl], ID[:], XG[:, ot, tsl],
                            start=False, stop=True, sig=True)
                u_used[ub] = mms
            ev = None
            for ot in range(NKT):
                osl = slice(ot * 256, (ot + 1) * 256)
                ev = kb.act(dst[:, ot, :], ps[:, osl], fn,
                            bias=bias[:, ot:ot + 1], waits=[mms],
                            sig=(ot == NKT - 1))
            evac_toks.append(ev)
        t1 = None
        for ot in range(NKT):
            kb.tt(FG[:, ot, :], FG[:, ot, :], C[:, ot, :], ALU.mult,
                  waits=[evac_toks[0], evac_toks[2]])
            kb.tt(IG[:, ot, :], IG[:, ot, :], GG[:, ot, :], ALU.mult,
                  waits=[evac_toks[1]])
            t1 = kb.tt(C[:, ot, :], FG[:, ot, :], IG[:, ot, :], ALU.add,
                       waits=[], sig=(ot == NKT - 1))
        cb_tok = None
        for ot in range(NKT):
            cb_tok = kb.copy(CBF[:, ot, :], C[:, ot, :], waits=[t1],
                             sig=(ot == NKT - 1))

    cf_prev = [None]

    def evac_cfu(ot, ps, tok):
        t1 = kb.act(H16[:, 0:256], ps[:, 0:256], AF.Identity,
                    waits=[tok, cf_prev[0]], sig=True)
        cf_prev[0] = kb.dma(cfu_out[ot], H16[:, 0:256], waits=[t1])
        return t1

    ProjFM(kb, uo_d, NKT, lambda kt: CBF[:, kt, :], evac_cfu, T=256,
           rhs_waits=[cb_tok], ps_guard=[kb.tok("scalar")])


def emit_layer0b(kb, W, xdram, cfudram, xodram):
    ones_bf = kb.sb("ones_bf", [128, 4], BF16)
    kb.memset(ones_bf[:], 1.0, sig=True)
    ZC = kb.sb("zeroc", [128, 4], F32)
    t_zc = kb.memset(ZC[:], 0.0, sig=True)
    sel2_np = np.zeros((128, 32), np.float32)
    sel2_np[0:64, 0] = 1.0
    sel2_np[64:128, 1] = 1.0
    sel2_d = kb.cin("l0sel2", sel2_np[:, 0:2].astype(BF))
    SEL2 = kb.sb("sel2_sb", [128, 2], BF16)
    t_sel2 = kb.dma(SEL2[:], sel2_d)
    mixw = {}
    for nm in ("x_w", "x_a", "x_g", "x_k", "x_v", "x_r"):
        vec = W[nm].reshape(-1)
        a, _ = stage_cols(kb, "l0p_" + nm, 1.0 + vec)
        b, _ = stage_cols(kb, "l0m_" + nm, -vec)
        mixw[nm] = (a, b)
    kk_sb, _ = stage_cols(kb, "l0kk", W["k_k"].reshape(-1))
    rk_sb, _ = stage_cols(kb, "l0rk", W["r_k"].reshape(-1))
    lnw_sb, _ = stage_cols(kb, "l0lnw", W["ln_w"])
    lnb_sb, _ = stage_cols(kb, "l0lnb", W["ln_b"])
    bo_sb, _ = stage_cols(kb, "l0bo", W["bo"])
    w0_sb, _ = stage_cols(kb, "l0w0",
                          np.pad(W["w0"].reshape(-1), (0, 112)), n=1)
    a0_sb, _ = stage_cols(kb, "l0a0",
                          np.pad(W["a0"].reshape(-1), (0, 112)), n=1)
    w1_d = kb.cin("l0w1", W["w1"])
    w2_d = kb.cin("l0w2", W["w2"])
    a1_d = kb.cin("l0a1", W["a1"])
    a2_d = kb.cin("l0a2", W["a2"])
    g1_d = kb.cin("l0g1", W["g1"])
    g2_d = kb.cin("l0g2", W["g2"])
    wr_d = kb.cin("l0wr", W["Wr"])
    wk_d = kb.cin("l0wk", W["Wk"])
    wv_d = kb.cin("l0wv", W["Wv"])
    wo_d = kb.cin("l0wo", W["Wo"])
    wof_d = kb.cin("l0wof", W["woT"])
    sdram = kb.dtmp("s_spill", [NKT, 128, 2048], F32)
    row_dram = kb.dtmp("l0rows", [16, 2048], F32)
    row2_dram = kb.dtmp("l0rows2", [4, 2048], F32)

    XN = kb.abuf([128, NKT, 2064], BF16)
    H16 = kb.abuf([128, 4096], F32)
    kb.WST = kb.abuf([128, 2, NKT, 128], F32)
    kb.WBF = kb.abuf([128, 2, 2, NKT, 128], BF16)
    A = kb.abuf([128, NKT, 2048], BF16)
    Bb = kb.abuf([128, NKT, 2048], BF16)
    Cc = kb.abuf([128, NKT * 4096], U8)
    Dd = kb.abuf([128, NKT * 4096], U8)
    Ff = kb.abuf([128, 2048], F32)
    Q0 = H16[0:2, 0:2048]
    Q1 = H16[0:2, 2048:4096]

    GA = A
    WOIN = Bb
    MB = Dd[:, 16384:24576].bitcast(F32)
    RB = Dd[:, 24576:32768].bitcast(F32)
    XST = Ff[:, 0:2048]
    KK = Cc[:, 0:8192].bitcast(F32)
    VV = Cc[:, 8192:16384].bitcast(F32)
    RRt = Cc[:, 16384:24576].bitcast(F32)
    STT = Cc[:, 24576:32768].bitcast(F32)
    WBC = Dd[:, 0:8192].bitcast(F32)
    RAB = Dd[:, 8192:16384].bitcast(F32)
    SQH = Dd[:, 16384:20480].bitcast(BF16)
    XWK = Dd[:, 20480:28672].bitcast(F32)
    RSB = Dd[:, 28672:32768].bitcast(BF16)
    CFUB = Dd.bitcast(BF16).rearrange("p (a b) -> p a b", b=2048)

    xn_toks = emit_ln(kb, xdram, XN, A, Bb, XST, H16[0:16, 0:2048],
                      RB, MB, ones_bf[:, 0:1], H16[:, 0:2048])
    ln_done = kb.tok("vector")

    def xn_t(kt):
        return XN[:, kt, 16:2064]

    def xnsh_t(kt):
        return XN[:, kt, 15:2063]

    kb.proj_guard = kb.tok("tensor")

    HW = Ff[:, 0:2048].bitcast(BF16)
    hw_tok = [None]

    def evac_hidden(ot, ps, tok):
        t1 = kb.act(HW[0:64, 0:2048], ps[0:64, :], AF.Tanh,
                    waits=[tok, hw_tok[0]], sig=True)
        hw_tok[0] = t1
        return t1

    ProjFM(kb, w1_d, 1, xn_t, evac_hidden, rhs_waits=xn_toks,
           scale_cols=mixw["x_w"][0], scale2_cols=mixw["x_w"][1],
           rhs2_fn=xnsh_t, m_rows=64)
    t_hw = hw_tok[0]
    w2bf = kb.sb("l0w2bf", [64, 2, 16], BF16)
    w2st = kb.sb("l0w2st", [64, 2, 16], F32)
    t_w2 = kb.dma(w2st[:, 0, :], w2_d, waits=[])
    t_w2 = kb.copy(w2bf[:, 0, :], w2st[:, 0, :], waits=[t_w2], sig=True)
    zmm = None
    for c in range(4):
        sl = slice(c * 512, (c + 1) * 512)
        zmm = kb.mm(kb.PA[0:16, sl], w2bf[0:64, 0, :], HW[0:64, sl],
                    start=True, stop=True, waits=[t_w2, t_hw],
                    sig=(c == 3))
    t_w16 = kb.act(H16[0:16, 0:2048], kb.PA[0:16, :], AF.Sigmoid,
                   bias=w0_sb[0:16, 0:1], waits=[zmm], sig=True)
    t_w16b = kb.ts(H16[0:16, 0:2048], H16[0:16, 0:2048],
                   float(np.exp(-0.5)), None,
                   ALU.mult, waits=[t_w16], sig=True, eng="vector")
    t_wrow = kb.dma(row_dram, H16[0:16, 0:2048], waits=[t_w16b])
    ProjFM(kb, a1_d, 1, xn_t, evac_hidden, rhs_waits=xn_toks,
           scale_cols=mixw["x_a"][0], scale2_cols=mixw["x_a"][1],
           rhs2_fn=xnsh_t, m_rows=64, ps_guard=[t_w16])
    t_ha = hw_tok[0]
    t_a2 = kb.dma(w2st[:, 1, :], a2_d, waits=[])
    t_a2 = kb.copy(w2bf[:, 1, :], w2st[:, 1, :], waits=[t_a2], sig=True)
    zmm = None
    for c in range(4):
        sl = slice(c * 512, (c + 1) * 512)
        zmm = kb.mm(kb.PA[0:16, sl], w2bf[0:64, 1, :], HW[0:64, sl],
                    start=True, stop=True, waits=[t_a2, t_ha, t_w16b],
                    sig=(c == 3))
    t_a16 = kb.act(H16[0:16, 0:2048], kb.PA[0:16, :], AF.Sigmoid,
                   bias=a0_sb[0:16, 0:1], waits=[zmm, t_wrow], sig=True)
    arow_dram = kb.dtmp("l0arow", [16, 2048], F32)
    t_arow = kb.dma(arow_dram, H16[0:16, 0:2048], waits=[t_a16])
    ProjFM(kb, g1_d, 1, xn_t, evac_hidden, rhs_waits=xn_toks,
           scale_cols=mixw["x_g"][0], scale2_cols=mixw["x_g"][1],
           rhs2_fn=xnsh_t, m_rows=64, ps_guard=[t_a16])
    t_hg = hw_tok[0]

    kb.dbg("hg", HW[0:64, 0:2048], waits=[t_hg])

    def evac_ga(ot, ps, tok):
        return kb.act(GA[:, ot, :], ps[:], AF.Sigmoid, waits=[tok], sig=True)

    ProjFM(kb, g2_d, NKT, lambda kt: HW[0:64, 0:2048], evac_ga,
           rhs_waits=[t_hg], n_kt=1, k_rows=64, ps_guard=[t_a16])
    ga_done = kb.tok("scalar")

    bs_dram = kb.dtmp("l0bs", [2, 2048], F32)
    woin_tok = ga_done
    g = {}
    for dt in range(NKT):
        def evac_kk(ot, ps, tok, dt=dt, g=g):
            t1 = kb.ts(KK, ps[:], kk_sb[:, dt:dt + 1], None, ALU.mult,
                       waits=[tok], sig=True)
            return t1
        ProjFM(kb, wk_d, 1, xn_t, evac_kk, rhs_waits=xn_toks,
               scale_cols=mixw["x_k"][0], scale2_cols=mixw["x_k"][1],
               rhs2_fn=xnsh_t, ot_off=dt,
               ps_guard=[g.get("rr_e") or woin_tok])
        kk_e = kb.tok("vector")

        def evac_vv(ot, ps, tok, g=g):
            return kb.act(VV, ps[:], AF.Identity,
                          waits=[tok, g.get("x3")], sig=True)
        ProjFM(kb, wv_d, 1, xn_t, evac_vv, rhs_waits=xn_toks,
               scale_cols=mixw["x_v"][0], scale2_cols=mixw["x_v"][1],
               rhs2_fn=xnsh_t, ot_off=dt, ps_guard=[kk_e])
        vv_e = kb.tok("scalar")

        def evac_rr(ot, ps, tok, g=g):
            return kb.act(RRt, ps[:], AF.Identity,
                          waits=[tok, g.get("scan")], sig=True)
        ProjFM(kb, wr_d, 1, xn_t, evac_rr, rhs_waits=xn_toks,
               scale_cols=mixw["x_r"][0], scale2_cols=mixw["x_r"][1],
               rhs2_fn=xnsh_t, ot_off=dt, ps_guard=[vv_e])
        rr_e = kb.tok("scalar")
        g["rr_e"] = rr_e

        sq_t = kb.act(SQH, KK, AF.Square, waits=[kk_e, g.get("sqh")],
                      sig=True)
        nmm = None
        for c in range(4):
            sl = slice(c * 512, (c + 1) * 512)
            nmm = kb.mm(kb.PB[0:2, sl], SEL2[:], SQH[:, sl], start=True,
                        stop=True, waits=[sq_t, t_sel2, g.get("h16")],
                        sig=(c == 3))
        n1 = kb.act(Q0, kb.PB[0:2, :], AF.Sqrt, bias=1e-24,
                    waits=[nmm], sig=True)
        n2 = kb.recip(Q0, Q0, waits=[n1], sig=True)
        b1t = kb.tt(XWK, KK, RRt, ALU.mult,
                    waits=[kk_e, rr_e, g.get("xwk")], sig=False)
        b2t = kb.ts(SQH, XWK, rk_sb[:, dt:dt + 1], None, ALU.mult,
                    waits=[], sig=True)
        bmm = None
        for c in range(4):
            sl = slice(c * 512, (c + 1) * 512)
            bmm = kb.mm(kb.PB[0:2, sl], SEL2[:], SQH[:, sl], start=True,
                        stop=True, waits=[b2t, n1], sig=(c == 3))
        bev = kb.act(Q1, kb.PB[0:2, :], AF.Identity, waits=[bmm],
                     sig=True)
        bsc = kb.tt(Q1, Q1, Q0, ALU.mult, waits=[bev], hard=[n2], sig=True)
        bst = kb.dma(bs_dram, Q1, waits=[bsc], eng="gpsimd")
        a2r = kb.dma(Q1, arow_dram[2 * dt:2 * dt + 2],
                     waits=[bst], eng="gpsimd")
        n3 = kb.tt(Q0, Q0, Q1, ALU.mult, waits=[a2r, bsc], hard=[n2],
                   sig=True)
        n4 = kb.dma(row2_dram[0:2], Q0, waits=[n3], eng="gpsimd")
        b1 = kb.dma(RAB[0:64, :],
                    row2_dram[0:1].squeeze(0).partition_broadcast(64),
                    waits=[n4, g.get("rab")], eng="gpsimd")
        b2 = kb.dma(RAB[64:128, :],
                    row2_dram[1:2].squeeze(0).partition_broadcast(64),
                    waits=[n4], eng="gpsimd")
        bw1 = kb.dma(WBC[0:64, :],
                     row_dram[2 * dt:2 * dt + 1].squeeze(0)
                     .partition_broadcast(64),
                     waits=[t_wrow, g.get("scan")], eng="gpsimd")
        bw2 = kb.dma(WBC[64:128, :],
                     row_dram[2 * dt + 1:2 * dt + 2].squeeze(0)
                     .partition_broadcast(64),
                     waits=[t_wrow, g.get("scan")], eng="gpsimd")
        dbg_toks = []
        if getattr(kb, "dbg_on", False) and dt == 0:
            def dmp(name, ap, w):
                d = kb.dout("dbg_" + name, list(ap.shape), ap.dtype)
                t = kb.dma(d, ap, waits=w, eng="gpsimd")
                dbg_toks.append(t)
                return t
            dmp("kk0", KK, [kk_e])
            dmp("vv0", VV, [vv_e])
            dmp("rr0", RRt, [rr_e])
            dmp("rab0", RAB, [b1, b2])
            dmp("wbc0", WBC, [bw1, bw2])
        kv1 = kb.tt(KK, KK, VV, ALU.mult, waits=[kk_e, vv_e, b1t] + dbg_toks,
                    sig=False)
        kv2 = kb.tt(KK, KK, RAB, ALU.mult, waits=[b1, b2], sig=True)
        sc_t = kb.emit(
            "vector",
            lambda e, o=STT, d0=WBC, d1=KK, ini=ZC[:, 0:1]:
                e.tensor_tensor_scan(o, d0, d1, ini, ALU.mult, ALU.add),
            waits=[kv2, bw1, bw2, t_zc, g.get("stt")], sig=True)
        bb1 = kb.dma(RAB[0:64, :],
                     bs_dram[0:1].squeeze(0).partition_broadcast(64),
                     waits=[bst, kv2], eng="gpsimd")
        bb2 = kb.dma(RAB[64:128, :],
                     bs_dram[1:2].squeeze(0).partition_broadcast(64),
                     waits=[bst, kv2], eng="gpsimd")
        dbg2 = []
        if getattr(kb, "dbg_on", False) and dt == 0:
            d = kb.dout("dbg_stt0", list(STT.shape), STT.dtype)
            dbg2.append(kb.dma(d, STT, waits=[sc_t], eng="gpsimd"))
            d = kb.dout("dbg_bsb0", list(RAB.shape), RAB.dtype)
            dbg2.append(kb.dma(d, RAB, waits=[bb1, bb2], eng="gpsimd"))
        x1 = kb.tt(XWK, STT, RRt, ALU.mult, waits=[sc_t, rr_e] + dbg2,
                   sig=False)
        x2 = kb.tt(VV, VV, RAB, ALU.mult, waits=[bb1, bb2] + dbg2, sig=False)
        x3 = kb.tt(XWK, XWK, VV, ALU.add, waits=[], sig=True)
        g["stt"] = x1
        g["x3"] = x3
        xb_t = kb.copy(SQH, XWK, waits=[x3, g.get("sel_mm2")], sig=True)
        gq_t = kb.act(RSB, SQH, AF.Square, waits=[xb_t], sig=True)
        m1 = m2 = None
        for c in range(4):
            sl = slice(c * 512, (c + 1) * 512)
            m1 = kb.mm(kb.PB[0:2, sl], SEL2[:], SQH[:, sl], start=True,
                       stop=True, waits=[xb_t, bev], sig=(c == 3))
        s1 = kb.ts(Q0, kb.PB[0:2, :], 1.0 / 64, None, ALU.mult,
                   waits=[m1, n4], sig=True)
        for c in range(4):
            sl = slice(c * 512, (c + 1) * 512)
            m2 = kb.mm(kb.PB[0:2, sl], SEL2[:], RSB[:, sl], start=True,
                       stop=True, waits=[gq_t, s1], sig=(c == 3))
        g["sel_mm2"] = m2
        s2 = kb.act(Q1, Q0, AF.Square, waits=[s1, a2r], sig=True)
        s3 = kb.stt(Q1, kb.PB[0:2, :], 1.0 / 64, Q1,
                    ALU.mult, ALU.subtract, waits=[m2, s2], sig=True)
        t_dv = None
        if getattr(kb, "dbg_on", False) and dt == 0:
            d = kb.dout("dbg_var0", [2, 2048], F32)
            t_dv = kb.dma(d, Q1, waits=[s3], eng="gpsimd")
            d = kb.dout("dbg_mean0", [2, 2048], F32)
            kb.dma(d, Q0, waits=[s1], eng="gpsimd")
            d = kb.dout("dbg_sqh0", [128, 2048], BF16)
            kb.dma(d, SQH, waits=[xb_t, m1], eng="gpsimd")
        s4 = kb.act(Q1, Q1, AF.Sqrt, bias=1e-5, waits=[s3, t_dv], sig=True)
        s5 = kb.recip(Q1, Q1, waits=[s4], sig=True)
        n5 = kb.dma(row2_dram[0:2], Q1, waits=[b1, b2], hard=[]
                    ) if False else None
        n5 = kb.dma(row2_dram[0:2], Q1, waits=[s5, b1, b2], eng="gpsimd")
        s6 = kb.tt(Q0, Q0, Q1, ALU.mult, waits=[], hard=[s5], sig=True)
        s7 = kb.ts(Q0, Q0, -1.0, None, ALU.mult, waits=[s6], sig=True)
        g["h16"] = s7
        n6 = kb.dma(row2_dram[2:4], Q0, waits=[s7], eng="gpsimd")
        c1 = kb.dma(RAB[0:64, :],
                    row2_dram[0:1].squeeze(0).partition_broadcast(64),
                    waits=[n5, x2], eng="gpsimd")
        c2 = kb.dma(RAB[64:128, :],
                    row2_dram[1:2].squeeze(0).partition_broadcast(64),
                    waits=[n5, x2], eng="gpsimd")
        c3 = kb.dma(WBC[0:64, :],
                    row2_dram[2:3].squeeze(0).partition_broadcast(64),
                    waits=[n6, sc_t], eng="gpsimd")
        c4 = kb.dma(WBC[64:128, :],
                    row2_dram[3:4].squeeze(0).partition_broadcast(64),
                    waits=[n6, sc_t], eng="gpsimd")
        g["rab"] = c2
        if getattr(kb, "dbg_on", False) and dt == 0:
            d = kb.dout("dbg_xwk0", [128, 2048], F32)
            t_dx = kb.dma(d, XWK, waits=[x3], eng="gpsimd")
            d = kb.dout("dbg_rsb0", [128, 2048], F32)
            kb.dma(d, RAB, waits=[c1, c2], eng="gpsimd")
            d = kb.dout("dbg_gbb0", [128, 2048], F32)
            kb.dma(d, WBC, waits=[c3, c4], eng="gpsimd")
        else:
            t_dx = None
        o1 = kb.tt(XWK, XWK, RAB, ALU.mult, waits=[c1, c2, x3, t_dx],
                   sig=False)
        o2 = kb.tt(XWK, XWK, WBC, ALU.add, waits=[c3, c4], sig=False)
        o3 = kb.ts(XWK, XWK, lnw_sb[:, dt:dt + 1], lnb_sb[:, dt:dt + 1],
                   ALU.mult, ALU.add, waits=[], sig=False)
        woin_tok = kb.tt(WOIN[:, dt, :], XWK, GA[:, dt, :], ALU.mult,
                         waits=[ga_done], sig=True)
        g["woin"] = woin_tok
        g["xwk"] = woin_tok
        g["sqh"] = m1
        g["scan"] = x1
    woin_done = woin_tok
    kb.dbg("woin", WOIN, waits=[woin_done])
    kb.dbg("ga", GA, waits=[woin_done])
    kb.dbg("w16", row_dram, waits=[t_wrow])
    kb.dbg("a16", arow_dram, waits=[t_arow])

    cfu_waits = [woin_done] + list(getattr(kb, "cfu_wait", []))
    cfu_toks = []
    for kt in range(NKT):
        if len(cfudram.shape) == 4:
            cfu_toks.append(kb.dma(
                CFUB[:, kt, :].rearrange("p (k c) -> p k c", k=8),
                cfudram[:, kt].rearrange("k p c -> p k c"),
                waits=cfu_waits, eng="gpsimd"))
        else:
            cfu_toks.append(kb.dma(CFUB[:, kt, :], cfudram[kt],
                                   waits=cfu_waits, eng="gpsimd"))

    st_tok = [None]
    SOUT = Ff[:, 0:2048]

    def evac_s(ot, ps, tok):
        t1 = kb.act(SOUT, ps[:], AF.Identity, waits=[tok, st_tok[0]],
                    sig=True)
        st_tok[0] = kb.dma(sdram[ot], SOUT, waits=[t1])
        return t1

    ProjFM(kb, wo_d, NKT, lambda kt: WOIN[:, kt, :], evac_s,
           rhs_waits=[woin_done], ps_guard=[woin_done])
    s_done = ("dy", kb.counts["dy"])
    s_evac = kb.tok("scalar")

    GT = Cc[:, 0:8192].bitcast(F32)
    XT = Cc[:, 8192:16384].bitcast(F32)
    ST = Cc[:, 16384:24576].bitcast(F32)
    upd_tok = [None]

    def evac_f3(ot, ps, tok):
        tx = kb.dma(XT, xdram[ot], waits=[upd_tok[0]])
        tss = kb.dma(ST, sdram[ot], waits=[s_done, upd_tok[0]])
        t0 = kb.tt(GT, ps[:], CFUB[:, ot, :], ALU.add,
                   waits=[tok, upd_tok[0]] + cfu_toks, sig=True)
        t1 = kb.act(GT, GT, AF.Tanh, bias=bo_sb[:, ot:ot + 1],
                    waits=[t0], sig=False)
        t1 = kb.act(GT, GT, AF.Silu, waits=[t0], sig=True)
        t2 = kb.tt(GT, GT, ST, ALU.mult, waits=[t1, tss], sig=False)
        t2 = kb.tt(XT, XT, GT, ALU.add, waits=[tx], sig=True)
        upd_tok[0] = kb.dma(xodram[ot], XT, waits=[t2])
        return t2

    kb.dbg("s0", sdram, waits=[s_done])
    ProjFM(kb, wof_d, NKT, xn_t, evac_f3, rhs_waits=xn_toks + [woin_done],
           ps_guard=[s_evac])


def prep_weights_l0(inputs):
    W = dict(
        x_r=inputs["L0_attn_x_r"], x_w=inputs["L0_attn_x_w"],
        x_k=inputs["L0_attn_x_k"], x_v=inputs["L0_attn_x_v"],
        x_a=inputs["L0_attn_x_a"], x_g=inputs["L0_attn_x_g"],
        w0=inputs["L0_attn_w0"], w1=inputs["L0_attn_w1"],
        w2=inputs["L0_attn_w2"], a0=inputs["L0_attn_a0"],
        a1=inputs["L0_attn_a1"], a2=inputs["L0_attn_a2"],
        g1=inputs["L0_attn_g1"], g2=inputs["L0_attn_g2"],
        k_k=inputs["L0_attn_k_k"], r_k=inputs["L0_attn_r_k"],
        Wr=inputs["L0_attn_Wr"], Wk=inputs["L0_attn_Wk"],
        Wv=inputs["L0_attn_Wv"], Wo=inputs["L0_attn_Wo"],
        ln_w=inputs["L0_attn_ln_w"], ln_b=inputs["L0_attn_ln_b"],
        bf=inputs["L0_ffn_bf"], bi=inputs["L0_ffn_bi"],
        bc=inputs["L0_ffn_bc"], bo=inputs["L0_ffn_bo"],
        in_c=inputs["L0_ffn_in_c"],
        wfT=np.ascontiguousarray(inputs["L0_ffn_wf"].T),
        wiT=np.ascontiguousarray(inputs["L0_ffn_wi"].T),
        wcT=np.ascontiguousarray(inputs["L0_ffn_wc"].T),
        ufT=np.ascontiguousarray(inputs["L0_ffn_uf"].T),
        uiT=np.ascontiguousarray(inputs["L0_ffn_ui"].T),
        ucT=np.ascontiguousarray(inputs["L0_ffn_uc"].T),
        uoT=np.ascontiguousarray(inputs["L0_ffn_uo"].T),
        woT=np.ascontiguousarray(inputs["L0_ffn_wo"].T),
    )
    return W


def run_l0(x_full, inputs, dbg=False):
    W = prep_weights_l0(inputs)
    kba = KB()
    kba.dbg_on = dbg
    xpin = kba.din("XPin", [NKT, 128, 2048], F32)
    cfuo = kba.dout("CFUo", [NKT, 128, 256], F32)
    emit_layer0a(kba, W, xpin, cfuo)
    kba.build()
    in_maps = []
    for k in range(NCORES):
        m = dict(kba.consts)
        xp = x_full[:, k * 256:(k + 1) * 256, :]
        m["XPin"] = np.ascontiguousarray(
            xp.transpose(2, 0, 1).reshape(1024, 2048)
        ).reshape(NKT, 128, 2048)
        in_maps.append(m)
    res_a = run_bass_kernel_spmd(kba.nc, in_maps, core_ids=list(range(NCORES)))
    cfu = np.zeros((NKT, 128, 2048), np.float32)
    for k in range(NCORES):
        sl = res_a.results[k]["CFUo"]
        nf = ~np.isfinite(sl)
        if nf.any():
            print(f"[L0a-NANDBG] core {k}: {nf.sum()} nonfinite, "
                  f"tiles={sorted(set(np.where(nf.any(2))[0]))} "
                  f"pos={np.where(nf.any(0).any(0))[0][:6]}", flush=True)
        cfu[:, :, k * 256:(k + 1) * 256] = sl
    kbb = KB()
    kbb.dbg_on = dbg
    xin = kbb.din("Xin", [NKT, 128, 2048], F32)
    cfin = kbb.din("CFUin", [NKT, 128, 2048], F32)
    xout = kbb.dout("Xout", [NKT, 128, 2048], F32)
    emit_layer0b(kbb, W, xin, cfin, xout)
    kbb.build()
    in_maps = []
    for k in range(NCORES):
        m = dict(kbb.consts)
        m["Xin"] = np.ascontiguousarray(x_full[k].T.reshape(NKT, 128, 2048))
        m["CFUin"] = cfu
        in_maps.append(m)
    res_b = run_bass_kernel_spmd(kbb.nc, in_maps, core_ids=list(range(NCORES)))
    out = np.stack([
        res_b.results[k]["Xout"].reshape(1024, 2048).T for k in range(NCORES)])
    nf = ~np.isfinite(out)
    if nf.any():
        for k in range(NCORES):
            if nf[k].any():
                ft = np.where(nf[k].any(0))[0]
                tk = np.where(nf[k].any(1))[0]
                print(f"[L0b-NANDBG] core {k}: n={nf[k].sum()} "
                      f"ftiles={sorted(set(ft // 128))} nt={len(tk)} "
                      f"t[{tk[0]}..{tk[-1]}]", flush=True)
    if dbg:
        return out, res_a.results, res_b.results
    return out



def emit_layer2(kb, W, xdram, xodram):
    ones_bf = kb.sb("ones_bf", [128, 4], BF16)
    ones_row = kb.sb("ones_row", [1, 128], BF16)
    kb.memset(ones_bf[:], 1.0, sig=True)
    t_or = kb.memset(ones_row[:], 1.0, sig=True)
    onesf_col = kb.sb("onesf_col", [128, 1], F32)
    t_ofc = kb.memset(onesf_col[:], 1.0, sig=True)
    mq = kb.sb("l2mk", [16, 4], F32)
    VA = kb.sb("VA_sb", [128, 2, 72], BF16)
    t_va = kb.memset(VA[:], 0.0, sig=True)
    t_va = kb.memset(VA[:, 0, 64:65], 1.0, sig=True)
    t_va = kb.memset(VA[:, 1, 64:65], 1.0, sig=True)
    NM = kb.sb("NM_sb", [1, 2048], BF16)
    WINM = kb.sb("WINM_sb", [128, 256], BF16)
    u = np.arange(256)[None, :]
    p = np.arange(128)[:, None]
    win_np = ((u >= p) & (u - p <= 128)).astype(np.float32)
    win_d = kb.cin("l2win", win_np.astype(BF))
    t_win = kb.dma(WINM[:], win_d)
    sel_np = np.zeros((C, 16), np.float32)
    for d in range(C):
        sel_np[d, d // 64] = 1.0
    sel_d = kb.cin("l2sel16", sel_np.astype(BF))
    sel_sb = kb.sb("sel16", [128, NKT, 16], BF16)
    tok_sel = kb.dma(sel_sb[:], sel_d.rearrange("(k p) m -> p k m", p=128))
    idf_d = kb.cin("idf128", np.eye(128, dtype=np.float32))
    IDF = kb.sb("idf_sb", [128, 128], F32)
    t_idf = kb.dma(IDF[:], idf_d)
    MAGIC = kb.sb("magic", [128, 2], F32)
    t_mg = kb.memset(MAGIC[:, 0:1], 12582912.0, sig=True)
    t_mg = kb.memset(MAGIC[:, 1:2], -12582912.0, sig=True)

    qb_sb, _ = stage_cols(kb, "l2qb", W["bq"], scale=0.125)
    kb_sb, _ = stage_cols(kb, "l2kb", W["bk"])
    ob_sb, _ = stage_cols(kb, "l2ob", W["bo"])
    vb_row_d = kb.cin("l2vb_row",
                      np.ascontiguousarray(W["bv"].reshape(1, 1024)))
    vb_row = kb.sb("l2vb_sb", [1, 1024], F32)
    vb_bf = kb.sb("l2vb_bf", [1, 1024], BF16)
    tvb = kb.dma(vb_row[:], vb_row_d)
    tvb = kb.copy(vb_bf[:], vb_row[:], waits=[tvb], sig=True)
    cw0_sb, _ = stage_cols(kb, "l2cw0", W["conv_w"][:, 0, 0], n=32)
    cw1_sb, _ = stage_cols(kb, "l2cw1", W["conv_w"][:, 0, 1], n=32)
    cw2_sb, _ = stage_cols(kb, "l2cw2", W["conv_w"][:, 0, 2], n=32)
    cb_sb, _ = stage_cols(kb, "l2cb", W["conv_b"], n=32)

    wq_d = kb.cin("l2wq", W["Wq"])
    wk_d = kb.cin("l2wk", W["Wk"])
    wv_d = kb.cin("l2wv", W["Wv"])
    wo_d = kb.cin("l2wo", W["Wo"])
    w1_d = kb.cin("l2w1", np.ascontiguousarray(W["w1"].T))
    w2_d = kb.cin("l2w2", np.ascontiguousarray(W["w2"].T))
    scl125 = kb.sb("scl125", [128, NKT], F32)
    t_s125 = kb.memset(scl125[:], 0.125, sig=True)

    nm_dram = kb.dtmp("l2nm", [16, 2048], BF16)
    sdram = kb.dtmp("s_spill", [NKT, 128, 2048], F32)
    dn_dram = kb.dtmp("l2dn", [1, 2048], F32)
    h_dram = kb.dtmp("l2h", [32, 128, 2048], F32)
    w1q_dram = kb.dtmp("l2w1q", [1024, 4096], BF16)
    w2q_dram = kb.dtmp("l2w2q", [4096, 1024], BF16)
    am_dram = kb.dtmp("l2am", [16, 128], F32)
    sc_dram = kb.dtmp("l2sc", [1, 4], F32)

    XN = kb.abuf([128, NKT, 2064], BF16)
    H16 = kb.abuf([128, 2048], F32)
    off_wst = kb.ar_off
    kb.WST = kb.abuf([128, 2, NKT, 128], F32)
    off_wbf = kb.ar_off
    kb.WBF = kb.abuf([128, 2, 2, NKT, 128], BF16)
    W2A = kb.AR[:, off_wst:off_wst + 8192].bitcast(BF16).rearrange(
        "p (a b) -> p a b", b=128)
    W2B = kb.AR[:, off_wbf:off_wbf + 8192].bitcast(BF16).rearrange(
        "p (a b) -> p a b", b=128)
    Au8 = kb.abuf([128, NKT * 4096], U8)
    Bu8 = kb.abuf([128, NKT * 4096], U8)
    A = Au8.bitcast(BF16).rearrange("p (a b) -> p a b", b=2048)
    Bb = Bu8.bitcast(BF16).rearrange("p (a b) -> p a b", b=2048)
    Cc = kb.abuf([128, NKT * 4096], U8)
    Dd = kb.abuf([128, NKT * 4096], U8)
    Ff = kb.abuf([128, 2048], F32)

    QF, KF = A, Bb
    VT = Cc[:, 0:32768].bitcast(BF16).rearrange("p (a b) -> p a b", b=1024)
    MB = Dd[:, 16384:24576].bitcast(F32)
    RB = Dd[:, 24576:32768].bitcast(F32)
    VWBF = Dd[:, 0:16384].bitcast(BF16).rearrange("p (a b) -> p a b", b=1024)
    VWST = Dd[:, 16384:20480].bitcast(F32)
    SQ2 = Dd[:, 0:32768].bitcast(BF16).rearrange("p (a b) -> p a b", b=2048)
    NQ = Dd[:, 24576:32768].bitcast(F32)
    WOIN = Dd[:, 0:32768].bitcast(BF16).rearrange("p (a b) -> p a b", b=2048)
    XST = Ff[:, 0:2048]
    EX = Ff[:, 0:2048].bitcast(BF16).rearrange("p (a b) -> p a b", b=2048)
    SOUT = Ff[:, 0:2048]
    DNRB = H16
    OSH = Ff.bitcast(BF16)
    XQ1 = A

    xn_toks = emit_ln(kb, xdram, XN, A, Bb, XST, H16[0:16, :],
                      RB, MB, ones_bf[:, 0:1], H16, save_rows=True)
    ln_row_toks = kb.ln_row_toks

    def xn_t(kt):
        return XN[:, kt, 16:2064]

    kb.proj_guard = kb.tok("tensor")

    def evac_q(ot, ps, tok):
        return kb.act(QF[:, ot, :], ps[:], AF.Identity,
                      bias=qb_sb[:, ot:ot + 1], waits=[tok], sig=True)

    def evac_k(ot, ps, tok):
        return kb.act(KF[:, ot, :], ps[:], AF.Identity,
                      bias=kb_sb[:, ot:ot + 1], waits=[tok], sig=True)

    ProjFM(kb, wq_d, NKT, xn_t, evac_q, rhs_waits=xn_toks,
           scale_cols=scl125)
    q_done = kb.tok("scalar")
    ProjFM(kb, wk_d, NKT, xn_t, evac_k, rhs_waits=xn_toks,
           ps_guard=[q_done])
    k_done = kb.tok("scalar")

    vw_toks = []
    prev_cast = k_done
    for kt in range(NKT):
        tok = kb.dma(VWST, wv_d[kt * 128:(kt + 1) * 128, :],
                     waits=[prev_cast, k_done] +
                     (ln_row_toks if kt == 0 else []))
        prev_cast = kb.copy(VWBF[:, kt], VWST, waits=[tok, k_done], sig=True)
        vw_toks.append(prev_cast)
    ev_tok = [k_done, k_done]
    for tt in range(16):
        b = tt % 2
        ps = [kb.PA, kb.PB][b]
        tok_mm = None
        for c in range(2):
            sl = slice(c * 512, (c + 1) * 512)
            for kt in range(NKT):
                tok_mm = kb.mm(ps[:, sl], xn_t(kt)[:, tt * 128:(tt + 1) * 128],
                               VWBF[:, kt, sl], start=(kt == 0), stop=False,
                               waits=[vw_toks[kt], ev_tok[b], t_or, tvb,
                                      k_done])
            tok_mm = kb.mm(ps[:, sl], ones_row[:, :], vb_bf[:, sl],
                           start=False, stop=True, waits=[], sig=(c == 1))
        ev_tok[b] = kb.act(VT[:, tt, :], ps[:, 0:1024], AF.Identity,
                           waits=[tok_mm], sig=True)
    vt_tok = kb.tok("scalar")
    kb.dbg("qf", QF, waits=[q_done])
    kb.dbg("kf", KF, waits=[k_done])
    kb.dbg("vt", VT, waits=[vt_tok])

    AWQ = kb.AR[:, 0:8192].bitcast(F32)
    YWQ = kb.AR[:, 8192:12288].bitcast(BF16)
    SW1C = kb.sb("l2sw1", [128, 4], F32)
    SWACC = kb.sb("l2swacc", [128, 4], F32)

    def weight_mean(wd, rows, cols, accslot, guard):
        tok_prev = guard
        acc_tok = kb.memset(SWACC[:, accslot:accslot + 1], 0.0, sig=True)
        first = True
        for ch in range(rows // 128):
            for half in range(max(1, cols // 2048)):
                csl = slice(half * 2048, min(cols, (half + 1) * 2048))
                n = csl.stop - csl.start
                tok = kb.dma(AWQ[:, 0:n], wd[ch * 128:(ch + 1) * 128, csl],
                             waits=[tok_prev])
                r = kb.red(SW1C[:, 0:1], AWQ[:, 0:n], ALU.add, absval=True,
                           waits=[tok], sig=True)
                acc_tok = kb.tt(SWACC[:, accslot:accslot + 1],
                                SWACC[:, accslot:accslot + 1],
                                SW1C[:, 0:1], ALU.add, waits=[acc_tok],
                                hard=[r] + ([acc_tok] if first else []),
                                sig=True)
                first = False
                tok_prev = r
        return acc_tok

    t_m1 = weight_mean(w1_d, 1024, 4096, 0, vt_tok)
    kb.dbg("swacc", SWACC[:], waits=[t_m1])
    t_m2 = weight_mean(w2_d, 4096, 1024, 1, t_m1)
    sw_mm1 = kb.mm(kb.PA[0:1, 0:128], SWACC[:, 0:1], IDF[:],
                   start=True, stop=True, waits=[t_m1, t_idf, vt_tok],
                   sig=True)
    sw_mm2 = kb.mm(kb.PB[0:1, 0:128], SWACC[:, 1:2], IDF[:],
                   start=True, stop=True, waits=[t_m2], sig=True)
    e1_ = kb.act(H16[0:1, 0:128], kb.PA[0:1, 0:128], AF.Identity,
                 scale=1.0 / (4096 * 1024), waits=[sw_mm1], sig=True)
    e2_ = kb.act(H16[0:1, 128:256], kb.PB[0:1, 0:128], AF.Identity,
                 scale=1.0 / (4096 * 1024), waits=[sw_mm2], sig=True)
    r1_ = kb.red(SW1C[0:1, 0:1], H16[0:1, 0:128], ALU.add, waits=[e1_],
                 sig=True)
    r2_ = kb.red(SW1C[0:1, 1:2], H16[0:1, 128:256], ALU.add, waits=[e2_],
                 sig=True)
    t_sc = kb.ts(SW1C[0:1, 0:2], SW1C[0:1, 0:2], 1e-5, None, ALU.max,
                 waits=[], hard=[r1_, r2_], sig=True)
    t_rc = kb.recip(SW1C[0:1, 2:4], SW1C[0:1, 0:2], waits=[t_sc], sig=True)
    t_scd = kb.dma(sc_dram, SW1C[0:1, 0:4], waits=[t_rc], eng="gpsimd")
    SWB = kb.sb("l2swb", [128, 4], F32)
    t_swb = kb.dma(SWB[:], sc_dram.squeeze(0).partition_broadcast(128),
                   waits=[t_scd], eng="gpsimd")

    def quant_w(wd, wqd, rows, cols, swcol, guard):
        tok_prev = guard
        qt = None
        for ch in range(rows // 128):
            for half in range(max(1, cols // 2048)):
                csl = slice(half * 2048, min(cols, (half + 1) * 2048))
                n = csl.stop - csl.start
                tok = kb.dma(AWQ[:, 0:n], wd[ch * 128:(ch + 1) * 128, csl],
                             waits=[tok_prev])
                t1 = kb.ts(AWQ[:, 0:n], AWQ[:, 0:n],
                           SWB[:, swcol:swcol + 1], None, ALU.mult,
                           waits=[tok, t_swb], sig=True)
                t2 = kb.act(AWQ[:, 0:n], AWQ[:, 0:n], AF.Identity,
                            bias=MAGIC[:, 0:1], waits=[t1, t_mg], sig=False)
                t2 = kb.act(AWQ[:, 0:n], AWQ[:, 0:n], AF.Identity,
                            bias=MAGIC[:, 1:2], waits=[], sig=True)
                t3 = kb.ts(YWQ[:, 0:n], AWQ[:, 0:n],
                           -1.0, 1.0, ALU.max, ALU.min, waits=[t2], sig=True)
                tok_prev = kb.dma(wqd[ch * 128:(ch + 1) * 128, csl],
                                  YWQ[:, 0:n],
                                  waits=[t3], eng="gpsimd")
                qt = tok_prev
        return qt

    t_q1 = quant_w(w1_d, w1q_dram, 1024, 4096, 2, t_swb)
    t_q2 = quant_w(w2_d, w2q_dram, 4096, 1024, 3, t_q1)

    mm_tok = {}
    for idx2, src_t in enumerate((QF, KF)):
        stok = None
        for kt in range(NKT):
            stok = kb.act(SQ2[:, kt], src_t[:, kt, :], AF.Square,
                          waits=[vt_tok, mm_tok.get(0)] +
                          (ln_row_toks if kt == 0 and idx2 == 0 else []),
                          sig=(kt == NKT - 1))
        ps = [kb.PA, kb.PB][idx2]
        for c in range(4):
            sl = slice(c * 512, (c + 1) * 512)
            for kt in range(NKT):
                mm_tok[idx2] = kb.mm(
                    ps[0:16, sl], sel_sb[:, kt, :], SQ2[:, kt, sl],
                    start=(kt == 0), stop=(kt == NKT - 1),
                    waits=[stok, tok_sel, vt_tok, e1_, e2_],
                    sig=(kt == NKT - 1 and c == 3))
    t_nq = kb.act(H16[0:16, :], kb.PA[0:16, :], AF.Sqrt,
                  waits=[mm_tok[0], r1_, r2_], sig=True)
    t_nk = kb.act(NQ[0:16, :], kb.PB[0:16, :], AF.Sqrt,
                  waits=[mm_tok[1]], sig=True)
    t_mk = kb.red(mq[:, 0:1], NQ[0:16, :], ALU.max, waits=[t_nk], sig=True)
    t_mks = kb.ts(mq[:, 1:2], mq[:, 0:1], -1.0, None, ALU.mult,
                  waits=[], hard=[t_mk], sig=True)
    t_b = kb.ts(NQ[0:16, :], H16[0:16, :], mq[:, 1:2], 40.0, ALU.mult,
                ALU.add, waits=[t_nq], hard=[t_mks], sig=True)
    t_nm = kb.dma(nm_dram, NQ[0:16, :], waits=[t_b], eng="gpsimd")

    den_dram = kb.dtmp("l2den", [16, 2048], F32)
    rec_dram = kb.dtmp("l2rec", [16, 2048], F32)
    den_prev = None
    exp_last = t_b
    ex_last = {0: None, 1: None}
    pb_free = t_nk
    for h in range(16):
        dt, po = h // 2, (h % 2) * 64
        nm_tok = kb.dma(NM[:], nm_dram[h:h + 1, :], waits=[t_nm, exp_last],
                        eng="gpsimd")
        ctx_mm = None
        va_tok = {0: t_va, 1: t_va}
        banks_started = set()
        for ki in range(16):
            eb = ki % 2
            tq0 = ki * 128
            ln = min(256, T - tq0)
            sc_tok = kb.mm(
                kb.PA[:, 0:ln],
                KF[po:po + 64, dt, tq0:tq0 + 128],
                QF[po:po + 64, dt, tq0:tq0 + ln],
                start=True, stop=False,
                waits=[exp_last, nm_tok], sig=False)
            sc_tok = kb.mm(
                kb.PA[:, 0:ln], ones_row[:, :], NM[:, tq0:tq0 + ln],
                start=False, stop=True, waits=[], sig=True)
            e_tok = kb.act(EX[:, eb, 0:ln], kb.PA[:, 0:ln], AF.Exp,
                           waits=[sc_tok, ex_last[eb]], sig=True)
            m_tok = kb.tt(EX[:, eb, 0:ln], EX[:, eb, 0:ln], WINM[:, 0:ln],
                          ALU.mult, waits=[e_tok, t_win], sig=True)
            va_t = kb.copy(VA[:, eb, 0:64], VT[:, ki, h * 64:(h + 1) * 64],
                           waits=[vt_tok, va_tok[eb]], sig=True)
            segs = []
            g0 = tq0
            while g0 < tq0 + ln:
                g1 = min(tq0 + ln, (g0 // 512 + 1) * 512)
                segs.append((g0, g1))
                g0 = g1
            for si, (g0, g1) in enumerate(segs):
                bank = g0 // 512
                st_flag = bank not in banks_started
                banks_started.add(bank)
                ctx_mm = kb.mm(kb.PB[0:65, g0:g1],
                               VA[:, eb, 0:65], EX[:, eb, g0 - tq0:g1 - tq0],
                               start=st_flag, stop=(ki == 15),
                               waits=[va_t, m_tok, pb_free],
                               sig=(si == len(segs) - 1))
            ex_last[eb] = ctx_mm
            va_tok[eb] = ctx_mm
            exp_last = e_tok
        d1 = kb.act(DNRB[64:65, :], kb.PB[64:65, :], AF.Identity,
                    waits=[ctx_mm, den_prev], sig=True)
        den_prev = kb.dma(den_dram[h:h + 1], DNRB[64:65, :], waits=[d1],
                          eng="gpsimd")
        if po == 0:
            pb_free = kb.act(WOIN[0:64, dt, :], kb.PB[0:64, :], AF.Identity,
                             waits=[ctx_mm], sig=True)
        else:
            t5 = kb.act(OSH[0:64, 0:2048], kb.PB[0:64, :], AF.Identity,
                        waits=[ctx_mm, ex_last[0], ex_last[1]], sig=True)
            pb_free = kb.dma(WOIN[64:128, dt, :], OSH[0:64, 0:2048],
                             waits=[t5], eng="gpsimd")
            exp_last = pb_free
    ctx_done = pb_free
    tld = kb.dma(DNRB[0:16, :], den_dram, waits=[den_prev], eng="gpsimd")
    trc = kb.recip(DNRB[0:16, :], DNRB[0:16, :], waits=[tld], sig=True)
    trd = kb.dma(rec_dram, DNRB[0:16, :], waits=[trc], eng="gpsimd")
    norm_btok = {0: trd, 1: trd}
    norm_last = None
    for h in range(16):
        dt, po = h // 2, (h % 2) * 64
        bt = kb.dma(DNRB[po:po + 64, :],
                    rec_dram[h:h + 1].squeeze(0).partition_broadcast(64),
                    waits=[trd, norm_btok[h % 2]], eng="gpsimd")
        norm_last = kb.tt(WOIN[po:po + 64, dt, :], WOIN[po:po + 64, dt, :],
                          DNRB[po:po + 64, :], ALU.mult,
                          waits=[bt, ctx_done], sig=True)
        norm_btok[h % 2] = norm_last
    ctx_done = norm_last
    kb.dbg("woin", WOIN, waits=[ctx_done])

    st_tok = [None]

    def evac_s(ot, ps, tok):
        t1 = kb.act(SOUT, ps[:], AF.Identity,
                    bias=ob_sb[:, ot:ot + 1], waits=[tok, st_tok[0], t_nm],
                    sig=True)
        st_tok[0] = kb.dma(sdram[ot], SOUT, waits=[t1])
        return t1

    ProjFM(kb, wo_d, NKT, lambda kt: WOIN[:, kt, :], evac_s,
           rhs_waits=[ctx_done], ps_guard=[exp_last, ctx_done])
    s_done = ("dy", kb.counts["dy"])
    s_evac = kb.tok("scalar")

    IVB1 = Bu8[:, 0:8192].bitcast(F32)
    MAB2 = Bu8[:, 8192:16384].bitcast(F32)
    SCB = Bu8[:, 16384:24576].bitcast(F32)
    AMB = Bu8[:, 24576:32768].bitcast(F32)
    AMXC = kb.sb("l2amxc", [128, 20], F32)
    HH2 = Cc[:, 8192:16448].bitcast(F32)
    YY = Cc[:, 16448:24640].bitcast(F32)
    HQB = Cc[:, 0:32768].bitcast(BF16).rearrange(
        "p (a b) -> p a b", b=512)
    GT = Dd[:, 0:8192].bitcast(F32)
    XT = Dd[:, 8192:16384].bitcast(F32)
    ST2 = Dd[:, 16384:24576].bitcast(F32)
    W2STB = Dd[:, 24576:32768].bitcast(BF16).rearrange(
        "p (a b) -> p a b", b=128)

    kb.dbg("s2", sdram, waits=[("dy", kb.counts["dy"])])
    kb.dbg("scdram", sc_dram, waits=[t_scd])
    kb.dbg("w1q", w1q_dram, waits=[t_q1])

    def token_absmax(src_fn, n_kt, Mdst, guard):
        t1 = kb.memset(Mdst, 0.0, waits=[guard], sig=True)
        for kt in range(n_kt):
            kb.stt(Mdst, src_fn(kt), -1.0, Mdst, ALU.mult, ALU.max,
                   waits=[], hard=[t1] if kt == 0 else [])
            t1 = kb.stt(Mdst, src_fn(kt), 1.0, Mdst, ALU.mult, ALU.max,
                        waits=[], sig=(kt == n_kt - 1))
        return t1

    def amax_rows(Msrc, guard, ps_guard=(), dbgname=None):
        toks = []
        rtok = guard
        for ti in range(16):
            tmm = kb.mm(kb.PA[:, 0:128], Msrc[:, ti * 128:(ti + 1) * 128],
                        IDF[:], start=True, stop=True,
                        waits=[guard, t_idf, rtok] + list(ps_guard),
                        sig=True)
            rtok = kb.red(AMXC[:, ti:ti + 1], kb.PA[:, 0:128], ALU.max,
                          waits=[tmm], sig=True)
            toks.append(rtok)
        tmm = kb.mm(kb.PB[0:16, 0:128], AMXC[:, 0:16], IDF[:],
                    start=True, stop=True,
                    waits=toks + [t_idf, r1_, r2_], sig=True)
        ev = kb.act(H16[0:16, 0:128], kb.PB[0:16, 0:128], AF.Identity,
                    waits=[tmm, r1_, r2_], sig=True)
        if dbgname:
            kb.dbg(dbgname + "_amxc", AMXC[:], waits=toks)
            kb.dbg(dbgname + "_h16", H16[0:16, 0:128], waits=[ev])
        td = kb.dma(am_dram, H16[0:16, 0:128], waits=[ev], eng="gpsimd")
        tb = kb.dma(AMB, am_dram.rearrange("a b -> (a b)")
                    .partition_broadcast(128), waits=[td], eng="gpsimd")
        return tb

    xq_guard = kb.tok("tensor")
    XF = Ff[:, 0:2048]
    MBB = Dd[:, 0:8192].bitcast(F32)
    RBB = Dd[:, 8192:16384].bitcast(F32)
    mrow_d, rrow_d = kb.ln_rows
    t_mbb = kb.dma(MBB, mrow_d.squeeze(0).partition_broadcast(128),
                   waits=[s_evac] + ln_row_toks, eng="gpsimd")
    t_rbb = kb.dma(RBB, rrow_d.squeeze(0).partition_broadcast(128),
                   waits=[t_mbb], eng="gpsimd")
    t_ms = kb.memset(SCB, 0.0, waits=[s_evac], sig=True)
    xf_prev = ctx_done
    t_mx = None
    for kt in range(NKT):
        tokd = kb.dma(XF, xdram[kt], waits=[xf_prev, s_evac])
        t1 = kb.tt(XF, XF, MBB, ALU.subtract, waits=[tokd, t_mbb],
                   hard=[t_ms] if kt == 0 else [], sig=True)
        kb.stt(SCB, XF, -1.0, SCB, ALU.mult, ALU.max, waits=[])
        t_mx = kb.stt(SCB, XF, 1.0, SCB, ALU.mult, ALU.max, waits=[],
                      sig=True)
        xf_prev = t_mx
    kb.dbg("scbraw", SCB, waits=[t_mx])
    t_amb = amax_rows(SCB, t_mx, ps_guard=[e1_, e2_, t_rc], dbgname="am1")
    t_c = kb.ts(AMB, AMB, 1e-30, None, ALU.max, waits=[t_amb], sig=True)
    t_iv = kb.ts(IVB1, AMB, 1.0 / 127, None, ALU.mult, waits=[t_c],
                 sig=True)
    t_iv = kb.ts(IVB1, IVB1, SWB[:, 0:1], None, ALU.mult, waits=[t_swb],
                 sig=True)
    t_iv = kb.tt(IVB1, IVB1, RBB, ALU.mult, waits=[t_rbb], sig=True)
    t_r = kb.recip(SCB, AMB, waits=[t_c], sig=True)
    t_sc1 = kb.ts(SCB, SCB, 127.0, None, ALU.mult, waits=[], hard=[t_r],
                  sig=True)
    xq_toks = []
    xq_prev = xq_guard
    for kt in range(NKT):
        tokd = kb.dma(XF, xdram[kt], waits=[xf_prev])
        t0 = kb.tt(XF, XF, MBB, ALU.subtract, waits=[tokd], sig=True)
        t1 = kb.tt(YY, XF, SCB, ALU.mult, waits=[t_sc1, t_q2, xq_prev],
                   sig=True)
        xf_prev = t1
        t2 = kb.act(YY, YY, AF.Identity, bias=MAGIC[:, 0:1],
                    waits=[t1, t_mg], sig=False)
        t3 = kb.act(XQ1[:, kt, :], YY, AF.Identity, bias=MAGIC[:, 1:2],
                    waits=[], sig=True)
        xq_prev = t3
        xq_toks.append(t3)

    kb.dbg("amb1", AMB, waits=[t_amb])
    kb.dbg("xq1", XQ1, waits=xq_toks)
    kb.dbg("scb", SCB, waits=[t_sc1])
    kb.dbg("ivb1", IVB1, waits=[t_iv])

    MAB_init = kb.memset(MAB2, 0.0, sig=True)
    w1bf = kb.WBF[:, 0]
    p1_prev = {0: t_q1, 1: t_q1}
    mab_tok = MAB_init
    prev_y3 = None
    prev_hst = None
    hh_edge = kb.memset(HH2[:, 0:1], 0.0, sig=True)
    hh_edge = kb.memset(HH2[:, 2049:2056], 0.0, sig=True)
    for fo in range(32):
        b = fo % 2
        wv = w1q_dram[:, fo * 128:(fo + 1) * 128].rearrange(
            "(k p) m -> p k m", p=128)
        tok_d = kb.dma(kb.WBF[:, b, 0], wv, waits=[p1_prev[b], t_q1])
        mm_t = None
        for c in range(4):
            sl = slice(c * 512, (c + 1) * 512)
            for kt in range(NKT):
                mm_t = kb.mm(kb.PA[:, sl] if b == 0 else kb.PB[:, sl],
                             kb.WBF[:, b, 0, kt], XQ1[:, kt, sl],
                             start=(kt == 0), stop=(kt == NKT - 1),
                             waits=[tok_d] + xq_toks,
                             sig=(c == 3 and kt == NKT - 1))
        p1_prev[b] = mm_t
        ps = kb.PA if b == 0 else kb.PB
        e1 = kb.tt(HH2[:, 1:2049], ps[:], IVB1, ALU.mult,
                   waits=[mm_t, t_iv, hh_edge, prev_y3], sig=True)
        y1 = kb.ts(YY, HH2[:, 0:2048], cw0_sb[:, fo:fo + 1], None, ALU.mult,
                   waits=[e1, prev_hst, mab_tok], sig=False)
        y2 = kb.stt(YY, HH2[:, 1:2049], cw1_sb[:, fo:fo + 1], YY,
                    ALU.mult, ALU.add, waits=[e1], sig=False)
        y3 = kb.stt(YY, HH2[:, 2:2050], cw2_sb[:, fo:fo + 1], YY,
                    ALU.mult, ALU.add, waits=[e1], sig=True)
        prev_y3 = y3
        sil = kb.act(YY, YY, AF.Silu, bias=cb_sb[:, fo:fo + 1],
                     waits=[y3], sig=True)
        hst = kb.dma(h_dram[fo], YY, waits=[sil], eng="gpsimd")
        prev_hst = hst
        kb.stt(MAB2, YY, -1.0, MAB2, ALU.mult, ALU.max, waits=[sil],
               hard=[MAB_init] if fo == 0 else [])
        mab_tok = kb.stt(MAB2, YY, 1.0, MAB2, ALU.mult, ALU.max,
                         waits=[], sig=True)
        p1_prev[b] = e1
    p1_done = mab_tok
    h_all = ("dg", kb.counts["dg"])

    kb.dbg("hd", h_dram, waits=[h_all])
    kb.dbg("amdram", am_dram, waits=[("dg", kb.counts["dg"])])
    kb.dbg("mab2", MAB2, waits=[p1_done])

    t_amb2 = amax_rows(MAB2, p1_done)
    t_c2 = kb.ts(AMB, AMB, 1e-5, None, ALU.max, waits=[t_amb2], sig=True)
    IVB2, SCB2 = IVB1, SCB
    t_iv2 = kb.ts(IVB2, AMB, 1.0 / 127, None, ALU.mult,
                  waits=[t_c2, p1_done], sig=True)
    t_iv2 = kb.ts(IVB2, IVB2, SWB[:, 1:2], None, ALU.mult, waits=[t_swb],
                  sig=True)
    t_r2 = kb.recip(SCB2, AMB, waits=[t_c2], sig=True)
    t_sc2 = kb.ts(SCB2, SCB2, 127.0, None, ALU.mult, waits=[], hard=[t_r2],
                  sig=True)

    upd_tok = [None]
    ldg = {0: t_sc2, 1: t_iv2}
    scg = {0: None, 1: None}
    w2_prev = [t_iv2]
    for tc in range(4):
        tsl = slice(tc * 512, (tc + 1) * 512)
        hq_toks = []
        YY2 = AMB
        for ft in range(32):
            sA = 1024 * (ft % 2)
            tok = kb.dma(YY2[:, sA:sA + 512], h_dram[ft][:, tsl],
                         waits=[ldg[ft % 2], h_all])
            t1 = kb.tt(YY2[:, sA + 512:sA + 1024], YY2[:, sA:sA + 512],
                       SCB2[:, tsl], ALU.mult,
                       waits=[tok, t_sc2, t_iv2, scg[ft % 2]] +
                       (w2_prev if ft < 2 else []), sig=True)
            ldg[ft % 2] = t1
            t2 = kb.act(YY2[:, sA + 512:sA + 1024],
                        YY2[:, sA + 512:sA + 1024], AF.Identity,
                        bias=MAGIC[:, 0:1], waits=[t1, t_mg], sig=False)
            t3 = kb.act(HQB[:, ft, :], YY2[:, sA + 512:sA + 1024],
                        AF.Identity, bias=MAGIC[:, 1:2], waits=[], sig=True)
            scg[ft % 2] = t3
            hq_toks.append(t3)
        mm8 = []
        w2tok = [p1_done if tc == 0 else w2tok[0],
                 p1_done if tc == 0 else w2tok[1]]
        for fo2 in range(8):
            b2 = fo2 % 2
            w2buf = [W2A, W2B][b2]
            wv = w2q_dram[:, fo2 * 128:(fo2 + 1) * 128]
            tok_d = kb.dma(w2buf[:],
                           wv.rearrange("(k p) m -> p k m", p=128),
                           waits=[w2tok[b2], t_q2])
            ps = [kb.PA, kb.PB][b2]
            psl = slice((fo2 // 2) * 512, (fo2 // 2) * 512 + 512)
            mm_t = None
            for ft in range(32):
                mm_t = kb.mm(ps[:, psl],
                             w2buf[:, ft, :],
                             HQB[:, ft, :],
                             start=(ft == 0), stop=(ft == 31),
                             waits=[tok_d, hq_toks[ft], upd_tok[0]],
                             sig=(ft == 31))
            w2tok[b2] = mm_t
            mm8.append((fo2, ps, psl, mm_t))
        GTc, XTc, STc = GT[:, 0:512], XT[:, 0:512], ST2[:, 0:512]
        for fo2, ps, psl, mm_t in mm8:
            e1 = kb.tt(GTc, ps[:, psl], IVB2[:, tsl], ALU.mult,
                       waits=[mm_t, t_iv2, upd_tok[0]], sig=True)
            sil = kb.act(GTc, GTc, AF.Silu, waits=[e1], sig=True)
            tx = kb.dma(XTc, xdram[fo2][:, tsl], waits=[upd_tok[0]])
            tss = kb.dma(STc, sdram[fo2][:, tsl], waits=[s_done, upd_tok[0]])
            t2 = kb.tt(GTc, GTc, STc, ALU.mult, waits=[sil, tss], sig=False)
            t2 = kb.tt(XTc, XTc, GTc, ALU.add, waits=[tx], sig=True)
            upd_tok[0] = kb.dma(xodram[fo2][:, tsl], XTc, waits=[t2])
        w2_prev = [w2tok[0], w2tok[1]]


def prep_weights_l2(inputs):
    return dict(
        Wq=inputs["L2_attn_Wq"], bq=inputs["L2_attn_bq"],
        Wk=inputs["L2_attn_Wk"], bk=inputs["L2_attn_bk"],
        Wv=inputs["L2_attn_Wv"], bv=inputs["L2_attn_bv"],
        Wo=inputs["L2_attn_Wo"], bo=inputs["L2_attn_bo"],
        w1=inputs["L2_ffn_w1"], w2=inputs["L2_ffn_w2"],
        conv_w=inputs["L2_ffn_conv_w"], conv_b=inputs["L2_ffn_conv_b"],
    )


def run_l2(x_full, inputs, dbg=False):
    W = prep_weights_l2(inputs)
    kb = KB()
    kb.dbg_on = dbg
    xin = kb.din("Xin", [NKT, 128, 2048], F32)
    xout = kb.dout("Xout", [NKT, 128, 2048], F32)
    emit_layer2(kb, W, xin, xout)
    kb.build()
    in_maps = []
    for k in range(NCORES):
        m = dict(kb.consts)
        m["Xin"] = np.ascontiguousarray(x_full[k].T.reshape(NKT, 128, 2048))
        in_maps.append(m)
    res = run_bass_kernel_spmd(kb.nc, in_maps, core_ids=list(range(NCORES)))
    out = np.stack([
        res.results[k]["Xout"].reshape(1024, 2048).T for k in range(NCORES)])
    _nancheck("L2", out)
    if dbg:
        return out, res.results
    return out



def prep_weights_l3(inputs):
    return dict(
        q_w=np.ascontiguousarray(
            inputs["L3_attn_q_w"].transpose(2, 0, 1).reshape(1024, 1024)),
        k_w=np.ascontiguousarray(
            inputs["L3_attn_k_w"].transpose(2, 0, 1).reshape(1024, 1024)),
        v_w=np.ascontiguousarray(
            inputs["L3_attn_v_w"].transpose(2, 0, 1).reshape(1024, 1024)),
        q_b=inputs["L3_attn_q_b"],
        k_b=inputs["L3_attn_k_b"],
        v_b=inputs["L3_attn_v_b"],
        l_w=np.ascontiguousarray(inputs["L3_attn_l_w"].T),
        l_b=inputs["L3_attn_l_b"],
        weight=inputs["L3_ffn_weight"],
        aux_weight1=inputs["L3_ffn_aux_weight1"],
        aux_weight2=inputs["L3_ffn_aux_weight2"],
        bias=inputs["L3_ffn_bias"],
    )


def build_l3_program(W, dbg=False):
    kb = KB()
    kb.dbg_on = dbg
    xin = kb.din("Xin", [NKT, 128, 2048], F32)
    xout = kb.dout("Xout", [NKT, 128, 2048], F32)
    emit_layer3(kb, W, xin, xout)
    kb.build()
    return kb


def run_l3(x_full, inputs, dbg=False):
    W = prep_weights_l3(inputs)
    kb = build_l3_program(W, dbg)
    in_maps = []
    for k in range(NCORES):
        m = dict(kb.consts)
        m["Xin"] = np.ascontiguousarray(x_full[k].T.reshape(NKT, 128, 2048))
        in_maps.append(m)
    res = run_bass_kernel_spmd(kb.nc, in_maps, core_ids=list(range(NCORES)))
    out = np.stack([
        res.results[k]["Xout"].reshape(1024, 2048).T for k in range(NCORES)])
    _nancheck("L3", out)
    if dbg:
        return out, res.results
    return out


def build_fused(inputs, dbg=False):
    W0 = prep_weights_l0(inputs)
    W1 = prep_weights_l1(inputs)
    W2 = prep_weights_l2(inputs)
    W3 = prep_weights_l3(inputs)
    kb = KB()
    kb.dbg_on = dbg
    xin = kb.din("Xin", [NKT, 128, 2048], F32)
    xpin = kb.din("XPin", [NKT, 128, 2048], F32)
    xout = kb.dout("Xout", [NKT, 128, 2048], F32)
    mk = kb.dout if dbg else kb.dtmp
    x1d = mk("X1d", [NKT, 128, 2048], F32)
    x2d = mk("X2d", [NKT, 128, 2048], F32)
    x3d = mk("X3d", [NKT, 128, 2048], F32)
    cfu_src = kb.dtmp("CFUsrc", [NKT, 128, 256], F32)
    cfu_all = kb.dtmp("CFUall", [NCORES, NKT, 128, 256], F32)
    kb.scope = "A_"
    emit_layer0a(kb, W0, xpin, cfu_src)
    cc_tok = kb.cc_allgather(cfu_all, cfu_src,
                             waits=[("dy", kb.counts.get("dy", 0))])
    kb.cfu_wait = [cc_tok]
    kb.barrier()
    if dbg:
        agd = kb.dout("CFUallDbg", [NCORES, NKT, 128, 256], F32)
        kb.dma(agd, cfu_all, waits=[cc_tok], eng="gpsimd")
    kb.ar_off = 0
    kb.sar_off = 0
    kb.scope = "B_"
    emit_layer0b(kb, W0, xin, cfu_all, x1d)
    kb.barrier()
    kb.ar_off = 0
    kb.sar_off = 0
    kb.scope = "C_"
    emit_layer1(kb, W1, x1d, x2d)
    kb.barrier()
    kb.ar_off = 0
    kb.sar_off = 0
    kb.scope = "D_"
    emit_layer2(kb, W2, x2d, x3d)
    kb.barrier()
    kb.ar_off = 0
    kb.sar_off = 0
    kb.scope = "E_"
    emit_layer3(kb, W3, x3d, xout)
    kb.build()
    return kb


def kernel(**inputs):
    x = np.asarray(inputs["L0_attn_x"], np.float32)
    kb = build_fused(inputs)
    in_maps = []
    for k in range(NCORES):
        m = dict(kb.consts)
        m["Xin"] = np.ascontiguousarray(x[k].T.reshape(NKT, 128, 2048))
        xp = x[:, k * 256:(k + 1) * 256, :]
        m["XPin"] = np.ascontiguousarray(
            xp.transpose(2, 0, 1).reshape(1024, 2048)).reshape(
                NKT, 128, 2048)
        in_maps.append(m)
    res = run_bass_kernel_spmd(kb.nc, in_maps, core_ids=list(range(NCORES)))
    out = np.stack([res.results[k]["Xout"].reshape(1024, 2048).T
                    for k in range(NCORES)])
    _nancheck("FUSED", out)
    return out


def kernel_unfused(**inputs):
    x = np.asarray(inputs["L0_attn_x"], np.float32)
    x = run_l0(x, inputs)
    x = run_l1(x, inputs)
    x = run_l2(x, inputs)
    x = run_l3(x, inputs)
    return x


# version 11
# speedup vs baseline: 1.0120x; 1.0120x over previous
import numpy as np
import ml_dtypes
from contextlib import ExitStack

import concourse.bass as bass
import concourse.mybir as mybir
from concourse.bass_utils import run_bass_kernel_spmd

F32 = mybir.dt.float32
BF16 = mybir.dt.bfloat16
U8 = mybir.dt.uint8
AF = mybir.ActivationFunctionType
ALU = mybir.AluOpType
AX = mybir.AxisListType

BATCH, T, C = 8, 2048, 1024
NKT = C // 128
NCORES = 8
BF = ml_dtypes.bfloat16
ESZ = {F32: 4, BF16: 2, U8: 1}


class KB:

    ENGS = ("sync", "vector", "scalar", "tensor", "gpsimd")

    def __init__(self, arena_bytes=212736):
        self.nc = bass.Bass("TRN2", target_bir_lowering=False, num_devices=8)
        self.scope = ""
        self.es = ExitStack()
        self.thunks = {e: [] for e in self.ENGS}
        self.counts = {}
        self.handles = {}
        self.last_wait = {}
        self.eng_sem = {"vector": "sv", "scalar": "sc", "tensor": "sn",
                        "gpsimd": "sg", "sync": "sy"}
        self.dma_sem = {"sync": "dy", "gpsimd": "dg", "scalar": "dc"}
        self.consts = {}
        self.out_names = []
        self.AR = self._sb_real("ARENA", [128, arena_bytes], U8)
        self.ar_off = 0
        self.ar_stack = []
        self.sar_off = 0
        self.PA = self.ps("PA", [128, 2048], F32)
        self.PB = self.ps("PB", [128, 2048], F32)
        self.proj_guard = None
        self._ccols = {}
        self._ccol_t = self._sb_real("CCOLS", [128, 16], F32)
        self.tail = {}
        self.DUM = self._sb_real("DUMMY", [2, 8], F32)

    def din(self, name, shape, dtype=F32):
        return self.nc.dram_tensor(self.scope + name, list(shape), dtype,
                                   kind="ExternalInput").ap()

    def dout(self, name, shape, dtype=F32):
        self.out_names.append(self.scope + name)
        return self.nc.dram_tensor(self.scope + name, list(shape), dtype,
                                   kind="ExternalOutput").ap()

    def dtmp(self, name, shape, dtype=F32):
        return self.nc.dram_tensor(self.scope + name, list(shape), dtype,
                                   kind="Internal").ap()

    def cin(self, name, arr):
        arr = np.ascontiguousarray(arr)
        name = self.scope + name
        self.consts[name] = arr
        dt = {np.dtype(np.float32): F32, np.dtype(BF): BF16}[arr.dtype]
        return self.nc.dram_tensor(name, list(arr.shape), dt,
                                   kind="ExternalInput").ap()

    def _sb_real(self, name, shape, dtype=F32):
        return self.es.enter_context(
            self.nc.sbuf_tensor(self.scope + name, list(shape), dtype))

    def sb(self, name, shape, dtype=F32):
        p = shape[0]
        n = int(np.prod(shape[1:])) * ESZ[dtype]
        n_al = (n + 63) & ~63
        self.sar_off += n_al
        off = self.AR.shape[1] - self.sar_off
        assert self.ar_off <= off, (
            f"arena collision: abuf at {self.ar_off}, named {name} at {off}")
        v = self.AR[0:p, off:off + n].bitcast(dtype)
        if len(shape) > 2:
            pat = " ".join(f"a{i}" for i in range(1, len(shape)))
            kw = {f"a{i}": shape[i] for i in range(2, len(shape))}
            v = v.rearrange(f"p ({pat}) -> p {pat}", **kw)
        return v

    def ps(self, name, shape, dtype=F32):
        return self.es.enter_context(
            self.nc.psum_tensor(name, list(shape), dtype))

    def abuf(self, shape, dtype):
        n = int(np.prod(shape[1:])) * ESZ[dtype]
        n_al = (n + 63) & ~63
        off = self.ar_off
        assert off + n_al <= self.AR.shape[1] - self.sar_off, (
            f"arena overflow: need {n_al} at {off} (named top at "
            f"{self.AR.shape[1] - self.sar_off})")
        self.ar_off = off + n_al
        v = self.AR[:, off:off + n].bitcast(dtype)
        if len(shape) > 2:
            pat = " ".join(f"a{i}" for i in range(1, len(shape)))
            kw = {f"a{i}": shape[i] for i in range(2, len(shape))}
            v = v.rearrange(f"p ({pat}) -> p {pat}", **kw)
        return v

    def apush(self):
        self.ar_stack.append(self.ar_off)

    def apop(self):
        self.ar_off = self.ar_stack.pop()

    def _wait(self, eng, toks, hard=()):
        own = self.eng_sem[eng]
        for t in list(toks) + list(hard):
            if t is None:
                continue
            sem, val = t
            if sem == own and t not in hard:
                continue
            cur = self.last_wait.get((eng, sem), 0)
            if val > cur:
                self.last_wait[(eng, sem)] = val
                self.thunks[eng].append(
                    lambda e, s=sem, v=val: e.wait_ge(self.handles[s], v))

    def emit(self, eng, fn, waits=(), sig=False, hard=()):
        self._wait(eng, waits, hard)
        self.tail[eng] = not sig
        if sig:
            sem = self.eng_sem[eng]
            self.counts[sem] = self.counts.get(sem, 0) + 1
            val = self.counts[sem]
            def thunk(e, fn=fn, s=sem):
                fn(e).then_inc(self.handles[s], 1)
            self.thunks[eng].append(thunk)
            return (sem, val)
        self.thunks[eng].append(lambda e, fn=fn: fn(e))
        return None

    def tok(self, eng):
        sem = self.eng_sem[eng]
        return (sem, self.counts.get(sem, 0))

    def dma(self, out, in_, waits=(), eng="sync"):
        self._wait(eng, waits)
        sem = self.dma_sem[eng]
        self.counts[sem] = self.counts.get(sem, 0) + 16
        val = self.counts[sem]
        def thunk(e, out=out, in_=in_, s=sem):
            e.dma_start(out=out, in_=in_).then_inc(self.handles[s], 16)
        self.thunks[eng].append(thunk)
        return (sem, val)

    def mm(self, out, lhsT, rhs, start, stop, waits=(), sig=False):
        return self.emit(
            "tensor",
            lambda e, o=out, l=lhsT, r=rhs, s1=start, s2=stop: e.matmul(
                o, l, r, start=s1, stop=s2),
            waits, sig)

    def ccol(self, val):
        val = float(val)
        if val not in self._ccols:
            i = len(self._ccols)
            assert i < 16
            self.memset(self._ccol_t[:, i:i + 1], val, sig=True)
            self._ccols[val] = (i, self.tok("vector"))
        return self._ccols[val]

    def act(self, out, in_, func, bias=None, scale=None, accum=None,
            waits=(), sig=False, hard=()):
        kw = {}
        if isinstance(bias, float):
            i, tok = self.ccol(bias)
            p = in_.shape[0]
            bias = self._ccol_t[0:p, i:i + 1]
            waits = list(waits) + [tok]
        if bias is not None:
            kw["bias"] = bias
        if scale is not None:
            kw["scale"] = scale
        if accum is not None:
            kw["accum_out"] = accum
        return self.emit(
            "scalar",
            lambda e, o=out, i=in_, f=func, kw=kw: e.activation(o, i, f, **kw),
            waits, sig, hard)

    def tt(self, out, a, b, op, waits=(), sig=False, eng="vector", hard=()):
        return self.emit(
            eng, lambda e, o=out, x=a, y=b, p=op: e.tensor_tensor(o, x, y, p),
            waits, sig, hard)

    def ts(self, out, in0, s1, s2, op0, op1=None, accum=None, waits=(),
           sig=False, eng="vector", hard=()):
        def fn(e, o=out, i=in0, a=s1, b=s2, p0=op0, p1=op1, ac=accum):
            kw = {}
            if ac is not None:
                kw["accum_out"] = ac
            if p1 is None:
                return e.tensor_scalar(o, i, a, None, p0, **kw)
            return e.tensor_scalar(o, i, a, b, p0, p1, **kw)
        return self.emit(eng, fn, waits, sig, hard)

    def stt(self, out, in0, scal, in1, op0, op1, waits=(), sig=False,
            eng="vector", hard=()):
        return self.emit(
            eng,
            lambda e, o=out, i0=in0, s=scal, i1=in1, p0=op0, p1=op1:
                e.scalar_tensor_tensor(o, i0, s, i1, p0, p1),
            waits, sig, hard)

    def red(self, out, in_, op, absval=None, negate=None, waits=(),
            sig=False, hard=()):
        return self.emit(
            "vector",
            lambda e, o=out, i=in_, p=op, a=absval, n=negate: e.tensor_reduce(
                o, i, AX.X, p, apply_absolute_value=a, negate=n),
            waits, sig, hard)

    def recip(self, out, in_, waits=(), sig=False):
        return self.emit(
            "vector", lambda e, o=out, i=in_: e.reciprocal(o, i), waits, sig)

    def copy(self, out, in_, waits=(), sig=False, eng="vector"):
        return self.emit(
            eng, lambda e, o=out, i=in_: e.tensor_copy(o, i), waits, sig)

    def memset(self, ap, val, waits=(), sig=False, eng="vector"):
        return self.emit(
            eng, lambda e, a=ap, v=val: e.memset(a, v), waits, sig)

    def cc_op(self, kind, op, out, in_, waits=()):
        self._wait("gpsimd", waits)
        sem = self.eng_sem["gpsimd"]
        self.counts[sem] = self.counts.get(sem, 0) + 1
        val = self.counts[sem]
        def thunk(e, o=out, i=in_, k=kind, p=op, s=sem):
            e.collective_compute(
                k, p, [list(range(NCORES))], [i], [o]).then_inc(
                    self.handles[s], 1)
        self.thunks["gpsimd"].append(thunk)
        return (sem, val)

    def cc_allgather(self, out, in_, waits=()):
        ag = self.cc_op("AllGather", ALU.bypass, out, in_, waits)
        if not hasattr(self, "_ccbar"):
            bs = self.dtmp("CCBARsrc", [1, 1], F32)
            bd = self.dtmp("CCBARdst", [1, 1], F32)
            t0 = self.memset(self.DUM[0:1, 4:5], 0.0, sig=True)
            t1 = self.dma(bs, self.DUM[0:1, 4:5], waits=[t0], eng="gpsimd")
            self._ccbar = (bs, bd, t1)
        bs, bd, t1 = self._ccbar
        return self.cc_op("AllReduce", ALU.add, bd, bs, waits=[t1, ag])

    def barrier(self):
        if self.tail.get("vector"):
            self.memset(self.DUM[0:1, 0:1], 0.0, sig=True)
        if self.tail.get("gpsimd"):
            self.memset(self.DUM[0:1, 1:2], 0.0, sig=True, eng="gpsimd")
        if self.tail.get("scalar"):
            self.act(self.DUM[0:1, 2:3], self.DUM[0:1, 3:4], AF.Identity,
                     sig=True)
        if self.tail.get("tensor"):
            self.mm(self.PA[0:1, 0:1], self.DUM[0:1, 0:1],
                    self.DUM[0:1, 0:1], start=True, stop=True, sig=True)
        snap = dict(self.counts)
        for eng in self.ENGS:
            for sem, val in snap.items():
                if val <= 0:
                    continue
                cur = self.last_wait.get((eng, sem), 0)
                if val > cur:
                    self.last_wait[(eng, sem)] = val
                    self.thunks[eng].append(
                        lambda e, s=sem, v=val: e.wait_ge(self.handles[s], v))

    def dbg(self, name, ap, waits=()):
        if not getattr(self, "dbg_on", False):
            return
        shape = list(ap.shape)
        dt = ap.dtype
        d = self.dout("dbg_" + name, shape, dt)
        self.dma(d, ap, waits=waits, eng="gpsimd")

    def build(self):
        for s in set(list(self.eng_sem.values()) + list(self.dma_sem.values())):
            self.handles[s] = self.es.enter_context(self.nc.semaphore(s))
        for sem in self.dma_sem.values():
            if self.counts.get(sem, 0):
                self.thunks["sync"].append(
                    lambda e, s=sem, v=self.counts[sem]: e.wait_ge(
                        self.handles[s], v))
        block = self.es.enter_context(self.nc.Block())
        for eng in self.ENGS:
            lst = self.thunks[eng]
            dec = getattr(block, eng)
            def body(e, lst=lst):
                for t in lst:
                    t(e)
            dec(body)
        self.es.close()
        return self.nc



def stage_cols(kb, name, vec, scale=1.0, n=None):
    arr = (np.asarray(vec, np.float64) * scale).astype(np.float32)
    if n is None:
        n = arr.size // 128
    d = kb.cin(name, arr.reshape(n, 128).T.copy())
    s = kb.sb(name + "_sb", [128, n], F32)
    tok = kb.dma(s[:], d)
    return s, tok


_BCN = [0]


def _nancheck(tag, out):
    nf = ~np.isfinite(out)
    if nf.any():
        for k in range(out.shape[0]):
            if nf[k].any():
                ft = np.where(nf[k].any(0))[0]
                tk = np.where(nf[k].any(1))[0]
                print(f"[{tag}-NANDBG] core {k}: n={nf[k].sum()} "
                      f"ftiles={sorted(set(ft // 128))} nt={len(tk)} "
                      f"t[{tk[0]}..{tk[-1]}]", flush=True)

def bcast_row(kb, dst128, src_row, waits):
    _BCN[0] += 1
    n = src_row.shape[-1]
    tmp = kb.dtmp(f"bc_tmp{_BCN[0]}", [1, n], F32)
    t1 = kb.dma(tmp, src_row, waits=waits, eng="gpsimd")
    return kb.dma(dst128, tmp.squeeze(0).partition_broadcast(128),
                  waits=[t1], eng="gpsimd")


class ProjFM:

    def __init__(self, kb, wdram, n_ot, rhs_fn, evac, T=T,
                 scale_cols=None, scale2_cols=None, rhs2_fn=None,
                 rhs_waits=(), n_kt=NKT, ps_guard=(), ot_off=0, m_rows=128,
                 k_rows=128):
        self.kb = kb
        nc2 = 2 if rhs2_fn is not None else 1
        wst, wbf = kb.WST, kb.WBF
        cast_tok = [kb.proj_guard, kb.proj_guard]
        evac_tok = [None, None]
        mm_prev = [kb.proj_guard, kb.proj_guard]
        guard = list(ps_guard)
        for ot in range(n_ot):
            b = ot % 2
            wot = ot + ot_off
            wv = wdram[:, wot * m_rows:(wot + 1) * m_rows].rearrange(
                "(k p) m -> p k m", p=k_rows)
            tok_d = kb.dma(wst[0:k_rows, b, 0:n_kt, 0:m_rows], wv,
                           waits=[cast_tok[b]])
            ct = None
            for kt in range(n_kt):
                last = kt == n_kt - 1
                if scale_cols is None:
                    ct = kb.copy(wbf[0:k_rows, b, 0, kt, 0:m_rows],
                                 wst[0:k_rows, b, kt, 0:m_rows],
                                 waits=[tok_d, mm_prev[b]],
                                 sig=(last and nc2 == 1))
                else:
                    ct = kb.ts(wbf[0:k_rows, b, 0, kt, 0:m_rows],
                               wst[0:k_rows, b, kt, 0:m_rows],
                               scale_cols[0:k_rows, kt:kt + 1], None,
                               ALU.mult, waits=[tok_d, mm_prev[b]],
                               sig=(last and nc2 == 1))
                if nc2 == 2:
                    ct = kb.ts(wbf[0:k_rows, b, 1, kt, 0:m_rows],
                               wst[0:k_rows, b, kt, 0:m_rows],
                               scale2_cols[0:k_rows, kt:kt + 1], None,
                               ALU.mult, waits=[tok_d, mm_prev[b]], sig=last)
            cast_tok[b] = ct
            ps = [kb.PA, kb.PB][b]
            tok_mm = None
            bounds = list(range(0, T, 512))
            nchunk = len(bounds)
            for c, c0 in enumerate(bounds):
                sl = slice(c0, min(T, c0 + 512))
                n_in_group = n_kt * nc2
                idx = 0
                for kt in range(n_kt):
                    for v2 in range(nc2):
                        rhs = (rhs_fn(kt) if v2 == 0 else rhs2_fn(kt))
                        last = (idx == n_in_group - 1)
                        tok_mm = kb.mm(
                            ps[0:m_rows, sl],
                            wbf[0:k_rows, b, v2, kt, 0:m_rows],
                            rhs[:, sl],
                            start=(idx == 0), stop=last,
                            waits=[ct, evac_tok[b]] + list(rhs_waits) + guard,
                            sig=(last and c == nchunk - 1))
                        idx += 1
            mm_prev[b] = tok_mm
            evac_tok[b] = evac(ot, ps, tok_mm)
        kb.proj_guard = kb.tok("tensor")


def emit_ln(kb, xdram, xn, xbf, sq, xst, hrow, rb, mb, ones_bf, hrow2,
            save_rows=False):
    mz = kb.memset(xn[:, :, 0:16], 0.0, sig=True)
    prev = None
    ct = st_ = None
    for kt in range(NKT):
        tokd = kb.dma(xst, xdram[kt], waits=[prev])
        ct = kb.copy(xbf[:, kt], xst, waits=[tokd], sig=True)
        prev = ct
        st_ = kb.act(sq[:, kt], xbf[:, kt], AF.Square, waits=[ct],
                     sig=(kt == NKT - 1))
    xb_tok, sq_tok = ct, st_
    t_s = t_q = None
    for c in range(4):
        sl = slice(c * 512, (c + 1) * 512)
        for kt in range(NKT):
            t_s = kb.mm(kb.PA[0:1, sl], ones_bf, xbf[:, kt, sl],
                        start=(kt == 0), stop=(kt == NKT - 1),
                        waits=[xb_tok], sig=(kt == NKT - 1 and c == 3))
        for kt in range(NKT):
            t_q = kb.mm(kb.PB[0:1, sl], ones_bf, sq[:, kt, sl],
                        start=(kt == 0), stop=(kt == NKT - 1),
                        waits=[sq_tok], sig=(kt == NKT - 1 and c == 3))
    e1 = kb.act(hrow[0:1, :], kb.PA[0:1, :], AF.Identity, waits=[t_s],
                sig=True)
    e2 = kb.act(rb[0:1, :], kb.PB[0:1, :], AF.Identity, waits=[t_q],
                sig=True)
    tokm = bcast_row(kb, mb, hrow[0:1, :], [e1])
    tokr = bcast_row(kb, rb, rb[0:1, :], [e2])
    t1 = kb.ts(mb, mb, 1.0 / C, None, ALU.mult, waits=[tokm], sig=True)
    t3 = kb.act(hrow2, mb, AF.Square, waits=[t1], sig=True)
    t4 = kb.stt(rb, rb, 1.0 / C, hrow2, ALU.mult, ALU.subtract,
                waits=[tokr, t3], sig=True)
    t5 = kb.act(rb, rb, AF.Sqrt, bias=1e-5, waits=[t4], sig=True)
    tokr = kb.recip(rb, rb, waits=[t5], sig=True)
    tokm = t1
    if save_rows:
        _BCN[0] += 1
        mrow_d = kb.dtmp(f"ln_mrow{_BCN[0]}", [1, T], F32)
        rrow_d = kb.dtmp(f"ln_rrow{_BCN[0]}", [1, T], F32)
        msave = kb.dma(mrow_d, mb[0:1, :], waits=[tokm], eng="gpsimd")
        rsave = kb.dma(rrow_d, rb[0:1, :], waits=[tokr], eng="gpsimd")
        kb.ln_rows = (mrow_d, rrow_d)
        kb.ln_row_toks = [msave, rsave]
    out_toks = []
    prev = ct
    for kt in range(NKT):
        tokd = kb.dma(xst, xdram[kt], waits=[prev])
        kb.tt(xst, xst, mb, ALU.subtract, waits=[tokd, tokm, mz])
        tq = kb.tt(xn[:, kt, 16:2064], xst, rb, ALU.mult,
                   waits=[tokr], sig=True)
        prev = tq
        out_toks.append(tq)
    return out_toks



def emit_layer3(kb, W, xdram, xodram):
    ones_bf = kb.sb("ones_bf", [128, 4], BF16)
    ones_row = kb.sb("ones_row", [1, 128], BF16)
    onescol_bf = kb.sb("onec_bf", [128, 1], BF16)
    kb.memset(ones_bf[:], 1.0, sig=True)
    t_or = kb.memset(ones_row[:], 1.0, sig=True)
    t_oc = kb.memset(onescol_bf[:], 1.0, sig=True)
    bias_sb = kb.sb("l3bias", [128, 256], F32)
    DD = kb.sb("DD", [128, 8], F32)
    mq = kb.sb("l3mq", [16, 4], F32)
    bound_bf = kb.sb("l3bound_bf", [16, 2048], BF16)
    qb_sb, _ = stage_cols(kb, "l3qb", W["q_b"].reshape(-1))
    kb_sb, _ = stage_cols(kb, "l3kb", W["k_b"].reshape(-1))
    lb_sb, _ = stage_cols(kb, "l3lb", W["l_b"].reshape(-1))
    fb_sb, _ = stage_cols(kb, "l3fb", W["bias"])
    sel_np = np.zeros((C, 16), np.float32)
    for d in range(C):
        sel_np[d, d // 64] = 1.0
    sel_d = kb.cin("l3sel16", sel_np.astype(BF))
    id16_d = kb.cin("id16", np.eye(16).astype(BF))
    id16 = kb.sb("id16_sb", [16, 16], BF16)
    t_id16 = kb.dma(id16[:], id16_d)
    sel_sb = kb.sb("sel16", [128, NKT, 16], BF16)
    tok_sel = kb.dma(sel_sb[:], sel_d.rearrange("(k p) m -> p k m", p=128))
    vb_row_d = kb.cin("l3vb_row", W["v_b"].reshape(1, 1024))
    vb_row = kb.sb("l3vb_sb", [1, 1024], F32)
    vb_bf = kb.sb("l3vb_bf", [1, 1024], BF16)
    tvb = kb.dma(vb_row[:], vb_row_d)
    tvb = kb.copy(vb_bf[:], vb_row[:], waits=[tvb], sig=True)

    qw_d = kb.cin("l3qw", W["q_w"])
    kw_d = kb.cin("l3kw", W["k_w"])
    vw_d = kb.cin("l3vw", W["v_w"])
    lw_d = kb.cin("l3lw", W["l_w"])
    w1_d = kb.cin("l3w1", W["weight"])
    w2_d = kb.cin("l3w2", W["aux_weight1"])
    w3_d = kb.cin("l3w3", W["aux_weight2"])

    sdram = kb.dtmp("s_spill", [NKT, 128, 2048], F32)

    XN = kb.abuf([128, NKT, 2064], BF16)
    H16 = kb.abuf([128, 2048], F32)
    kb.WST = kb.abuf([128, 2, NKT, 128], F32)
    kb.WBF = kb.abuf([128, 2, 2, NKT, 128], BF16)
    A = kb.abuf([128, NKT, 2048], BF16)
    Bb = kb.abuf([128, NKT, 2048], BF16)
    Cc = kb.abuf([128, NKT * 4096], U8)
    Dd = kb.abuf([128, NKT * 4096], U8)
    Ff = kb.abuf([128, 2048], F32)

    QF, KF = A, Bb
    VT = Cc[:, 0:32768].bitcast(BF16).rearrange("p (a b) -> p a b", b=1024)
    GT = Cc[:, 0:8192].bitcast(F32)
    XT = Cc[:, 8192:16384].bitcast(F32)
    ST = Cc[:, 16384:24576].bitcast(F32)
    VPT = kb.sb("VPT_sb", [128, 2, 64], BF16)
    MB = Dd[:, 16384:24576].bitcast(F32)
    RB = Dd[:, 24576:32768].bitcast(F32)
    VWBF = Dd[:, 0:16384].bitcast(BF16).rearrange("p (a b) -> p a b", b=1024)
    VWST = Dd[:, 16384:20480].bitcast(F32)
    SQ2 = Dd[:, 0:32768].bitcast(BF16).rearrange("p (a b) -> p a b", b=2048)
    NQ = Dd[:, 24576:32768].bitcast(F32)
    WOIN = Dd[:, 0:32768].bitcast(BF16).rearrange("p (a b) -> p a b", b=2048)
    XST = Ff[:, 0:2048]
    EX = Ff[:, 0:2048].bitcast(BF16).rearrange("p (a b) -> p a b", b=2048)
    SOUT = Ff[:, 0:2048]

    xn_toks = emit_ln(kb, xdram, XN, A, Bb, XST, H16[0:16, :],
                      RB, MB, ones_bf[:, 0:1], H16)

    kb.dbg("xn", XN, waits=xn_toks)

    def xn_t(kt):
        return XN[:, kt, 16:2064]

    kb.proj_guard = kb.tok("tensor")

    def evac_q(ot, ps, tok):
        return kb.act(QF[:, ot, :], ps[:], AF.Identity,
                      bias=qb_sb[:, ot:ot + 1], waits=[tok], sig=True)

    def evac_k(ot, ps, tok):
        return kb.act(KF[:, ot, :], ps[:], AF.Identity,
                      bias=kb_sb[:, ot:ot + 1], waits=[tok], sig=True)

    ProjFM(kb, qw_d, NKT, xn_t, evac_q, rhs_waits=xn_toks)
    q_done = kb.tok("scalar")
    ProjFM(kb, kw_d, NKT, xn_t, evac_k, rhs_waits=xn_toks, ps_guard=[q_done])
    k_done = kb.tok("scalar")

    vw_toks = []
    prev_cast = kb.proj_guard
    for kt in range(NKT):
        tok = kb.dma(VWST, vw_d[kt * 128:(kt + 1) * 128, :],
                     waits=[prev_cast])
        prev_cast = kb.copy(VWBF[:, kt], VWST, waits=[tok], sig=True)
        vw_toks.append(prev_cast)
    ev_tok = [k_done, k_done]
    for tt in range(16):
        b = tt % 2
        ps = [kb.PA, kb.PB][b]
        tok_mm = None
        for c in range(2):
            sl = slice(c * 512, (c + 1) * 512)
            for kt in range(NKT):
                tok_mm = kb.mm(ps[:, sl], xn_t(kt)[:, tt * 128:(tt + 1) * 128],
                               VWBF[:, kt, sl], start=(kt == 0), stop=False,
                               waits=[vw_toks[kt], ev_tok[b], t_or, tvb,
                                      k_done])
            tok_mm = kb.mm(ps[:, sl], ones_row[:, :], vb_bf[:, sl],
                           start=False, stop=True, waits=[], sig=(c == 1))
        ev_tok[b] = kb.act(VT[:, tt, :], ps[:, 0:1024], AF.Identity,
                           waits=[tok_mm], sig=True)
    vt_tok = kb.tok("scalar")
    kb.dbg("qf", QF, waits=[q_done])
    kb.dbg("kf", KF, waits=[k_done])
    kb.dbg("vt", VT, waits=[vt_tok])

    mm_tok = {}
    for idx, src_t in enumerate((KF, QF)):
        stok = None
        for kt in range(NKT):
            stok = kb.act(SQ2[:, kt], src_t[:, kt, :], AF.Square,
                          waits=[vt_tok, mm_tok.get(0)],
                          sig=(kt == NKT - 1))
        ps = [kb.PA, kb.PB][idx]
        for c in range(4):
            sl = slice(c * 512, (c + 1) * 512)
            for kt in range(NKT):
                mm_tok[idx] = kb.mm(
                    ps[0:16, sl], sel_sb[:, kt, :], SQ2[:, kt, sl],
                    start=(kt == 0), stop=(kt == NKT - 1),
                    waits=[stok, tok_sel, vt_tok],
                    sig=(kt == NKT - 1 and c == 3))
    t_nk = kb.act(H16[0:16, :], kb.PA[0:16, :], AF.Sqrt,
                  waits=[mm_tok[0]], sig=True)
    t_nq = kb.act(NQ[0:16, :], kb.PB[0:16, :], AF.Sqrt,
                  waits=[mm_tok[1]], sig=True)
    t_mq = kb.red(mq[:, 0:1], NQ[0:16, :], ALU.max, waits=[t_nq],
                  sig=True)
    kb.dbg("nk", H16[0:16, :], waits=[t_nk])
    kb.dbg("nq", NQ[0:16, :], waits=[t_nq])
    kb.dbg("mq", mq[:], waits=[t_mq])
    t_mqs = kb.ts(mq[:, 1:2], mq[:, 0:1], -0.0625, None, ALU.mult,
                  waits=[], hard=[t_mq], sig=True)
    t_b = kb.ts(NQ[0:16, :], H16[0:16, :], mq[:, 1:2], 20.0, ALU.mult,
                ALU.add, waits=[t_nk], hard=[t_mqs], sig=True)
    t_bb = kb.copy(bound_bf[:], NQ[0:16, :], waits=[t_b], sig=True)

    bias_prev = t_nk
    for ki in range(16):
        tok = kb.mm(kb.PA[:, 0:16],
                    bound_bf[0:16, ki * 128:(ki + 1) * 128], id16[:],
                    start=True, stop=True,
                    waits=[t_bb, t_id16, bias_prev], sig=True)
        bias_prev = kb.act(bias_sb[:, ki * 16:ki * 16 + 16],
                           kb.PA[:, 0:16], AF.Identity, waits=[tok], sig=True)
    bias_done = bias_prev

    kb.dbg("bias", bias_sb[:], waits=[bias_done])
    kb.dbg("boundbf", bound_bf[:], waits=[t_bb])

    ctx_evac = t_nq
    exp_last = bias_done
    recip_last = None
    ctx_last = {0: None, 1: None}
    for h in range(16):
        dt, po = h // 2, (h % 2) * 64
        ctx_mm = None
        for ki in range(16):
            eb = ki % 2
            sc_tok = None
            for c in range(4):
                sl = slice(c * 512, (c + 1) * 512)
                sc_tok = kb.mm(
                    kb.PA[:, sl],
                    KF[po:po + 64, dt, ki * 128:(ki + 1) * 128],
                    QF[po:po + 64, dt, sl],
                    start=True, stop=True,
                    waits=[exp_last, bias_done], sig=(c == 3))
            e1_tok = kb.act(EX[:, eb, :], kb.PA[:], AF.Exp,
                            bias=bias_sb[:, ki * 16 + h:ki * 16 + h + 1],
                            scale=0.0625,
                            waits=[sc_tok, ctx_last[eb], recip_last],
                            hard=[bias_done], sig=True)
            exp_last = e1_tok
            e_tok = kb.act(EX[:, eb, :], EX[:, eb, :], AF.Square,
                           accum=DD[:, 0:1], waits=[], sig=True)
            d_tok = kb.act(DD[:, 2:3], DD[:, 0:1], AF.Identity,
                           waits=[], sig=True)
            recip_last = kb.recip(DD[:, 1:2], DD[:, 2:3], waits=[d_tok],
                                  sig=True)
            vp_tok = kb.ts(VPT[:, eb, :], VT[:, ki, h * 64:(h + 1) * 64],
                           DD[:, 1:2], None, ALU.mult,
                           waits=[ctx_last[eb]], hard=[recip_last], sig=True)
            for c in range(4):
                sl = slice(c * 512, (c + 1) * 512)
                ctx_mm = kb.mm(kb.PB[po:po + 64, sl], VPT[:, eb, :],
                               EX[:, eb, sl],
                               start=(ki == 0), stop=(ki == 15),
                               waits=[vp_tok, e_tok, ctx_evac],
                               sig=(c == 3))
            ctx_last[eb] = ctx_mm
        ctx_evac = kb.act(WOIN[po:po + 64, dt, :], kb.PB[po:po + 64, :],
                          AF.Identity, waits=[ctx_mm], sig=True)
    ctx_done = ctx_evac

    st_tok = [None]

    def evac_s(ot, ps, tok):
        t1 = kb.act(SOUT, ps[:], AF.Identity,
                    bias=lb_sb[:, ot:ot + 1], waits=[tok, st_tok[0]],
                    sig=True)
        st_tok[0] = kb.dma(sdram[ot], SOUT, waits=[t1])
        return t1

    kb.dbg("woin", WOIN, waits=[ctx_done])
    ProjFM(kb, lw_d, NKT, lambda kt: WOIN[:, kt, :], evac_s,
           rhs_waits=[ctx_done], ps_guard=[exp_last, ctx_done])
    s_done = ("dy", kb.counts["dy"])
    s_evac_done = kb.tok("scalar")

    def evac_f1(ot, ps, tok):
        return kb.act(QF[:, ot, :], ps[:], AF.Identity, waits=[tok], sig=True)

    ProjFM(kb, w1_d, NKT, xn_t, evac_f1, rhs_waits=xn_toks + [ctx_done],
           ps_guard=[s_evac_done])
    f1_tok = kb.tok("scalar")

    def evac_f2(ot, ps, tok):
        return kb.act(KF[:, ot, :], ps[:], AF.Identity, waits=[tok], sig=True)

    ProjFM(kb, w2_d, NKT, lambda kt: QF[:, kt, :], evac_f2,
           rhs_waits=[f1_tok], ps_guard=[f1_tok])
    f2_tok = kb.tok("scalar")

    upd_tok = [None]

    def evac_f3(ot, ps, tok):
        tx = kb.dma(XT, xdram[ot], waits=[upd_tok[0]])
        tss = kb.dma(ST, sdram[ot], waits=[s_done, upd_tok[0]])
        t1 = kb.act(GT, ps[:], AF.Tanh, bias=fb_sb[:, ot:ot + 1],
                    waits=[tok, upd_tok[0], vt_tok], sig=False)
        t1 = kb.act(GT, GT, AF.Silu, waits=[tok], sig=True)
        t2 = kb.tt(GT, GT, ST, ALU.mult, waits=[t1, tss], sig=False)
        t2 = kb.tt(XT, XT, GT, ALU.add, waits=[tx], sig=True)
        upd_tok[0] = kb.dma(xodram[ot], XT, waits=[t2])
        return t2

    kb.dbg("s", sdram, waits=[s_done])
    kb.dbg("f2", KF, waits=[f2_tok])
    ProjFM(kb, w3_d, NKT, lambda kt: KF[:, kt, :], evac_f3,
           rhs_waits=[f2_tok], ps_guard=[f2_tok])



def emit_layer1(kb, W, xdram, xodram):
    ones_bf = kb.sb("ones_bf", [128, 4], BF16)
    ones_row = kb.sb("ones_row", [1, 128], BF16)
    kb.memset(ones_bf[:], 1.0, sig=True)
    t_or = kb.memset(ones_row[:], 1.0, sig=True)
    mq = kb.sb("l1mk", [16, 4], F32)
    VA = kb.sb("VA_sb", [128, 2, 72], BF16)
    t_va = kb.memset(VA[:], 0.0, sig=True)
    t_va = kb.memset(VA[:, 0, 64:65], 1.0, sig=True)
    t_va = kb.memset(VA[:, 1, 64:65], 1.0, sig=True)
    NM = kb.sb("NM_sb", [1, 2048], BF16)
    TRI = kb.sb("TRI_sb", [128, 128], BF16)
    tri_np = (np.arange(128)[:, None] <= np.arange(128)[None, :])
    tri_d = kb.cin("l1tri", tri_np.astype(BF))
    t_tri = kb.dma(TRI[:], tri_d)
    sel_np = np.zeros((C, 16), np.float32)
    for d in range(C):
        sel_np[d, d // 64] = 1.0
    sel_d = kb.cin("l1sel16", sel_np.astype(BF))
    sel_sb = kb.sb("sel16", [128, NKT, 16], BF16)
    tok_sel = kb.dma(sel_sb[:], sel_d.rearrange("(k p) m -> p k m", p=128))

    qkvb = W["b_qkv"]
    qb_sb, tqb = stage_cols(kb, "l1qb", qkvb[0:1024], scale=0.125)
    kb_sb, tkb = stage_cols(kb, "l1kb", qkvb[1024:2048])
    ob_sb, tob = stage_cols(kb, "l1ob", W["b_out"])
    vb_row_d = kb.cin("l1vb_row",
                      np.ascontiguousarray(qkvb[2048:3072].reshape(1, 1024)))
    vb_row = kb.sb("l1vb_sb", [1, 1024], F32)
    vb_bf = kb.sb("l1vb_bf", [1, 1024], BF16)
    tvb = kb.dma(vb_row[:], vb_row_d)
    tvb = kb.copy(vb_bf[:], vb_row[:], waits=[tvb], sig=True)
    b1_sb, tb1 = stage_cols(kb, "l1b1", W["b1"], n=32)
    b2_sb, tb2 = stage_cols(kb, "l1b2", W["b2"])

    wqkv_d = kb.cin("l1wqkv", W["w_qkv"])
    wout_d = kb.cin("l1wout", W["w_out"])
    u1_d = kb.cin("l1u1", W["U1"])
    v1_d = kb.cin("l1v1", W["V1"])
    u2_d = kb.cin("l1u2", W["U2"])
    v2_d = kb.cin("l1v2", W["V2"])

    dk = 64
    inv = (10000.0 ** (-(np.arange(0, dk, 2, dtype=np.float32)) / dk)
           ).astype(np.float32)
    ang = np.arange(T, dtype=np.float32)[:, None] * inv[None, :]
    cos, sin = np.cos(ang), np.sin(ang)
    cos_e = np.repeat(cos.T, 2, axis=0)
    sin_e = np.repeat(sin.T, 2, axis=0)
    sgn = np.ones((64, 1), np.float32)
    sgn[0::2] = -1.0
    sin_s = sin_e * sgn
    cos128 = np.tile(cos_e, (2, 1))
    sin128 = np.tile(sin_s, (2, 1))
    cq_d = kb.cin("l1cosq", (cos128 * 0.125).astype(BF))
    sq_d = kb.cin("l1sinq", (sin128 * 0.125).astype(BF))
    ck_d = kb.cin("l1cosk", cos128.astype(BF))
    sk_d = kb.cin("l1sink", sin128.astype(BF))
    perm = np.zeros((128, 128), np.float32)
    for f in range(64):
        perm[2 * f, 2 * f + 1] = 1.0
        perm[2 * f + 1, 2 * f] = 1.0
    perm_d = kb.cin("l1perm", perm.astype(BF))
    PERM = kb.sb("l1perm_sb", [128, 128], BF16)
    t_perm = kb.dma(PERM[:], perm_d)

    nm_dram = kb.dtmp("l1nm", [16, 2048], BF16)
    sdram = kb.dtmp("s_spill", [NKT, 128, 2048], F32)
    dn_dram = kb.dtmp("l1dn", [1, 2048], F32)

    XN = kb.abuf([128, NKT, 2064], BF16)
    H16 = kb.abuf([128, 2048], F32)
    kb.WST = kb.abuf([128, 2, NKT, 128], F32)
    kb.WBF = kb.abuf([128, 2, 2, NKT, 128], BF16)
    A = kb.abuf([128, NKT, 2048], BF16)
    Bb = kb.abuf([128, NKT, 2048], BF16)
    Cc = kb.abuf([128, NKT * 4096], U8)
    Dd = kb.abuf([128, NKT * 4096], U8)
    Ff = kb.abuf([128, 2048], F32)

    QF, KF = A, Bb
    VT = Cc[:, 0:32768].bitcast(BF16).rearrange("p (a b) -> p a b", b=1024)
    ROPET = Cc[:, 0:16384].bitcast(BF16).rearrange("p (a b) -> p a b", b=2048)
    GT = Cc[:, 0:8192].bitcast(F32)
    XT = Cc[:, 8192:16384].bitcast(F32)
    ST = Cc[:, 16384:24576].bitcast(F32)
    HS = Cc[:, 24576:28672].bitcast(BF16)
    HS2 = Cc[:, 28672:32768].bitcast(BF16)
    MB = Dd[:, 16384:24576].bitcast(F32)
    RB = Dd[:, 24576:32768].bitcast(F32)
    VWBF = Dd[:, 0:16384].bitcast(BF16).rearrange("p (a b) -> p a b", b=1024)
    VWST = Dd[:, 16384:20480].bitcast(F32)
    SQ2 = Dd[:, 0:32768].bitcast(BF16).rearrange("p (a b) -> p a b", b=2048)
    NQ = Dd[:, 24576:32768].bitcast(F32)
    WOIN = Dd[:, 0:32768].bitcast(BF16).rearrange("p (a b) -> p a b", b=2048)
    TQ = Dd[:, 16384:24576].bitcast(F32)
    SW = Dd[:, 24576:32768].bitcast(F32)
    XST = Ff[:, 0:2048]
    EX = Ff[:, 0:2048].bitcast(BF16).rearrange("p (a b) -> p a b", b=2048)
    SOUT = Ff[:, 0:2048]
    DNRB = H16
    OSH = Ff.bitcast(BF16)

    xn_toks = emit_ln(kb, xdram, XN, A, Bb, XST, H16[0:16, :],
                      RB, MB, ones_bf[:, 0:1], H16)

    def xn_t(kt):
        return XN[:, kt, 16:2064]

    t_cq = kb.dma(ROPET[:, 0, :], cq_d)
    t_sq = kb.dma(ROPET[:, 1, :], sq_d)
    t_ck = kb.dma(ROPET[:, 2, :], ck_d)
    t_sk = kb.dma(ROPET[:, 3, :], sk_d)
    rope_toks = [t_cq, t_sq, t_ck, t_sk, t_perm]

    kb.proj_guard = kb.tok("tensor")

    HSB = HS
    rope_prev = [None]

    def mk_evac_rope(dst, cos_t, sin_t, bias_cols):
        prev = rope_prev

        def evac(ot, ps, tok):
            t1 = kb.act(TQ, ps[:], AF.Identity, bias=bias_cols[:, ot:ot + 1],
                        waits=[tok, prev[0]], sig=True)
            t1b = kb.act(HSB, ps[:], AF.Identity,
                         bias=bias_cols[:, ot:ot + 1], waits=[tok], sig=True)
            tok2 = None
            for c in range(4):
                sl = slice(c * 512, (c + 1) * 512)
                tok2 = kb.mm(ps[:, sl], PERM[:], HSB[:, sl],
                             start=True, stop=True,
                             waits=[t1, t1b] + rope_toks, sig=(c == 3))
            t3 = kb.act(SW, ps[:], AF.Identity, waits=[tok2], sig=True)
            kb.tt(TQ, TQ, cos_t, ALU.mult, waits=[t3, t1])
            kb.tt(SW, SW, sin_t, ALU.mult, waits=[])
            t5 = kb.tt(dst[:, ot, :], TQ, SW, ALU.add, waits=[], sig=True)
            prev[0] = t5
            return t5
        return evac

    ProjFM(kb, wqkv_d, NKT, xn_t,
           mk_evac_rope(QF, ROPET[:, 0, :], ROPET[:, 1, :], qb_sb),
           rhs_waits=xn_toks, ot_off=0)
    q_done = kb.tok("vector")
    ProjFM(kb, wqkv_d, NKT, xn_t,
           mk_evac_rope(KF, ROPET[:, 2, :], ROPET[:, 3, :], kb_sb),
           rhs_waits=xn_toks, ot_off=8, ps_guard=[q_done])
    k_done = kb.tok("vector")

    vw_toks = []
    prev_cast = k_done
    for kt in range(NKT):
        tok = kb.dma(VWST, wqkv_d[kt * 128:(kt + 1) * 128, 2048:3072],
                     waits=[prev_cast, k_done])
        prev_cast = kb.copy(VWBF[:, kt], VWST, waits=[tok, k_done], sig=True)
        vw_toks.append(prev_cast)
    ev_tok = [k_done, k_done]
    for tt in range(16):
        b = tt % 2
        ps = [kb.PA, kb.PB][b]
        tok_mm = None
        for c in range(2):
            sl = slice(c * 512, (c + 1) * 512)
            for kt in range(NKT):
                tok_mm = kb.mm(ps[:, sl], xn_t(kt)[:, tt * 128:(tt + 1) * 128],
                               VWBF[:, kt, sl], start=(kt == 0), stop=False,
                               waits=[vw_toks[kt], ev_tok[b], t_or, tvb,
                                      k_done])
            tok_mm = kb.mm(ps[:, sl], ones_row[:, :], vb_bf[:, sl],
                           start=False, stop=True, waits=[], sig=(c == 1))
        ev_tok[b] = kb.act(VT[:, tt, :], ps[:, 0:1024], AF.Identity,
                           waits=[tok_mm], sig=True)
    vt_tok = kb.tok("scalar")
    kb.dbg("qf", QF, waits=[q_done])
    kb.dbg("kf", KF, waits=[k_done])
    kb.dbg("vt", VT, waits=[vt_tok])

    mm_tok = {}
    for idx, src_t in enumerate((QF, KF)):
        stok = None
        for kt in range(NKT):
            stok = kb.act(SQ2[:, kt], src_t[:, kt, :], AF.Square,
                          waits=[vt_tok, mm_tok.get(0)],
                          sig=(kt == NKT - 1))
        ps = [kb.PA, kb.PB][idx]
        for c in range(4):
            sl = slice(c * 512, (c + 1) * 512)
            for kt in range(NKT):
                mm_tok[idx] = kb.mm(
                    ps[0:16, sl], sel_sb[:, kt, :], SQ2[:, kt, sl],
                    start=(kt == 0), stop=(kt == NKT - 1),
                    waits=[stok, tok_sel, vt_tok],
                    sig=(kt == NKT - 1 and c == 3))
    t_nq = kb.act(H16[0:16, :], kb.PA[0:16, :], AF.Sqrt,
                  waits=[mm_tok[0]], sig=True)
    t_nk = kb.act(NQ[0:16, :], kb.PB[0:16, :], AF.Sqrt,
                  waits=[mm_tok[1]], sig=True)
    t_mk = kb.red(mq[:, 0:1], NQ[0:16, :], ALU.max, waits=[t_nk], sig=True)
    t_mks = kb.ts(mq[:, 1:2], mq[:, 0:1], -1.0, None, ALU.mult,
                  waits=[], hard=[t_mk], sig=True)
    t_b = kb.ts(NQ[0:16, :], H16[0:16, :], mq[:, 1:2], 40.0, ALU.mult,
                ALU.add, waits=[t_nq], hard=[t_mks], sig=True)
    t_nm = kb.dma(nm_dram, NQ[0:16, :], waits=[t_b], eng="gpsimd")

    den_dram = kb.dtmp("l1den", [16, 2048], F32)
    rec_dram = kb.dtmp("l1rec", [16, 2048], F32)
    den_prev = None
    exp_last = t_b
    ex_last = {0: None, 1: None}
    nm_tok = None
    pb_free = t_nk
    for h in range(16):
        dt, po = h // 2, (h % 2) * 64
        nm_tok = kb.dma(NM[:], nm_dram[h:h + 1, :], waits=[t_nm, exp_last], eng="gpsimd")
        ctx_mm = None
        va_tok = {0: t_va, 1: t_va}
        for ki in range(16):
            eb = ki % 2
            tq0 = ki * 128
            ln = T - tq0
            sc_tok = None
            nch = (ln + 511) // 512
            for c in range(nch):
                a0 = c * 512
                a1 = min(ln, a0 + 512)
                sc_tok = kb.mm(
                    kb.PA[:, a0:a1],
                    KF[po:po + 64, dt, tq0:tq0 + 128],
                    QF[po:po + 64, dt, tq0 + a0:tq0 + a1],
                    start=True, stop=False,
                    waits=[exp_last, nm_tok], sig=False)
                sc_tok = kb.mm(
                    kb.PA[:, a0:a1], ones_row[:, :],
                    NM[:, tq0 + a0:tq0 + a1],
                    start=False, stop=True,
                    waits=[], sig=(c == nch - 1))
            e_tok = kb.act(EX[:, eb, 0:ln], kb.PA[:, 0:ln], AF.Exp,
                           waits=[sc_tok, ex_last[eb]], sig=True)
            m_tok = kb.tt(EX[:, eb, 0:128], EX[:, eb, 0:128], TRI[:],
                          ALU.mult, waits=[e_tok, t_tri], sig=True)
            va_t = kb.copy(VA[:, eb, 0:64], VT[:, ki, h * 64:(h + 1) * 64],
                           waits=[vt_tok, va_tok[eb]], sig=True)
            segs = []
            g0 = tq0
            while g0 < T:
                g1 = min(T, (g0 // 512 + 1) * 512)
                segs.append((g0, g1))
                g0 = g1
            for si, (g0, g1) in enumerate(segs):
                ctx_mm = kb.mm(kb.PB[0:65, g0:g1],
                               VA[:, eb, 0:65], EX[:, eb, g0 - tq0:g1 - tq0],
                               start=(ki == 0), stop=(ki == 15),
                               waits=[va_t, m_tok, pb_free],
                               sig=(si == len(segs) - 1))
            ex_last[eb] = ctx_mm
            va_tok[eb] = ctx_mm
            exp_last = e_tok
        d1 = kb.act(DNRB[64:65, :], kb.PB[64:65, :], AF.Identity,
                    waits=[ctx_mm, den_prev], sig=True)
        den_prev = kb.dma(den_dram[h:h + 1], DNRB[64:65, :], waits=[d1],
                          eng="gpsimd")
        if po == 0:
            pb_free = kb.act(WOIN[0:64, dt, :], kb.PB[0:64, :], AF.Identity,
                             waits=[ctx_mm], sig=True)
        else:
            t5 = kb.act(OSH[0:64, 0:2048], kb.PB[0:64, :], AF.Identity,
                        waits=[ctx_mm, ex_last[0], ex_last[1]], sig=True)
            pb_free = kb.dma(WOIN[64:128, dt, :], OSH[0:64, 0:2048],
                             waits=[t5], eng="gpsimd")
        exp_last = pb_free if po else exp_last
    ctx_done = pb_free

    tld = kb.dma(DNRB[0:16, :], den_dram, waits=[den_prev], eng="gpsimd")
    trc = kb.recip(DNRB[0:16, :], DNRB[0:16, :], waits=[tld], sig=True)
    trd = kb.dma(rec_dram, DNRB[0:16, :], waits=[trc], eng="gpsimd")
    norm_btok = {0: trd, 1: trd}
    norm_last = None
    for h in range(16):
        dt, po = h // 2, (h % 2) * 64
        bt = kb.dma(DNRB[po:po + 64, :],
                    rec_dram[h:h + 1].squeeze(0).partition_broadcast(64),
                    waits=[trd, norm_btok[h % 2]], eng="gpsimd")
        norm_last = kb.tt(WOIN[po:po + 64, dt, :], WOIN[po:po + 64, dt, :],
                          DNRB[po:po + 64, :], ALU.mult,
                          waits=[bt, ctx_done], sig=True)
        norm_btok[h % 2] = norm_last
    ctx_done = norm_last

    kb.dbg("woin", WOIN, waits=[ctx_done])

    st_tok = [None]

    def evac_s(ot, ps, tok):
        t1 = kb.act(SOUT, ps[:], AF.Identity,
                    bias=ob_sb[:, ot:ot + 1], waits=[tok, st_tok[0], t_nm],
                    sig=True)
        st_tok[0] = kb.dma(sdram[ot], SOUT, waits=[t1])
        return t1

    ProjFM(kb, wout_d, NKT, lambda kt: WOIN[:, kt, :], evac_s,
           rhs_waits=[ctx_done], ps_guard=[exp_last, ctx_done])
    s_done = ("dy", kb.counts["dy"])
    s_evac_done = kb.tok("scalar")

    PBUF = Dd[:, 0:8192].bitcast(BF16).rearrange("p (a b) -> p a b", b=2048)

    def evac_p(ot, ps, tok):
        return kb.act(PBUF[:, ot, :], ps[:], AF.Identity, waits=[tok],
                      sig=True)

    ProjFM(kb, u1_d, 2, xn_t, evac_p, rhs_waits=xn_toks + [ctx_done],
           ps_guard=[s_evac_done])
    p_tok = kb.tok("scalar")

    QS = Dd[:, 8192:16384].bitcast(BF16).rearrange("p (a b) -> p a b", b=2048)
    u2st = Dd[:, 16384:18432].bitcast(F32).rearrange("p (a b) -> p a b", b=256)
    u2bf = Dd[:, 18432:19456].bitcast(BF16).rearrange("p (a b) -> p a b", b=256)
    hq_last = p_tok
    qmm_last = {0: p_tok, 1: p_tok}
    u2c_prev = [p_tok, p_tok]
    q_mm = None
    for th in range(2):
        tsl = slice(th * 1024, (th + 1) * 1024)
        for fo in range(32):
            b = fo % 2
            wv = v1_d[:, fo * 128:(fo + 1) * 128].rearrange(
                "(k p) m -> p k m", p=128)
            tok_d = kb.dma(kb.WST[:, b, 0:2], wv, waits=[u2c_prev[b]])
            ct = None
            for kt in range(2):
                ct = kb.copy(kb.WBF[:, b, 0, kt], kb.WST[:, b, kt],
                             waits=[tok_d, qmm_last[b]], sig=(kt == 1))
            tu = kb.dma(u2st[:, b, :], u2_d[fo * 128:(fo + 1) * 128, :],
                        waits=[u2c_prev[b]])
            cu = kb.copy(u2bf[:, b, :], u2st[:, b, :], waits=[tu], sig=True)
            u2c_prev[b] = cu
            h_mm = None
            for c in range(2):
                sl = slice(c * 512, (c + 1) * 512)
                gl = slice(th * 1024 + c * 512, th * 1024 + (c + 1) * 512)
                for kt in range(2):
                    h_mm = kb.mm(kb.PA[:, sl], kb.WBF[:, b, 0, kt],
                                 PBUF[:, kt, gl],
                                 start=(kt == 0), stop=(kt == 1),
                                 waits=[ct, p_tok, hq_last],
                                 sig=(c == 1 and kt == 1))
            hsb = [HS, HS2][b]
            hq = kb.act(hsb[:, 0:1024], kb.PA[:, 0:1024], AF.Gelu_apprx_tanh,
                        bias=b1_sb[:, fo:fo + 1],
                        waits=[h_mm, qmm_last[b]], sig=True)
            hq_last = hq
            for qt in range(2):
                for c in range(2):
                    sl = slice(qt * 1024 + c * 512, qt * 1024 + (c + 1) * 512)
                    hsl = slice(c * 512, (c + 1) * 512)
                    q_mm = kb.mm(kb.PB[:, sl],
                                 u2bf[:, b, qt * 128:(qt + 1) * 128],
                                 hsb[:, hsl],
                                 start=(fo == 0), stop=(fo == 31),
                                 waits=[cu, hq, s_evac_done],
                                 sig=(qt == 1 and c == 1))
            qmm_last[b] = q_mm
        eq = None
        for qt in range(2):
            eq = kb.act(QS[:, qt, tsl], kb.PB[:, qt * 1024:(qt + 1) * 1024],
                        AF.Identity, waits=[q_mm], sig=True)
        hq_last = eq
        qmm_last = {0: eq, 1: eq}
        u2c_prev = [eq, eq]
    q_done2 = kb.tok("scalar")

    kb.dbg("s", sdram, waits=[s_done])
    kb.dbg("pbuf", PBUF, waits=[p_tok])
    kb.dbg("qs", QS, waits=[q_done2])

    upd_tok = [None]

    def evac_f3(ot, ps, tok):
        tx = kb.dma(XT, xdram[ot], waits=[upd_tok[0]])
        tss = kb.dma(ST, sdram[ot], waits=[s_done, upd_tok[0]])
        t1 = kb.act(GT, ps[:], AF.Silu, bias=b2_sb[:, ot:ot + 1],
                    waits=[tok, upd_tok[0]], sig=True)
        t2 = kb.tt(GT, GT, ST, ALU.mult, waits=[t1, tss], sig=False)
        t2 = kb.tt(XT, XT, GT, ALU.add, waits=[tx], sig=True)
        upd_tok[0] = kb.dma(xodram[ot], XT, waits=[t2])
        return t2

    ProjFM(kb, v2_d, NKT, lambda kt: QS[:, kt, :], evac_f3,
           rhs_waits=[q_done2], ps_guard=[q_done2], n_kt=2)


def prep_weights_l1(inputs):
    return dict(
        w_qkv=np.ascontiguousarray(inputs["L1_attn_w_qkv"].T),
        b_qkv=inputs["L1_attn_b_qkv"],
        w_out=np.ascontiguousarray(inputs["L1_attn_w_out"].T),
        b_out=inputs["L1_attn_b_out"],
        U1=inputs["L1_ffn_U1"], V1=inputs["L1_ffn_V1"],
        b1=inputs["L1_ffn_b1"],
        U2=inputs["L1_ffn_U2"], V2=inputs["L1_ffn_V2"],
        b2=inputs["L1_ffn_b2"],
    )


def run_l1(x_full, inputs, dbg=False):
    W = prep_weights_l1(inputs)
    kb = KB()
    kb.dbg_on = dbg
    xin = kb.din("Xin", [NKT, 128, 2048], F32)
    xout = kb.dout("Xout", [NKT, 128, 2048], F32)
    emit_layer1(kb, W, xin, xout)
    kb.build()
    in_maps = []
    for k in range(NCORES):
        m = dict(kb.consts)
        m["Xin"] = np.ascontiguousarray(x_full[k].T.reshape(NKT, 128, 2048))
        in_maps.append(m)
    res = run_bass_kernel_spmd(kb.nc, in_maps, core_ids=list(range(NCORES)))
    out = np.stack([
        res.results[k]["Xout"].reshape(1024, 2048).T for k in range(NCORES)])
    _nancheck("L1", out)
    if dbg:
        return out, res.results
    return out



def emit_layer0a(kb, W, xpdram, cfu_out):
    ones_bf = kb.sb("ones_bf", [128, 4], BF16)
    kb.memset(ones_bf[:], 1.0, sig=True)
    ONESF = kb.sb("onesf", [128, 256], F32)
    t_of = kb.memset(ONESF[:], 1.0, sig=True)
    id_d = kb.cin("id128", np.eye(128).astype(BF))
    ID = kb.sb("id128_sb", [128, 128], BF16)
    t_id = kb.dma(ID[:], id_d)
    bf_sb, _ = stage_cols(kb, "l0bf", W["bf"])
    bi_sb, _ = stage_cols(kb, "l0bi", W["bi"])
    bc_sb, _ = stage_cols(kb, "l0bc", W["bc"])
    inc_sb, _ = stage_cols(kb, "l0inc", W["in_c"])
    wf_d = kb.cin("l0wf", W["wfT"])
    wi_d = kb.cin("l0wi", W["wiT"])
    wc_d = kb.cin("l0wc", W["wcT"])
    u_d = {0: kb.cin("l0uf", W["ufT"]), 1: kb.cin("l0ui", W["uiT"]),
           2: kb.cin("l0uc", W["ucT"])}
    uo_d = kb.cin("l0uo", W["uoT"])
    ubf_dram = kb.dtmp("l0ubf", [3, 1024, 1024], BF16)

    XNP = kb.abuf([128, NKT, 2064], BF16)
    H16 = kb.abuf([128, 2048], F32)
    kb.WST = kb.abuf([128, 2, NKT, 128], F32)
    kb.WBF = kb.abuf([128, 2, 2, NKT, 128], BF16)
    A = kb.abuf([128, NKT * 4096], U8)
    Bb = kb.abuf([128, NKT * 4096], U8)
    Cc = kb.abuf([128, NKT * 4096], U8)
    Dd = kb.abuf([128, NKT * 4096], U8)
    Ff = kb.abuf([128, 2048], F32)

    Abf = A.bitcast(BF16).rearrange("p (a b) -> p a b", b=2048)
    Bbf = Bb.bitcast(BF16).rearrange("p (a b) -> p a b", b=2048)
    XF = Abf
    XI = Bbf
    XC = Cc.bitcast(BF16).rearrange("p (a b) -> p a b", b=2048)
    MB = Dd[:, 16384:24576].bitcast(F32)
    RB = Dd[:, 24576:32768].bitcast(F32)
    C = Dd[:, 0:8192].bitcast(F32).rearrange("p (a b) -> p a b", b=256)
    CBF = Dd[:, 8192:12288].bitcast(BF16).rearrange("p (a b) -> p a b", b=256)
    FG = Dd[:, 16384:24576].bitcast(F32).rearrange("p (a b) -> p a b", b=256)
    IG = Dd[:, 24576:32768].bitcast(F32).rearrange("p (a b) -> p a b", b=256)
    UBUF = Dd[:, 12288:16384].bitcast(BF16).rearrange(
        "p (a b c) -> p a b c", b=NKT, c=128)
    GG = Ff.bitcast(F32).rearrange("p (a b) -> p a b", b=256)
    XST = Ff[:, 0:2048]

    xn_toks = emit_ln(kb, xpdram, XNP, Abf, Bbf, XST, H16[0:16, :],
                      RB, MB, ones_bf[:, 0:1], H16)
    ln_done = kb.tok("vector")

    def xnp_t(kt):
        return XNP[:, kt, 16:2064]

    kb.proj_guard = kb.tok("tensor")

    UST = Dd[:, 0:2048].bitcast(BF16).rearrange(
        "p (a b) -> p a b", b=128)
    ucast_tok = None
    for g3 in range(3):
        for kt in range(NKT):
            t1 = kb.dma(UST[:, :], u_d[g3][kt * 128:(kt + 1) * 128, :]
                        .rearrange("p (a b) -> p a b", b=128),
                        waits=[ucast_tok], eng="gpsimd")
            ucast_tok = kb.dma(ubf_dram[g3, kt * 128:(kt + 1) * 128, :]
                               .rearrange("p (a b) -> p a b", b=128),
                               UST[:, :], waits=[t1], eng="gpsimd")
    u_ready = ucast_tok

    def mk_evac(dst):
        def evac(ot, ps, tok):
            return kb.act(dst[:, ot, :], ps[:], AF.Identity, waits=[tok],
                          sig=True)
        return evac

    ProjFM(kb, wf_d, NKT, xnp_t, mk_evac(XF), rhs_waits=xn_toks)
    ProjFM(kb, wi_d, NKT, xnp_t, mk_evac(XI), ps_guard=[kb.tok("scalar")])
    ProjFM(kb, wc_d, NKT, xnp_t, mk_evac(XC), ps_guard=[kb.tok("scalar")])
    xw_done = kb.tok("scalar")

    c_tok = None
    for ot in range(NKT):
        c_tok = kb.ts(C[:, ot, :], ONESF[:], inc_sb[:, ot:ot + 1], None,
                      ALU.mult, waits=[t_of, ln_done, u_ready],
                      sig=(ot == NKT - 1))
    cb_tok = None
    for ot in range(NKT):
        cb_tok = kb.copy(CBF[:, ot, :], C[:, ot, :], waits=[c_tok],
                         sig=(ot == NKT - 1))

    gate_ps = [kb.PA, kb.PB]
    u_used = {0: u_ready, 1: u_ready}
    for t in range(8):
        tsl = slice(t * 256, (t + 1) * 256)
        evac_toks = []
        for gi, (XG, bias, dst, fn) in enumerate((
                (XF, bf_sb, FG, AF.Sigmoid),
                (XI, bi_sb, IG, AF.Sigmoid),
                (XC, bc_sb, GG, AF.Tanh))):
            ps = gate_ps[gi % 2]
            guard = evac_toks[-2] if len(evac_toks) >= 2 else cb_tok
            mms = None
            for ot in range(NKT):
                ub = ot % 2
                osl = slice(ot * 256, (ot + 1) * 256)
                tu = kb.dma(UBUF[:, ub], ubf_dram[gi, :, ot * 128:
                            (ot + 1) * 128].rearrange(
                                "(k p) m -> p k m", p=128),
                            waits=[u_used[ub], u_ready])
                for kt in range(NKT):
                    mms = kb.mm(ps[:, osl], UBUF[:, ub, kt, :],
                                CBF[:, kt, :],
                                start=(kt == 0), stop=False,
                                waits=[tu, cb_tok, guard, t_id, xw_done])
                mms = kb.mm(ps[:, osl], ID[:], XG[:, ot, tsl],
                            start=False, stop=True, sig=True)
                u_used[ub] = mms
            ev = None
            for ot in range(NKT):
                osl = slice(ot * 256, (ot + 1) * 256)
                ev = kb.act(dst[:, ot, :], ps[:, osl], fn,
                            bias=bias[:, ot:ot + 1], waits=[mms],
                            sig=(ot == NKT - 1))
            evac_toks.append(ev)
        t1 = None
        for ot in range(NKT):
            kb.tt(FG[:, ot, :], FG[:, ot, :], C[:, ot, :], ALU.mult,
                  waits=[evac_toks[0], evac_toks[2]])
            kb.tt(IG[:, ot, :], IG[:, ot, :], GG[:, ot, :], ALU.mult,
                  waits=[evac_toks[1]])
            t1 = kb.tt(C[:, ot, :], FG[:, ot, :], IG[:, ot, :], ALU.add,
                       waits=[], sig=(ot == NKT - 1))
        cb_tok = None
        for ot in range(NKT):
            cb_tok = kb.copy(CBF[:, ot, :], C[:, ot, :], waits=[t1],
                             sig=(ot == NKT - 1))

    cf_prev = [None]

    def evac_cfu(ot, ps, tok):
        t1 = kb.act(H16[:, 0:256], ps[:, 0:256], AF.Identity,
                    waits=[tok, cf_prev[0]], sig=True)
        cf_prev[0] = kb.dma(cfu_out[ot], H16[:, 0:256], waits=[t1])
        return t1

    ProjFM(kb, uo_d, NKT, lambda kt: CBF[:, kt, :], evac_cfu, T=256,
           rhs_waits=[cb_tok], ps_guard=[kb.tok("scalar")])


def emit_layer0b(kb, W, xdram, cfudram, xodram):
    ones_bf = kb.sb("ones_bf", [128, 4], BF16)
    kb.memset(ones_bf[:], 1.0, sig=True)
    ZC = kb.sb("zeroc", [128, 4], F32)
    t_zc = kb.memset(ZC[:], 0.0, sig=True)
    sel2_np = np.zeros((128, 32), np.float32)
    sel2_np[0:64, 0] = 1.0
    sel2_np[64:128, 1] = 1.0
    sel2_d = kb.cin("l0sel2", sel2_np[:, 0:2].astype(BF))
    SEL2 = kb.sb("sel2_sb", [128, 2], BF16)
    t_sel2 = kb.dma(SEL2[:], sel2_d)
    mixw = {}
    for nm in ("x_w", "x_a", "x_g", "x_k", "x_v", "x_r"):
        vec = W[nm].reshape(-1)
        a, _ = stage_cols(kb, "l0p_" + nm, 1.0 + vec)
        b, _ = stage_cols(kb, "l0m_" + nm, -vec)
        mixw[nm] = (a, b)
    kk_sb, _ = stage_cols(kb, "l0kk", W["k_k"].reshape(-1))
    rk_sb, _ = stage_cols(kb, "l0rk", W["r_k"].reshape(-1))
    lnw_sb, _ = stage_cols(kb, "l0lnw", W["ln_w"])
    lnb_sb, _ = stage_cols(kb, "l0lnb", W["ln_b"])
    bo_sb, _ = stage_cols(kb, "l0bo", W["bo"])
    w0_sb, _ = stage_cols(kb, "l0w0",
                          np.pad(W["w0"].reshape(-1), (0, 112)), n=1)
    a0_sb, _ = stage_cols(kb, "l0a0",
                          np.pad(W["a0"].reshape(-1), (0, 112)), n=1)
    w1_d = kb.cin("l0w1", W["w1"])
    w2_d = kb.cin("l0w2", W["w2"])
    a1_d = kb.cin("l0a1", W["a1"])
    a2_d = kb.cin("l0a2", W["a2"])
    g1_d = kb.cin("l0g1", W["g1"])
    g2_d = kb.cin("l0g2", W["g2"])
    wr_d = kb.cin("l0wr", W["Wr"])
    wk_d = kb.cin("l0wk", W["Wk"])
    wv_d = kb.cin("l0wv", W["Wv"])
    wo_d = kb.cin("l0wo", W["Wo"])
    wof_d = kb.cin("l0wof", W["woT"])
    sdram = kb.dtmp("s_spill", [NKT, 128, 2048], F32)
    row_dram = kb.dtmp("l0rows", [16, 2048], F32)
    row2_dram = kb.dtmp("l0rows2", [4, 2048], F32)

    XN = kb.abuf([128, NKT, 2064], BF16)
    H16 = kb.abuf([128, 4096], F32)
    kb.WST = kb.abuf([128, 2, NKT, 128], F32)
    kb.WBF = kb.abuf([128, 2, 2, NKT, 128], BF16)
    A = kb.abuf([128, NKT, 2048], BF16)
    Bb = kb.abuf([128, NKT, 2048], BF16)
    Cc = kb.abuf([128, NKT * 4096], U8)
    Dd = kb.abuf([128, NKT * 4096], U8)
    Ff = kb.abuf([128, 2048], F32)
    Q0 = H16[0:2, 0:2048]
    Q1 = H16[0:2, 2048:4096]

    GA = A
    WOIN = Bb
    MB = Dd[:, 16384:24576].bitcast(F32)
    RB = Dd[:, 24576:32768].bitcast(F32)
    XST = Ff[:, 0:2048]
    KK = Cc[:, 0:8192].bitcast(F32)
    VV = Cc[:, 8192:16384].bitcast(F32)
    RRt = Cc[:, 16384:24576].bitcast(F32)
    STT = Cc[:, 24576:32768].bitcast(F32)
    WBC = Dd[:, 0:8192].bitcast(F32)
    RAB = Dd[:, 8192:16384].bitcast(F32)
    SQH = Dd[:, 16384:20480].bitcast(BF16)
    XWK = Dd[:, 20480:28672].bitcast(F32)
    RSB = Dd[:, 28672:32768].bitcast(BF16)
    CFUB = Dd.bitcast(BF16).rearrange("p (a b) -> p a b", b=2048)

    xn_toks = emit_ln(kb, xdram, XN, A, Bb, XST, H16[0:16, 0:2048],
                      RB, MB, ones_bf[:, 0:1], H16[:, 0:2048])
    ln_done = kb.tok("vector")

    def xn_t(kt):
        return XN[:, kt, 16:2064]

    def xnsh_t(kt):
        return XN[:, kt, 15:2063]

    kb.proj_guard = kb.tok("tensor")

    HW = Ff[:, 0:2048].bitcast(BF16)
    hw_tok = [None]

    def evac_hidden(ot, ps, tok):
        t1 = kb.act(HW[0:64, 0:2048], ps[0:64, :], AF.Tanh,
                    waits=[tok, hw_tok[0]], sig=True)
        hw_tok[0] = t1
        return t1

    ProjFM(kb, w1_d, 1, xn_t, evac_hidden, rhs_waits=xn_toks,
           scale_cols=mixw["x_w"][0], scale2_cols=mixw["x_w"][1],
           rhs2_fn=xnsh_t, m_rows=64)
    t_hw = hw_tok[0]
    w2bf = kb.sb("l0w2bf", [64, 2, 16], BF16)
    w2st = kb.sb("l0w2st", [64, 2, 16], F32)
    t_w2 = kb.dma(w2st[:, 0, :], w2_d, waits=[])
    t_w2 = kb.copy(w2bf[:, 0, :], w2st[:, 0, :], waits=[t_w2], sig=True)
    zmm = None
    for c in range(4):
        sl = slice(c * 512, (c + 1) * 512)
        zmm = kb.mm(kb.PA[0:16, sl], w2bf[0:64, 0, :], HW[0:64, sl],
                    start=True, stop=True, waits=[t_w2, t_hw],
                    sig=(c == 3))
    t_w16 = kb.act(H16[0:16, 0:2048], kb.PA[0:16, :], AF.Sigmoid,
                   bias=w0_sb[0:16, 0:1], waits=[zmm], sig=True)
    t_w16b = kb.ts(H16[0:16, 0:2048], H16[0:16, 0:2048],
                   float(np.exp(-0.5)), None,
                   ALU.mult, waits=[t_w16], sig=True, eng="vector")
    t_wrow = kb.dma(row_dram, H16[0:16, 0:2048], waits=[t_w16b])
    ProjFM(kb, a1_d, 1, xn_t, evac_hidden, rhs_waits=xn_toks,
           scale_cols=mixw["x_a"][0], scale2_cols=mixw["x_a"][1],
           rhs2_fn=xnsh_t, m_rows=64, ps_guard=[t_w16])
    t_ha = hw_tok[0]
    t_a2 = kb.dma(w2st[:, 1, :], a2_d, waits=[])
    t_a2 = kb.copy(w2bf[:, 1, :], w2st[:, 1, :], waits=[t_a2], sig=True)
    zmm = None
    for c in range(4):
        sl = slice(c * 512, (c + 1) * 512)
        zmm = kb.mm(kb.PA[0:16, sl], w2bf[0:64, 1, :], HW[0:64, sl],
                    start=True, stop=True, waits=[t_a2, t_ha, t_w16b],
                    sig=(c == 3))
    t_a16 = kb.act(H16[0:16, 0:2048], kb.PA[0:16, :], AF.Sigmoid,
                   bias=a0_sb[0:16, 0:1], waits=[zmm, t_wrow], sig=True)
    arow_dram = kb.dtmp("l0arow", [16, 2048], F32)
    t_arow = kb.dma(arow_dram, H16[0:16, 0:2048], waits=[t_a16])
    ProjFM(kb, g1_d, 1, xn_t, evac_hidden, rhs_waits=xn_toks,
           scale_cols=mixw["x_g"][0], scale2_cols=mixw["x_g"][1],
           rhs2_fn=xnsh_t, m_rows=64, ps_guard=[t_a16])
    t_hg = hw_tok[0]

    kb.dbg("hg", HW[0:64, 0:2048], waits=[t_hg])

    def evac_ga(ot, ps, tok):
        return kb.act(GA[:, ot, :], ps[:], AF.Sigmoid, waits=[tok], sig=True)

    ProjFM(kb, g2_d, NKT, lambda kt: HW[0:64, 0:2048], evac_ga,
           rhs_waits=[t_hg], n_kt=1, k_rows=64, ps_guard=[t_a16])
    ga_done = kb.tok("scalar")

    bs_dram = kb.dtmp("l0bs", [2, 2048], F32)
    woin_tok = ga_done
    g = {}
    for dt in range(NKT):
        def evac_kk(ot, ps, tok, dt=dt, g=g):
            t1 = kb.ts(KK, ps[:], kk_sb[:, dt:dt + 1], None, ALU.mult,
                       waits=[tok, g.get("woin")], sig=True)
            return t1
        ProjFM(kb, wk_d, 1, xn_t, evac_kk, rhs_waits=xn_toks,
               scale_cols=mixw["x_k"][0], scale2_cols=mixw["x_k"][1],
               rhs2_fn=xnsh_t, ot_off=dt, ps_guard=[woin_tok])
        kk_e = kb.tok("vector")

        def evac_vv(ot, ps, tok, g=g):
            return kb.act(VV, ps[:], AF.Identity,
                          waits=[tok, g.get("woin")], sig=True)
        ProjFM(kb, wv_d, 1, xn_t, evac_vv, rhs_waits=xn_toks,
               scale_cols=mixw["x_v"][0], scale2_cols=mixw["x_v"][1],
               rhs2_fn=xnsh_t, ot_off=dt)
        vv_e = kb.tok("scalar")

        def evac_rr(ot, ps, tok, g=g):
            return kb.act(RRt, ps[:], AF.Identity,
                          waits=[tok, g.get("woin")], sig=True)
        ProjFM(kb, wr_d, 1, xn_t, evac_rr, rhs_waits=xn_toks,
               scale_cols=mixw["x_r"][0], scale2_cols=mixw["x_r"][1],
               rhs2_fn=xnsh_t, ot_off=dt)
        rr_e = kb.tok("scalar")

        sq_t = kb.act(SQH, KK, AF.Square, waits=[kk_e, g.get("sqh")],
                      sig=True)
        nmm = None
        for c in range(4):
            sl = slice(c * 512, (c + 1) * 512)
            nmm = kb.mm(kb.PB[0:2, sl], SEL2[:], SQH[:, sl], start=True,
                        stop=True, waits=[sq_t, t_sel2, g.get("h16")],
                        sig=(c == 3))
        n1 = kb.act(Q0, kb.PB[0:2, :], AF.Sqrt, bias=1e-24,
                    waits=[nmm], sig=True)
        n2 = kb.recip(Q0, Q0, waits=[n1], sig=True)
        b1t = kb.tt(XWK, KK, RRt, ALU.mult,
                    waits=[kk_e, rr_e, g.get("xwk")], sig=False)
        b2t = kb.ts(SQH, XWK, rk_sb[:, dt:dt + 1], None, ALU.mult,
                    waits=[], sig=True)
        bmm = None
        for c in range(4):
            sl = slice(c * 512, (c + 1) * 512)
            bmm = kb.mm(kb.PA[0:2, sl], SEL2[:], SQH[:, sl], start=True,
                        stop=True, waits=[b2t, g.get("h16")], sig=(c == 3))
        bev = kb.act(Q1, kb.PA[0:2, :], AF.Identity, waits=[bmm],
                     sig=True)
        bsc = kb.tt(Q1, Q1, Q0, ALU.mult, waits=[bev], hard=[n2], sig=True)
        bst = kb.dma(bs_dram, Q1, waits=[bsc], eng="gpsimd")
        a2r = kb.dma(Q1, arow_dram[2 * dt:2 * dt + 2],
                     waits=[bst], eng="gpsimd")
        n3 = kb.tt(Q0, Q0, Q1, ALU.mult, waits=[a2r, bsc], hard=[n2],
                   sig=True)
        n4 = kb.dma(row2_dram[0:2], Q0, waits=[n3], eng="gpsimd")
        b1 = kb.dma(RAB[0:64, :],
                    row2_dram[0:1].squeeze(0).partition_broadcast(64),
                    waits=[n4, g.get("rab")], eng="gpsimd")
        b2 = kb.dma(RAB[64:128, :],
                    row2_dram[1:2].squeeze(0).partition_broadcast(64),
                    waits=[n4], eng="gpsimd")
        bw1 = kb.dma(WBC[0:64, :],
                     row_dram[2 * dt:2 * dt + 1].squeeze(0)
                     .partition_broadcast(64),
                     waits=[t_wrow, g.get("scan")], eng="gpsimd")
        bw2 = kb.dma(WBC[64:128, :],
                     row_dram[2 * dt + 1:2 * dt + 2].squeeze(0)
                     .partition_broadcast(64),
                     waits=[t_wrow, g.get("scan")], eng="gpsimd")
        dbg_toks = []
        if getattr(kb, "dbg_on", False) and dt == 0:
            def dmp(name, ap, w):
                d = kb.dout("dbg_" + name, list(ap.shape), ap.dtype)
                t = kb.dma(d, ap, waits=w, eng="gpsimd")
                dbg_toks.append(t)
                return t
            dmp("kk0", KK, [kk_e])
            dmp("vv0", VV, [vv_e])
            dmp("rr0", RRt, [rr_e])
            dmp("rab0", RAB, [b1, b2])
            dmp("wbc0", WBC, [bw1, bw2])
        kv1 = kb.tt(KK, KK, VV, ALU.mult, waits=[kk_e, vv_e, b1t] + dbg_toks,
                    sig=False)
        kv2 = kb.tt(KK, KK, RAB, ALU.mult, waits=[b1, b2], sig=True)
        sc_t = kb.emit(
            "vector",
            lambda e, o=STT, d0=WBC, d1=KK, ini=ZC[:, 0:1]:
                e.tensor_tensor_scan(o, d0, d1, ini, ALU.mult, ALU.add),
            waits=[kv2, bw1, bw2, t_zc, g.get("stt")], sig=True)
        bb1 = kb.dma(RAB[0:64, :],
                     bs_dram[0:1].squeeze(0).partition_broadcast(64),
                     waits=[bst, kv2], eng="gpsimd")
        bb2 = kb.dma(RAB[64:128, :],
                     bs_dram[1:2].squeeze(0).partition_broadcast(64),
                     waits=[bst, kv2], eng="gpsimd")
        dbg2 = []
        if getattr(kb, "dbg_on", False) and dt == 0:
            d = kb.dout("dbg_stt0", list(STT.shape), STT.dtype)
            dbg2.append(kb.dma(d, STT, waits=[sc_t], eng="gpsimd"))
            d = kb.dout("dbg_bsb0", list(RAB.shape), RAB.dtype)
            dbg2.append(kb.dma(d, RAB, waits=[bb1, bb2], eng="gpsimd"))
        x1 = kb.tt(XWK, STT, RRt, ALU.mult, waits=[sc_t, rr_e] + dbg2,
                   sig=False)
        x2 = kb.tt(VV, VV, RAB, ALU.mult, waits=[bb1, bb2] + dbg2, sig=False)
        x3 = kb.tt(XWK, XWK, VV, ALU.add, waits=[], sig=True)
        g["stt"] = x1
        xb_t = kb.copy(SQH, XWK, waits=[x3, g.get("sel_mm2")], sig=True)
        gq_t = kb.act(RSB, SQH, AF.Square, waits=[xb_t], sig=True)
        m1 = m2 = None
        for c in range(4):
            sl = slice(c * 512, (c + 1) * 512)
            m1 = kb.mm(kb.PA[0:2, sl], SEL2[:], SQH[:, sl], start=True,
                       stop=True, waits=[xb_t, bev], sig=(c == 3))
        for c in range(4):
            sl = slice(c * 512, (c + 1) * 512)
            m2 = kb.mm(kb.PB[0:2, sl], SEL2[:], RSB[:, sl], start=True,
                       stop=True, waits=[gq_t, n1], sig=(c == 3))
        g["sel_mm2"] = m2
        s1 = kb.ts(Q0, kb.PA[0:2, :], 1.0 / 64, None, ALU.mult,
                   waits=[m1, n4], sig=True)
        s2 = kb.act(Q1, Q0, AF.Square, waits=[s1, a2r], sig=True)
        s3 = kb.stt(Q1, kb.PB[0:2, :], 1.0 / 64, Q1,
                    ALU.mult, ALU.subtract, waits=[m2, s2], sig=True)
        t_dv = None
        if getattr(kb, "dbg_on", False) and dt == 0:
            d = kb.dout("dbg_var0", [2, 2048], F32)
            t_dv = kb.dma(d, Q1, waits=[s3], eng="gpsimd")
            d = kb.dout("dbg_mean0", [2, 2048], F32)
            kb.dma(d, Q0, waits=[s1], eng="gpsimd")
            d = kb.dout("dbg_sqh0", [128, 2048], BF16)
            kb.dma(d, SQH, waits=[xb_t, m1], eng="gpsimd")
        s4 = kb.act(Q1, Q1, AF.Sqrt, bias=1e-5, waits=[s3, t_dv], sig=True)
        s5 = kb.recip(Q1, Q1, waits=[s4], sig=True)
        n5 = kb.dma(row2_dram[0:2], Q1, waits=[b1, b2], hard=[]
                    ) if False else None
        n5 = kb.dma(row2_dram[0:2], Q1, waits=[s5, b1, b2], eng="gpsimd")
        s6 = kb.tt(Q0, Q0, Q1, ALU.mult, waits=[], hard=[s5], sig=True)
        s7 = kb.ts(Q0, Q0, -1.0, None, ALU.mult, waits=[s6], sig=True)
        g["h16"] = s7
        n6 = kb.dma(row2_dram[2:4], Q0, waits=[s7], eng="gpsimd")
        c1 = kb.dma(RAB[0:64, :],
                    row2_dram[0:1].squeeze(0).partition_broadcast(64),
                    waits=[n5, x2], eng="gpsimd")
        c2 = kb.dma(RAB[64:128, :],
                    row2_dram[1:2].squeeze(0).partition_broadcast(64),
                    waits=[n5, x2], eng="gpsimd")
        c3 = kb.dma(WBC[0:64, :],
                    row2_dram[2:3].squeeze(0).partition_broadcast(64),
                    waits=[n6, sc_t], eng="gpsimd")
        c4 = kb.dma(WBC[64:128, :],
                    row2_dram[3:4].squeeze(0).partition_broadcast(64),
                    waits=[n6, sc_t], eng="gpsimd")
        g["rab"] = c2
        if getattr(kb, "dbg_on", False) and dt == 0:
            d = kb.dout("dbg_xwk0", [128, 2048], F32)
            t_dx = kb.dma(d, XWK, waits=[x3], eng="gpsimd")
            d = kb.dout("dbg_rsb0", [128, 2048], F32)
            kb.dma(d, RAB, waits=[c1, c2], eng="gpsimd")
            d = kb.dout("dbg_gbb0", [128, 2048], F32)
            kb.dma(d, WBC, waits=[c3, c4], eng="gpsimd")
        else:
            t_dx = None
        o1 = kb.tt(XWK, XWK, RAB, ALU.mult, waits=[c1, c2, x3, t_dx],
                   sig=False)
        o2 = kb.tt(XWK, XWK, WBC, ALU.add, waits=[c3, c4], sig=False)
        o3 = kb.ts(XWK, XWK, lnw_sb[:, dt:dt + 1], lnb_sb[:, dt:dt + 1],
                   ALU.mult, ALU.add, waits=[], sig=False)
        woin_tok = kb.tt(WOIN[:, dt, :], XWK, GA[:, dt, :], ALU.mult,
                         waits=[ga_done], sig=True)
        g["woin"] = woin_tok
        g["xwk"] = woin_tok
        g["sqh"] = m1
        g["scan"] = x1
    woin_done = woin_tok
    kb.dbg("woin", WOIN, waits=[woin_done])
    kb.dbg("ga", GA, waits=[woin_done])
    kb.dbg("w16", row_dram, waits=[t_wrow])
    kb.dbg("a16", arow_dram, waits=[t_arow])

    cfu_waits = [woin_done] + list(getattr(kb, "cfu_wait", []))
    cfu_toks = []
    for kt in range(NKT):
        if len(cfudram.shape) == 4:
            cfu_toks.append(kb.dma(
                CFUB[:, kt, :].rearrange("p (k c) -> p k c", k=8),
                cfudram[:, kt].rearrange("k p c -> p k c"),
                waits=cfu_waits, eng="gpsimd"))
        else:
            cfu_toks.append(kb.dma(CFUB[:, kt, :], cfudram[kt],
                                   waits=cfu_waits, eng="gpsimd"))

    st_tok = [None]
    SOUT = Ff[:, 0:2048]

    def evac_s(ot, ps, tok):
        t1 = kb.act(SOUT, ps[:], AF.Identity, waits=[tok, st_tok[0]],
                    sig=True)
        st_tok[0] = kb.dma(sdram[ot], SOUT, waits=[t1])
        return t1

    ProjFM(kb, wo_d, NKT, lambda kt: WOIN[:, kt, :], evac_s,
           rhs_waits=[woin_done], ps_guard=[woin_done])
    s_done = ("dy", kb.counts["dy"])
    s_evac = kb.tok("scalar")

    GT = Cc[:, 0:8192].bitcast(F32)
    XT = Cc[:, 8192:16384].bitcast(F32)
    ST = Cc[:, 16384:24576].bitcast(F32)
    upd_tok = [None]

    def evac_f3(ot, ps, tok):
        tx = kb.dma(XT, xdram[ot], waits=[upd_tok[0]])
        tss = kb.dma(ST, sdram[ot], waits=[s_done, upd_tok[0]])
        t0 = kb.tt(GT, ps[:], CFUB[:, ot, :], ALU.add,
                   waits=[tok, upd_tok[0]] + cfu_toks, sig=True)
        t1 = kb.act(GT, GT, AF.Tanh, bias=bo_sb[:, ot:ot + 1],
                    waits=[t0], sig=False)
        t1 = kb.act(GT, GT, AF.Silu, waits=[t0], sig=True)
        t2 = kb.tt(GT, GT, ST, ALU.mult, waits=[t1, tss], sig=False)
        t2 = kb.tt(XT, XT, GT, ALU.add, waits=[tx], sig=True)
        upd_tok[0] = kb.dma(xodram[ot], XT, waits=[t2])
        return t2

    kb.dbg("s0", sdram, waits=[s_done])
    ProjFM(kb, wof_d, NKT, xn_t, evac_f3, rhs_waits=xn_toks + [woin_done],
           ps_guard=[s_evac])


def prep_weights_l0(inputs):
    W = dict(
        x_r=inputs["L0_attn_x_r"], x_w=inputs["L0_attn_x_w"],
        x_k=inputs["L0_attn_x_k"], x_v=inputs["L0_attn_x_v"],
        x_a=inputs["L0_attn_x_a"], x_g=inputs["L0_attn_x_g"],
        w0=inputs["L0_attn_w0"], w1=inputs["L0_attn_w1"],
        w2=inputs["L0_attn_w2"], a0=inputs["L0_attn_a0"],
        a1=inputs["L0_attn_a1"], a2=inputs["L0_attn_a2"],
        g1=inputs["L0_attn_g1"], g2=inputs["L0_attn_g2"],
        k_k=inputs["L0_attn_k_k"], r_k=inputs["L0_attn_r_k"],
        Wr=inputs["L0_attn_Wr"], Wk=inputs["L0_attn_Wk"],
        Wv=inputs["L0_attn_Wv"], Wo=inputs["L0_attn_Wo"],
        ln_w=inputs["L0_attn_ln_w"], ln_b=inputs["L0_attn_ln_b"],
        bf=inputs["L0_ffn_bf"], bi=inputs["L0_ffn_bi"],
        bc=inputs["L0_ffn_bc"], bo=inputs["L0_ffn_bo"],
        in_c=inputs["L0_ffn_in_c"],
        wfT=np.ascontiguousarray(inputs["L0_ffn_wf"].T),
        wiT=np.ascontiguousarray(inputs["L0_ffn_wi"].T),
        wcT=np.ascontiguousarray(inputs["L0_ffn_wc"].T),
        ufT=np.ascontiguousarray(inputs["L0_ffn_uf"].T),
        uiT=np.ascontiguousarray(inputs["L0_ffn_ui"].T),
        ucT=np.ascontiguousarray(inputs["L0_ffn_uc"].T),
        uoT=np.ascontiguousarray(inputs["L0_ffn_uo"].T),
        woT=np.ascontiguousarray(inputs["L0_ffn_wo"].T),
    )
    return W


def run_l0(x_full, inputs, dbg=False):
    W = prep_weights_l0(inputs)
    kba = KB()
    kba.dbg_on = dbg
    xpin = kba.din("XPin", [NKT, 128, 2048], F32)
    cfuo = kba.dout("CFUo", [NKT, 128, 256], F32)
    emit_layer0a(kba, W, xpin, cfuo)
    kba.build()
    in_maps = []
    for k in range(NCORES):
        m = dict(kba.consts)
        xp = x_full[:, k * 256:(k + 1) * 256, :]
        m["XPin"] = np.ascontiguousarray(
            xp.transpose(2, 0, 1).reshape(1024, 2048)
        ).reshape(NKT, 128, 2048)
        in_maps.append(m)
    res_a = run_bass_kernel_spmd(kba.nc, in_maps, core_ids=list(range(NCORES)))
    cfu = np.zeros((NKT, 128, 2048), np.float32)
    for k in range(NCORES):
        sl = res_a.results[k]["CFUo"]
        nf = ~np.isfinite(sl)
        if nf.any():
            print(f"[L0a-NANDBG] core {k}: {nf.sum()} nonfinite, "
                  f"tiles={sorted(set(np.where(nf.any(2))[0]))} "
                  f"pos={np.where(nf.any(0).any(0))[0][:6]}", flush=True)
        cfu[:, :, k * 256:(k + 1) * 256] = sl
    kbb = KB()
    kbb.dbg_on = dbg
    xin = kbb.din("Xin", [NKT, 128, 2048], F32)
    cfin = kbb.din("CFUin", [NKT, 128, 2048], F32)
    xout = kbb.dout("Xout", [NKT, 128, 2048], F32)
    emit_layer0b(kbb, W, xin, cfin, xout)
    kbb.build()
    in_maps = []
    for k in range(NCORES):
        m = dict(kbb.consts)
        m["Xin"] = np.ascontiguousarray(x_full[k].T.reshape(NKT, 128, 2048))
        m["CFUin"] = cfu
        in_maps.append(m)
    res_b = run_bass_kernel_spmd(kbb.nc, in_maps, core_ids=list(range(NCORES)))
    out = np.stack([
        res_b.results[k]["Xout"].reshape(1024, 2048).T for k in range(NCORES)])
    nf = ~np.isfinite(out)
    if nf.any():
        for k in range(NCORES):
            if nf[k].any():
                ft = np.where(nf[k].any(0))[0]
                tk = np.where(nf[k].any(1))[0]
                print(f"[L0b-NANDBG] core {k}: n={nf[k].sum()} "
                      f"ftiles={sorted(set(ft // 128))} nt={len(tk)} "
                      f"t[{tk[0]}..{tk[-1]}]", flush=True)
    if dbg:
        return out, res_a.results, res_b.results
    return out



def emit_layer2(kb, W, xdram, xodram):
    ones_bf = kb.sb("ones_bf", [128, 4], BF16)
    ones_row = kb.sb("ones_row", [1, 128], BF16)
    kb.memset(ones_bf[:], 1.0, sig=True)
    t_or = kb.memset(ones_row[:], 1.0, sig=True)
    onesf_col = kb.sb("onesf_col", [128, 1], F32)
    t_ofc = kb.memset(onesf_col[:], 1.0, sig=True)
    mq = kb.sb("l2mk", [16, 4], F32)
    VA = kb.sb("VA_sb", [128, 2, 72], BF16)
    t_va = kb.memset(VA[:], 0.0, sig=True)
    t_va = kb.memset(VA[:, 0, 64:65], 1.0, sig=True)
    t_va = kb.memset(VA[:, 1, 64:65], 1.0, sig=True)
    NM = kb.sb("NM_sb", [1, 2048], BF16)
    WINM = kb.sb("WINM_sb", [128, 256], BF16)
    u = np.arange(256)[None, :]
    p = np.arange(128)[:, None]
    win_np = ((u >= p) & (u - p <= 128)).astype(np.float32)
    win_d = kb.cin("l2win", win_np.astype(BF))
    t_win = kb.dma(WINM[:], win_d)
    sel_np = np.zeros((C, 16), np.float32)
    for d in range(C):
        sel_np[d, d // 64] = 1.0
    sel_d = kb.cin("l2sel16", sel_np.astype(BF))
    sel_sb = kb.sb("sel16", [128, NKT, 16], BF16)
    tok_sel = kb.dma(sel_sb[:], sel_d.rearrange("(k p) m -> p k m", p=128))
    idf_d = kb.cin("idf128", np.eye(128, dtype=np.float32))
    IDF = kb.sb("idf_sb", [128, 128], F32)
    t_idf = kb.dma(IDF[:], idf_d)
    MAGIC = kb.sb("magic", [128, 2], F32)
    t_mg = kb.memset(MAGIC[:, 0:1], 12582912.0, sig=True)
    t_mg = kb.memset(MAGIC[:, 1:2], -12582912.0, sig=True)

    qb_sb, _ = stage_cols(kb, "l2qb", W["bq"], scale=0.125)
    kb_sb, _ = stage_cols(kb, "l2kb", W["bk"])
    ob_sb, _ = stage_cols(kb, "l2ob", W["bo"])
    vb_row_d = kb.cin("l2vb_row",
                      np.ascontiguousarray(W["bv"].reshape(1, 1024)))
    vb_row = kb.sb("l2vb_sb", [1, 1024], F32)
    vb_bf = kb.sb("l2vb_bf", [1, 1024], BF16)
    tvb = kb.dma(vb_row[:], vb_row_d)
    tvb = kb.copy(vb_bf[:], vb_row[:], waits=[tvb], sig=True)
    cw0_sb, _ = stage_cols(kb, "l2cw0", W["conv_w"][:, 0, 0], n=32)
    cw1_sb, _ = stage_cols(kb, "l2cw1", W["conv_w"][:, 0, 1], n=32)
    cw2_sb, _ = stage_cols(kb, "l2cw2", W["conv_w"][:, 0, 2], n=32)
    cb_sb, _ = stage_cols(kb, "l2cb", W["conv_b"], n=32)

    wq_d = kb.cin("l2wq", W["Wq"])
    wk_d = kb.cin("l2wk", W["Wk"])
    wv_d = kb.cin("l2wv", W["Wv"])
    wo_d = kb.cin("l2wo", W["Wo"])
    w1_d = kb.cin("l2w1", np.ascontiguousarray(W["w1"].T))
    w2_d = kb.cin("l2w2", np.ascontiguousarray(W["w2"].T))
    scl125 = kb.sb("scl125", [128, NKT], F32)
    t_s125 = kb.memset(scl125[:], 0.125, sig=True)

    nm_dram = kb.dtmp("l2nm", [16, 2048], BF16)
    sdram = kb.dtmp("s_spill", [NKT, 128, 2048], F32)
    dn_dram = kb.dtmp("l2dn", [1, 2048], F32)
    h_dram = kb.dtmp("l2h", [32, 128, 2048], F32)
    w1q_dram = kb.dtmp("l2w1q", [1024, 4096], BF16)
    w2q_dram = kb.dtmp("l2w2q", [4096, 1024], BF16)
    am_dram = kb.dtmp("l2am", [16, 128], F32)
    sc_dram = kb.dtmp("l2sc", [1, 4], F32)

    XN = kb.abuf([128, NKT, 2064], BF16)
    H16 = kb.abuf([128, 2048], F32)
    off_wst = kb.ar_off
    kb.WST = kb.abuf([128, 2, NKT, 128], F32)
    off_wbf = kb.ar_off
    kb.WBF = kb.abuf([128, 2, 2, NKT, 128], BF16)
    W2A = kb.AR[:, off_wst:off_wst + 8192].bitcast(BF16).rearrange(
        "p (a b) -> p a b", b=128)
    W2B = kb.AR[:, off_wbf:off_wbf + 8192].bitcast(BF16).rearrange(
        "p (a b) -> p a b", b=128)
    Au8 = kb.abuf([128, NKT * 4096], U8)
    Bu8 = kb.abuf([128, NKT * 4096], U8)
    A = Au8.bitcast(BF16).rearrange("p (a b) -> p a b", b=2048)
    Bb = Bu8.bitcast(BF16).rearrange("p (a b) -> p a b", b=2048)
    Cc = kb.abuf([128, NKT * 4096], U8)
    Dd = kb.abuf([128, NKT * 4096], U8)
    Ff = kb.abuf([128, 2048], F32)

    QF, KF = A, Bb
    VT = Cc[:, 0:32768].bitcast(BF16).rearrange("p (a b) -> p a b", b=1024)
    MB = Dd[:, 16384:24576].bitcast(F32)
    RB = Dd[:, 24576:32768].bitcast(F32)
    VWBF = Dd[:, 0:16384].bitcast(BF16).rearrange("p (a b) -> p a b", b=1024)
    VWST = Dd[:, 16384:20480].bitcast(F32)
    SQ2 = Dd[:, 0:32768].bitcast(BF16).rearrange("p (a b) -> p a b", b=2048)
    NQ = Dd[:, 24576:32768].bitcast(F32)
    WOIN = Dd[:, 0:32768].bitcast(BF16).rearrange("p (a b) -> p a b", b=2048)
    XST = Ff[:, 0:2048]
    EX = Ff[:, 0:2048].bitcast(BF16).rearrange("p (a b) -> p a b", b=2048)
    SOUT = Ff[:, 0:2048]
    DNRB = H16
    OSH = Ff.bitcast(BF16)
    XQ1 = A

    xn_toks = emit_ln(kb, xdram, XN, A, Bb, XST, H16[0:16, :],
                      RB, MB, ones_bf[:, 0:1], H16, save_rows=True)
    ln_row_toks = kb.ln_row_toks

    def xn_t(kt):
        return XN[:, kt, 16:2064]

    kb.proj_guard = kb.tok("tensor")

    def evac_q(ot, ps, tok):
        return kb.act(QF[:, ot, :], ps[:], AF.Identity,
                      bias=qb_sb[:, ot:ot + 1], waits=[tok], sig=True)

    def evac_k(ot, ps, tok):
        return kb.act(KF[:, ot, :], ps[:], AF.Identity,
                      bias=kb_sb[:, ot:ot + 1], waits=[tok], sig=True)

    ProjFM(kb, wq_d, NKT, xn_t, evac_q, rhs_waits=xn_toks,
           scale_cols=scl125)
    q_done = kb.tok("scalar")
    ProjFM(kb, wk_d, NKT, xn_t, evac_k, rhs_waits=xn_toks,
           ps_guard=[q_done])
    k_done = kb.tok("scalar")

    vw_toks = []
    prev_cast = k_done
    for kt in range(NKT):
        tok = kb.dma(VWST, wv_d[kt * 128:(kt + 1) * 128, :],
                     waits=[prev_cast, k_done] +
                     (ln_row_toks if kt == 0 else []))
        prev_cast = kb.copy(VWBF[:, kt], VWST, waits=[tok, k_done], sig=True)
        vw_toks.append(prev_cast)
    ev_tok = [k_done, k_done]
    for tt in range(16):
        b = tt % 2
        ps = [kb.PA, kb.PB][b]
        tok_mm = None
        for c in range(2):
            sl = slice(c * 512, (c + 1) * 512)
            for kt in range(NKT):
                tok_mm = kb.mm(ps[:, sl], xn_t(kt)[:, tt * 128:(tt + 1) * 128],
                               VWBF[:, kt, sl], start=(kt == 0), stop=False,
                               waits=[vw_toks[kt], ev_tok[b], t_or, tvb,
                                      k_done])
            tok_mm = kb.mm(ps[:, sl], ones_row[:, :], vb_bf[:, sl],
                           start=False, stop=True, waits=[], sig=(c == 1))
        ev_tok[b] = kb.act(VT[:, tt, :], ps[:, 0:1024], AF.Identity,
                           waits=[tok_mm], sig=True)
    vt_tok = kb.tok("scalar")
    kb.dbg("qf", QF, waits=[q_done])
    kb.dbg("kf", KF, waits=[k_done])
    kb.dbg("vt", VT, waits=[vt_tok])

    AWQ = kb.AR[:, 0:8192].bitcast(F32)
    YWQ = kb.AR[:, 8192:12288].bitcast(BF16)
    SW1C = kb.sb("l2sw1", [128, 4], F32)
    SWACC = kb.sb("l2swacc", [128, 4], F32)

    def weight_mean(wd, rows, cols, accslot, guard):
        tok_prev = guard
        acc_tok = kb.memset(SWACC[:, accslot:accslot + 1], 0.0, sig=True)
        first = True
        for ch in range(rows // 128):
            for half in range(max(1, cols // 2048)):
                csl = slice(half * 2048, min(cols, (half + 1) * 2048))
                n = csl.stop - csl.start
                tok = kb.dma(AWQ[:, 0:n], wd[ch * 128:(ch + 1) * 128, csl],
                             waits=[tok_prev])
                r = kb.red(SW1C[:, 0:1], AWQ[:, 0:n], ALU.add, absval=True,
                           waits=[tok], sig=True)
                acc_tok = kb.tt(SWACC[:, accslot:accslot + 1],
                                SWACC[:, accslot:accslot + 1],
                                SW1C[:, 0:1], ALU.add, waits=[acc_tok],
                                hard=[r] + ([acc_tok] if first else []),
                                sig=True)
                first = False
                tok_prev = r
        return acc_tok

    t_m1 = weight_mean(w1_d, 1024, 4096, 0, vt_tok)
    kb.dbg("swacc", SWACC[:], waits=[t_m1])
    t_m2 = weight_mean(w2_d, 4096, 1024, 1, t_m1)
    sw_mm1 = kb.mm(kb.PA[0:1, 0:128], SWACC[:, 0:1], IDF[:],
                   start=True, stop=True, waits=[t_m1, t_idf, vt_tok],
                   sig=True)
    sw_mm2 = kb.mm(kb.PB[0:1, 0:128], SWACC[:, 1:2], IDF[:],
                   start=True, stop=True, waits=[t_m2], sig=True)
    e1_ = kb.act(H16[0:1, 0:128], kb.PA[0:1, 0:128], AF.Identity,
                 scale=1.0 / (4096 * 1024), waits=[sw_mm1], sig=True)
    e2_ = kb.act(H16[0:1, 128:256], kb.PB[0:1, 0:128], AF.Identity,
                 scale=1.0 / (4096 * 1024), waits=[sw_mm2], sig=True)
    r1_ = kb.red(SW1C[0:1, 0:1], H16[0:1, 0:128], ALU.add, waits=[e1_],
                 sig=True)
    r2_ = kb.red(SW1C[0:1, 1:2], H16[0:1, 128:256], ALU.add, waits=[e2_],
                 sig=True)
    t_sc = kb.ts(SW1C[0:1, 0:2], SW1C[0:1, 0:2], 1e-5, None, ALU.max,
                 waits=[], hard=[r1_, r2_], sig=True)
    t_rc = kb.recip(SW1C[0:1, 2:4], SW1C[0:1, 0:2], waits=[t_sc], sig=True)
    t_scd = kb.dma(sc_dram, SW1C[0:1, 0:4], waits=[t_rc], eng="gpsimd")
    SWB = kb.sb("l2swb", [128, 4], F32)
    t_swb = kb.dma(SWB[:], sc_dram.squeeze(0).partition_broadcast(128),
                   waits=[t_scd], eng="gpsimd")

    def quant_w(wd, wqd, rows, cols, swcol, guard):
        tok_prev = guard
        qt = None
        for ch in range(rows // 128):
            for half in range(max(1, cols // 2048)):
                csl = slice(half * 2048, min(cols, (half + 1) * 2048))
                n = csl.stop - csl.start
                tok = kb.dma(AWQ[:, 0:n], wd[ch * 128:(ch + 1) * 128, csl],
                             waits=[tok_prev])
                t1 = kb.ts(AWQ[:, 0:n], AWQ[:, 0:n],
                           SWB[:, swcol:swcol + 1], None, ALU.mult,
                           waits=[tok, t_swb], sig=True)
                t2 = kb.act(AWQ[:, 0:n], AWQ[:, 0:n], AF.Identity,
                            bias=MAGIC[:, 0:1], waits=[t1, t_mg], sig=False)
                t2 = kb.act(AWQ[:, 0:n], AWQ[:, 0:n], AF.Identity,
                            bias=MAGIC[:, 1:2], waits=[], sig=True)
                t3 = kb.ts(YWQ[:, 0:n], AWQ[:, 0:n],
                           -1.0, 1.0, ALU.max, ALU.min, waits=[t2], sig=True)
                tok_prev = kb.dma(wqd[ch * 128:(ch + 1) * 128, csl],
                                  YWQ[:, 0:n],
                                  waits=[t3], eng="gpsimd")
                qt = tok_prev
        return qt

    t_q1 = quant_w(w1_d, w1q_dram, 1024, 4096, 2, t_swb)
    t_q2 = quant_w(w2_d, w2q_dram, 4096, 1024, 3, t_q1)

    mm_tok = {}
    for idx2, src_t in enumerate((QF, KF)):
        stok = None
        for kt in range(NKT):
            stok = kb.act(SQ2[:, kt], src_t[:, kt, :], AF.Square,
                          waits=[vt_tok, mm_tok.get(0)] +
                          (ln_row_toks if kt == 0 and idx2 == 0 else []),
                          sig=(kt == NKT - 1))
        ps = [kb.PA, kb.PB][idx2]
        for c in range(4):
            sl = slice(c * 512, (c + 1) * 512)
            for kt in range(NKT):
                mm_tok[idx2] = kb.mm(
                    ps[0:16, sl], sel_sb[:, kt, :], SQ2[:, kt, sl],
                    start=(kt == 0), stop=(kt == NKT - 1),
                    waits=[stok, tok_sel, vt_tok, e1_, e2_],
                    sig=(kt == NKT - 1 and c == 3))
    t_nq = kb.act(H16[0:16, :], kb.PA[0:16, :], AF.Sqrt,
                  waits=[mm_tok[0], r1_, r2_], sig=True)
    t_nk = kb.act(NQ[0:16, :], kb.PB[0:16, :], AF.Sqrt,
                  waits=[mm_tok[1]], sig=True)
    t_mk = kb.red(mq[:, 0:1], NQ[0:16, :], ALU.max, waits=[t_nk], sig=True)
    t_mks = kb.ts(mq[:, 1:2], mq[:, 0:1], -1.0, None, ALU.mult,
                  waits=[], hard=[t_mk], sig=True)
    t_b = kb.ts(NQ[0:16, :], H16[0:16, :], mq[:, 1:2], 40.0, ALU.mult,
                ALU.add, waits=[t_nq], hard=[t_mks], sig=True)
    t_nm = kb.dma(nm_dram, NQ[0:16, :], waits=[t_b], eng="gpsimd")

    den_dram = kb.dtmp("l2den", [16, 2048], F32)
    rec_dram = kb.dtmp("l2rec", [16, 2048], F32)
    den_prev = None
    exp_last = t_b
    ex_last = {0: None, 1: None}
    pb_free = t_nk
    for h in range(16):
        dt, po = h // 2, (h % 2) * 64
        nm_tok = kb.dma(NM[:], nm_dram[h:h + 1, :], waits=[t_nm, exp_last],
                        eng="gpsimd")
        ctx_mm = None
        va_tok = {0: t_va, 1: t_va}
        banks_started = set()
        for ki in range(16):
            eb = ki % 2
            tq0 = ki * 128
            ln = min(256, T - tq0)
            sc_tok = kb.mm(
                kb.PA[:, 0:ln],
                KF[po:po + 64, dt, tq0:tq0 + 128],
                QF[po:po + 64, dt, tq0:tq0 + ln],
                start=True, stop=False,
                waits=[exp_last, nm_tok], sig=False)
            sc_tok = kb.mm(
                kb.PA[:, 0:ln], ones_row[:, :], NM[:, tq0:tq0 + ln],
                start=False, stop=True, waits=[], sig=True)
            e_tok = kb.act(EX[:, eb, 0:ln], kb.PA[:, 0:ln], AF.Exp,
                           waits=[sc_tok, ex_last[eb]], sig=True)
            m_tok = kb.tt(EX[:, eb, 0:ln], EX[:, eb, 0:ln], WINM[:, 0:ln],
                          ALU.mult, waits=[e_tok, t_win], sig=True)
            va_t = kb.copy(VA[:, eb, 0:64], VT[:, ki, h * 64:(h + 1) * 64],
                           waits=[vt_tok, va_tok[eb]], sig=True)
            segs = []
            g0 = tq0
            while g0 < tq0 + ln:
                g1 = min(tq0 + ln, (g0 // 512 + 1) * 512)
                segs.append((g0, g1))
                g0 = g1
            for si, (g0, g1) in enumerate(segs):
                bank = g0 // 512
                st_flag = bank not in banks_started
                banks_started.add(bank)
                ctx_mm = kb.mm(kb.PB[0:65, g0:g1],
                               VA[:, eb, 0:65], EX[:, eb, g0 - tq0:g1 - tq0],
                               start=st_flag, stop=(ki == 15),
                               waits=[va_t, m_tok, pb_free],
                               sig=(si == len(segs) - 1))
            ex_last[eb] = ctx_mm
            va_tok[eb] = ctx_mm
            exp_last = e_tok
        d1 = kb.act(DNRB[64:65, :], kb.PB[64:65, :], AF.Identity,
                    waits=[ctx_mm, den_prev], sig=True)
        den_prev = kb.dma(den_dram[h:h + 1], DNRB[64:65, :], waits=[d1],
                          eng="gpsimd")
        if po == 0:
            pb_free = kb.act(WOIN[0:64, dt, :], kb.PB[0:64, :], AF.Identity,
                             waits=[ctx_mm], sig=True)
        else:
            t5 = kb.act(OSH[0:64, 0:2048], kb.PB[0:64, :], AF.Identity,
                        waits=[ctx_mm, ex_last[0], ex_last[1]], sig=True)
            pb_free = kb.dma(WOIN[64:128, dt, :], OSH[0:64, 0:2048],
                             waits=[t5], eng="gpsimd")
            exp_last = pb_free
    ctx_done = pb_free
    tld = kb.dma(DNRB[0:16, :], den_dram, waits=[den_prev], eng="gpsimd")
    trc = kb.recip(DNRB[0:16, :], DNRB[0:16, :], waits=[tld], sig=True)
    trd = kb.dma(rec_dram, DNRB[0:16, :], waits=[trc], eng="gpsimd")
    norm_btok = {0: trd, 1: trd}
    norm_last = None
    for h in range(16):
        dt, po = h // 2, (h % 2) * 64
        bt = kb.dma(DNRB[po:po + 64, :],
                    rec_dram[h:h + 1].squeeze(0).partition_broadcast(64),
                    waits=[trd, norm_btok[h % 2]], eng="gpsimd")
        norm_last = kb.tt(WOIN[po:po + 64, dt, :], WOIN[po:po + 64, dt, :],
                          DNRB[po:po + 64, :], ALU.mult,
                          waits=[bt, ctx_done], sig=True)
        norm_btok[h % 2] = norm_last
    ctx_done = norm_last
    kb.dbg("woin", WOIN, waits=[ctx_done])

    st_tok = [None]

    def evac_s(ot, ps, tok):
        t1 = kb.act(SOUT, ps[:], AF.Identity,
                    bias=ob_sb[:, ot:ot + 1], waits=[tok, st_tok[0], t_nm],
                    sig=True)
        st_tok[0] = kb.dma(sdram[ot], SOUT, waits=[t1])
        return t1

    ProjFM(kb, wo_d, NKT, lambda kt: WOIN[:, kt, :], evac_s,
           rhs_waits=[ctx_done], ps_guard=[exp_last, ctx_done])
    s_done = ("dy", kb.counts["dy"])
    s_evac = kb.tok("scalar")

    IVB1 = Bu8[:, 0:8192].bitcast(F32)
    MAB2 = Bu8[:, 8192:16384].bitcast(F32)
    SCB = Bu8[:, 16384:24576].bitcast(F32)
    AMB = Bu8[:, 24576:32768].bitcast(F32)
    AMXC = kb.sb("l2amxc", [128, 20], F32)
    HH2 = Cc[:, 8192:16448].bitcast(F32)
    YY = Cc[:, 16448:24640].bitcast(F32)
    HQB = Cc[:, 0:32768].bitcast(BF16).rearrange(
        "p (a b) -> p a b", b=512)
    GT = Dd[:, 0:8192].bitcast(F32)
    XT = Dd[:, 8192:16384].bitcast(F32)
    ST2 = Dd[:, 16384:24576].bitcast(F32)
    W2STB = Dd[:, 24576:32768].bitcast(BF16).rearrange(
        "p (a b) -> p a b", b=128)

    kb.dbg("s2", sdram, waits=[("dy", kb.counts["dy"])])
    kb.dbg("scdram", sc_dram, waits=[t_scd])
    kb.dbg("w1q", w1q_dram, waits=[t_q1])

    def token_absmax(src_fn, n_kt, Mdst, guard):
        t1 = kb.memset(Mdst, 0.0, waits=[guard], sig=True)
        for kt in range(n_kt):
            kb.stt(Mdst, src_fn(kt), -1.0, Mdst, ALU.mult, ALU.max,
                   waits=[], hard=[t1] if kt == 0 else [])
            t1 = kb.stt(Mdst, src_fn(kt), 1.0, Mdst, ALU.mult, ALU.max,
                        waits=[], sig=(kt == n_kt - 1))
        return t1

    def amax_rows(Msrc, guard, ps_guard=(), dbgname=None):
        toks = []
        rtok = guard
        for ti in range(16):
            tmm = kb.mm(kb.PA[:, 0:128], Msrc[:, ti * 128:(ti + 1) * 128],
                        IDF[:], start=True, stop=True,
                        waits=[guard, t_idf, rtok] + list(ps_guard),
                        sig=True)
            rtok = kb.red(AMXC[:, ti:ti + 1], kb.PA[:, 0:128], ALU.max,
                          waits=[tmm], sig=True)
            toks.append(rtok)
        tmm = kb.mm(kb.PB[0:16, 0:128], AMXC[:, 0:16], IDF[:],
                    start=True, stop=True,
                    waits=toks + [t_idf, r1_, r2_], sig=True)
        ev = kb.act(H16[0:16, 0:128], kb.PB[0:16, 0:128], AF.Identity,
                    waits=[tmm, r1_, r2_], sig=True)
        if dbgname:
            kb.dbg(dbgname + "_amxc", AMXC[:], waits=toks)
            kb.dbg(dbgname + "_h16", H16[0:16, 0:128], waits=[ev])
        td = kb.dma(am_dram, H16[0:16, 0:128], waits=[ev], eng="gpsimd")
        tb = kb.dma(AMB, am_dram.rearrange("a b -> (a b)")
                    .partition_broadcast(128), waits=[td], eng="gpsimd")
        return tb

    xq_guard = kb.tok("tensor")
    XF = Ff[:, 0:2048]
    MBB = Dd[:, 0:8192].bitcast(F32)
    RBB = Dd[:, 8192:16384].bitcast(F32)
    mrow_d, rrow_d = kb.ln_rows
    t_mbb = kb.dma(MBB, mrow_d.squeeze(0).partition_broadcast(128),
                   waits=[s_evac] + ln_row_toks, eng="gpsimd")
    t_rbb = kb.dma(RBB, rrow_d.squeeze(0).partition_broadcast(128),
                   waits=[t_mbb], eng="gpsimd")
    t_ms = kb.memset(SCB, 0.0, waits=[s_evac], sig=True)
    xf_prev = ctx_done
    t_mx = None
    for kt in range(NKT):
        tokd = kb.dma(XF, xdram[kt], waits=[xf_prev, s_evac])
        t1 = kb.tt(XF, XF, MBB, ALU.subtract, waits=[tokd, t_mbb],
                   hard=[t_ms] if kt == 0 else [], sig=True)
        kb.stt(SCB, XF, -1.0, SCB, ALU.mult, ALU.max, waits=[])
        t_mx = kb.stt(SCB, XF, 1.0, SCB, ALU.mult, ALU.max, waits=[],
                      sig=True)
        xf_prev = t_mx
    kb.dbg("scbraw", SCB, waits=[t_mx])
    t_amb = amax_rows(SCB, t_mx, ps_guard=[e1_, e2_, t_rc], dbgname="am1")
    t_c = kb.ts(AMB, AMB, 1e-30, None, ALU.max, waits=[t_amb], sig=True)
    t_iv = kb.ts(IVB1, AMB, 1.0 / 127, None, ALU.mult, waits=[t_c],
                 sig=True)
    t_iv = kb.ts(IVB1, IVB1, SWB[:, 0:1], None, ALU.mult, waits=[t_swb],
                 sig=True)
    t_iv = kb.tt(IVB1, IVB1, RBB, ALU.mult, waits=[t_rbb], sig=True)
    t_r = kb.recip(SCB, AMB, waits=[t_c], sig=True)
    t_sc1 = kb.ts(SCB, SCB, 127.0, None, ALU.mult, waits=[], hard=[t_r],
                  sig=True)
    xq_toks = []
    xq_prev = xq_guard
    for kt in range(NKT):
        tokd = kb.dma(XF, xdram[kt], waits=[xf_prev])
        t0 = kb.tt(XF, XF, MBB, ALU.subtract, waits=[tokd], sig=True)
        t1 = kb.tt(YY, XF, SCB, ALU.mult, waits=[t_sc1, t_q2, xq_prev],
                   sig=True)
        xf_prev = t1
        t2 = kb.act(YY, YY, AF.Identity, bias=MAGIC[:, 0:1],
                    waits=[t1, t_mg], sig=False)
        t3 = kb.act(XQ1[:, kt, :], YY, AF.Identity, bias=MAGIC[:, 1:2],
                    waits=[], sig=True)
        xq_prev = t3
        xq_toks.append(t3)

    kb.dbg("amb1", AMB, waits=[t_amb])
    kb.dbg("xq1", XQ1, waits=xq_toks)
    kb.dbg("scb", SCB, waits=[t_sc1])
    kb.dbg("ivb1", IVB1, waits=[t_iv])

    MAB_init = kb.memset(MAB2, 0.0, sig=True)
    w1bf = kb.WBF[:, 0]
    p1_prev = {0: t_q1, 1: t_q1}
    mab_tok = MAB_init
    prev_y3 = None
    prev_hst = None
    hh_edge = kb.memset(HH2[:, 0:1], 0.0, sig=True)
    hh_edge = kb.memset(HH2[:, 2049:2056], 0.0, sig=True)
    for fo in range(32):
        b = fo % 2
        wv = w1q_dram[:, fo * 128:(fo + 1) * 128].rearrange(
            "(k p) m -> p k m", p=128)
        tok_d = kb.dma(kb.WBF[:, b, 0], wv, waits=[p1_prev[b], t_q1])
        mm_t = None
        for c in range(4):
            sl = slice(c * 512, (c + 1) * 512)
            for kt in range(NKT):
                mm_t = kb.mm(kb.PA[:, sl] if b == 0 else kb.PB[:, sl],
                             kb.WBF[:, b, 0, kt], XQ1[:, kt, sl],
                             start=(kt == 0), stop=(kt == NKT - 1),
                             waits=[tok_d] + xq_toks,
                             sig=(c == 3 and kt == NKT - 1))
        p1_prev[b] = mm_t
        ps = kb.PA if b == 0 else kb.PB
        e1 = kb.tt(HH2[:, 1:2049], ps[:], IVB1, ALU.mult,
                   waits=[mm_t, t_iv, hh_edge, prev_y3], sig=True)
        y1 = kb.ts(YY, HH2[:, 0:2048], cw0_sb[:, fo:fo + 1], None, ALU.mult,
                   waits=[e1, prev_hst, mab_tok], sig=False)
        y2 = kb.stt(YY, HH2[:, 1:2049], cw1_sb[:, fo:fo + 1], YY,
                    ALU.mult, ALU.add, waits=[e1], sig=False)
        y3 = kb.stt(YY, HH2[:, 2:2050], cw2_sb[:, fo:fo + 1], YY,
                    ALU.mult, ALU.add, waits=[e1], sig=True)
        prev_y3 = y3
        sil = kb.act(YY, YY, AF.Silu, bias=cb_sb[:, fo:fo + 1],
                     waits=[y3], sig=True)
        hst = kb.dma(h_dram[fo], YY, waits=[sil], eng="gpsimd")
        prev_hst = hst
        kb.stt(MAB2, YY, -1.0, MAB2, ALU.mult, ALU.max, waits=[sil],
               hard=[MAB_init] if fo == 0 else [])
        mab_tok = kb.stt(MAB2, YY, 1.0, MAB2, ALU.mult, ALU.max,
                         waits=[], sig=True)
        p1_prev[b] = e1
    p1_done = mab_tok
    h_all = ("dg", kb.counts["dg"])

    kb.dbg("hd", h_dram, waits=[h_all])
    kb.dbg("amdram", am_dram, waits=[("dg", kb.counts["dg"])])
    kb.dbg("mab2", MAB2, waits=[p1_done])

    t_amb2 = amax_rows(MAB2, p1_done)
    t_c2 = kb.ts(AMB, AMB, 1e-5, None, ALU.max, waits=[t_amb2], sig=True)
    IVB2, SCB2 = IVB1, SCB
    t_iv2 = kb.ts(IVB2, AMB, 1.0 / 127, None, ALU.mult,
                  waits=[t_c2, p1_done], sig=True)
    t_iv2 = kb.ts(IVB2, IVB2, SWB[:, 1:2], None, ALU.mult, waits=[t_swb],
                  sig=True)
    t_r2 = kb.recip(SCB2, AMB, waits=[t_c2], sig=True)
    t_sc2 = kb.ts(SCB2, SCB2, 127.0, None, ALU.mult, waits=[], hard=[t_r2],
                  sig=True)

    upd_tok = [None]
    ldg = {0: t_sc2, 1: t_iv2}
    scg = {0: None, 1: None}
    w2_prev = [t_iv2]
    for tc in range(4):
        tsl = slice(tc * 512, (tc + 1) * 512)
        hq_toks = []
        YY2 = AMB
        for ft in range(32):
            sA = 1024 * (ft % 2)
            tok = kb.dma(YY2[:, sA:sA + 512], h_dram[ft][:, tsl],
                         waits=[ldg[ft % 2], h_all])
            t1 = kb.tt(YY2[:, sA + 512:sA + 1024], YY2[:, sA:sA + 512],
                       SCB2[:, tsl], ALU.mult,
                       waits=[tok, t_sc2, t_iv2, scg[ft % 2]] +
                       (w2_prev if ft < 2 else []), sig=True)
            ldg[ft % 2] = t1
            t2 = kb.act(YY2[:, sA + 512:sA + 1024],
                        YY2[:, sA + 512:sA + 1024], AF.Identity,
                        bias=MAGIC[:, 0:1], waits=[t1, t_mg], sig=False)
            t3 = kb.act(HQB[:, ft, :], YY2[:, sA + 512:sA + 1024],
                        AF.Identity, bias=MAGIC[:, 1:2], waits=[], sig=True)
            scg[ft % 2] = t3
            hq_toks.append(t3)
        mm8 = []
        w2tok = [p1_done if tc == 0 else w2tok[0],
                 p1_done if tc == 0 else w2tok[1]]
        for fo2 in range(8):
            b2 = fo2 % 2
            w2buf = [W2A, W2B][b2]
            wv = w2q_dram[:, fo2 * 128:(fo2 + 1) * 128]
            tok_d = kb.dma(w2buf[:],
                           wv.rearrange("(k p) m -> p k m", p=128),
                           waits=[w2tok[b2], t_q2])
            ps = [kb.PA, kb.PB][b2]
            psl = slice((fo2 // 2) * 512, (fo2 // 2) * 512 + 512)
            mm_t = None
            for ft in range(32):
                mm_t = kb.mm(ps[:, psl],
                             w2buf[:, ft, :],
                             HQB[:, ft, :],
                             start=(ft == 0), stop=(ft == 31),
                             waits=[tok_d, hq_toks[ft], upd_tok[0]],
                             sig=(ft == 31))
            w2tok[b2] = mm_t
            mm8.append((fo2, ps, psl, mm_t))
        GTc, XTc, STc = GT[:, 0:512], XT[:, 0:512], ST2[:, 0:512]
        for fo2, ps, psl, mm_t in mm8:
            e1 = kb.tt(GTc, ps[:, psl], IVB2[:, tsl], ALU.mult,
                       waits=[mm_t, t_iv2, upd_tok[0]], sig=True)
            sil = kb.act(GTc, GTc, AF.Silu, waits=[e1], sig=True)
            tx = kb.dma(XTc, xdram[fo2][:, tsl], waits=[upd_tok[0]])
            tss = kb.dma(STc, sdram[fo2][:, tsl], waits=[s_done, upd_tok[0]])
            t2 = kb.tt(GTc, GTc, STc, ALU.mult, waits=[sil, tss], sig=False)
            t2 = kb.tt(XTc, XTc, GTc, ALU.add, waits=[tx], sig=True)
            upd_tok[0] = kb.dma(xodram[fo2][:, tsl], XTc, waits=[t2])
        w2_prev = [w2tok[0], w2tok[1]]


def prep_weights_l2(inputs):
    return dict(
        Wq=inputs["L2_attn_Wq"], bq=inputs["L2_attn_bq"],
        Wk=inputs["L2_attn_Wk"], bk=inputs["L2_attn_bk"],
        Wv=inputs["L2_attn_Wv"], bv=inputs["L2_attn_bv"],
        Wo=inputs["L2_attn_Wo"], bo=inputs["L2_attn_bo"],
        w1=inputs["L2_ffn_w1"], w2=inputs["L2_ffn_w2"],
        conv_w=inputs["L2_ffn_conv_w"], conv_b=inputs["L2_ffn_conv_b"],
    )


def run_l2(x_full, inputs, dbg=False):
    W = prep_weights_l2(inputs)
    kb = KB()
    kb.dbg_on = dbg
    xin = kb.din("Xin", [NKT, 128, 2048], F32)
    xout = kb.dout("Xout", [NKT, 128, 2048], F32)
    emit_layer2(kb, W, xin, xout)
    kb.build()
    in_maps = []
    for k in range(NCORES):
        m = dict(kb.consts)
        m["Xin"] = np.ascontiguousarray(x_full[k].T.reshape(NKT, 128, 2048))
        in_maps.append(m)
    res = run_bass_kernel_spmd(kb.nc, in_maps, core_ids=list(range(NCORES)))
    out = np.stack([
        res.results[k]["Xout"].reshape(1024, 2048).T for k in range(NCORES)])
    _nancheck("L2", out)
    if dbg:
        return out, res.results
    return out



def prep_weights_l3(inputs):
    return dict(
        q_w=np.ascontiguousarray(
            inputs["L3_attn_q_w"].transpose(2, 0, 1).reshape(1024, 1024)),
        k_w=np.ascontiguousarray(
            inputs["L3_attn_k_w"].transpose(2, 0, 1).reshape(1024, 1024)),
        v_w=np.ascontiguousarray(
            inputs["L3_attn_v_w"].transpose(2, 0, 1).reshape(1024, 1024)),
        q_b=inputs["L3_attn_q_b"],
        k_b=inputs["L3_attn_k_b"],
        v_b=inputs["L3_attn_v_b"],
        l_w=np.ascontiguousarray(inputs["L3_attn_l_w"].T),
        l_b=inputs["L3_attn_l_b"],
        weight=inputs["L3_ffn_weight"],
        aux_weight1=inputs["L3_ffn_aux_weight1"],
        aux_weight2=inputs["L3_ffn_aux_weight2"],
        bias=inputs["L3_ffn_bias"],
    )


def build_l3_program(W, dbg=False):
    kb = KB()
    kb.dbg_on = dbg
    xin = kb.din("Xin", [NKT, 128, 2048], F32)
    xout = kb.dout("Xout", [NKT, 128, 2048], F32)
    emit_layer3(kb, W, xin, xout)
    kb.build()
    return kb


def run_l3(x_full, inputs, dbg=False):
    W = prep_weights_l3(inputs)
    kb = build_l3_program(W, dbg)
    in_maps = []
    for k in range(NCORES):
        m = dict(kb.consts)
        m["Xin"] = np.ascontiguousarray(x_full[k].T.reshape(NKT, 128, 2048))
        in_maps.append(m)
    res = run_bass_kernel_spmd(kb.nc, in_maps, core_ids=list(range(NCORES)))
    out = np.stack([
        res.results[k]["Xout"].reshape(1024, 2048).T for k in range(NCORES)])
    _nancheck("L3", out)
    if dbg:
        return out, res.results
    return out


def build_fused(inputs, dbg=False):
    W0 = prep_weights_l0(inputs)
    W1 = prep_weights_l1(inputs)
    W2 = prep_weights_l2(inputs)
    W3 = prep_weights_l3(inputs)
    kb = KB()
    kb.dbg_on = dbg
    xin = kb.din("Xin", [NKT, 128, 2048], F32)
    xpin = kb.din("XPin", [NKT, 128, 2048], F32)
    xout = kb.dout("Xout", [NKT, 128, 2048], F32)
    mk = kb.dout if dbg else kb.dtmp
    x1d = mk("X1d", [NKT, 128, 2048], F32)
    x2d = mk("X2d", [NKT, 128, 2048], F32)
    x3d = mk("X3d", [NKT, 128, 2048], F32)
    cfu_src = kb.dtmp("CFUsrc", [NKT, 128, 256], F32)
    cfu_all = kb.dtmp("CFUall", [NCORES, NKT, 128, 256], F32)
    kb.scope = "A_"
    emit_layer0a(kb, W0, xpin, cfu_src)
    cc_tok = kb.cc_allgather(cfu_all, cfu_src,
                             waits=[("dy", kb.counts.get("dy", 0))])
    kb.cfu_wait = [cc_tok]
    kb.barrier()
    if dbg:
        agd = kb.dout("CFUallDbg", [NCORES, NKT, 128, 256], F32)
        kb.dma(agd, cfu_all, waits=[cc_tok], eng="gpsimd")
    kb.ar_off = 0
    kb.sar_off = 0
    kb.scope = "B_"
    emit_layer0b(kb, W0, xin, cfu_all, x1d)
    kb.barrier()
    kb.ar_off = 0
    kb.sar_off = 0
    kb.scope = "C_"
    emit_layer1(kb, W1, x1d, x2d)
    kb.barrier()
    kb.ar_off = 0
    kb.sar_off = 0
    kb.scope = "D_"
    emit_layer2(kb, W2, x2d, x3d)
    kb.barrier()
    kb.ar_off = 0
    kb.sar_off = 0
    kb.scope = "E_"
    emit_layer3(kb, W3, x3d, xout)
    kb.build()
    return kb


def kernel(**inputs):
    x = np.asarray(inputs["L0_attn_x"], np.float32)
    kb = build_fused(inputs)
    in_maps = []
    for k in range(NCORES):
        m = dict(kb.consts)
        m["Xin"] = np.ascontiguousarray(x[k].T.reshape(NKT, 128, 2048))
        xp = x[:, k * 256:(k + 1) * 256, :]
        m["XPin"] = np.ascontiguousarray(
            xp.transpose(2, 0, 1).reshape(1024, 2048)).reshape(
                NKT, 128, 2048)
        in_maps.append(m)
    res = run_bass_kernel_spmd(kb.nc, in_maps, core_ids=list(range(NCORES)))
    out = np.stack([res.results[k]["Xout"].reshape(1024, 2048).T
                    for k in range(NCORES)])
    _nancheck("FUSED", out)
    return out


def kernel_unfused(**inputs):
    x = np.asarray(inputs["L0_attn_x"], np.float32)
    x = run_l0(x, inputs)
    x = run_l1(x, inputs)
    x = run_l2(x, inputs)
    x = run_l3(x, inputs)
    return x
